# version 14
# baseline (speedup 1.0000x reference)
import numpy as np
import ml_dtypes
from contextlib import ExitStack

import concourse.bass as bass
import concourse.mybir as mybir
from concourse.bass_utils import run_bass_kernel_spmd

F32 = mybir.dt.float32
BF16 = mybir.dt.bfloat16
AF = mybir.ActivationFunctionType
ALU = mybir.AluOpType
AX = mybir.AxisListType

C = 1024
CCN = 8
NH = 16
HD = 64
M0 = 12.0
NEG = -1e9
EPS = 1e-5


ENGS = ("sync", "tensor", "scalar", "vector", "gpsimd")


def apof(x):
    return x.ap() if callable(getattr(x, "ap", None)) else x


def phase_barrier(p):
    for name in list(p.counts):
        if not name.startswith("c_"):
            p.wait("sync", (name, p.counts[name]))
    sigs = {e: p.signal(e) for e in ENGS}
    for e in ENGS:
        for e2, s in sigs.items():
            if e2 != e:
                p.wait(e, s)


class Prog:
    def __init__(self, name):
        self.name = name
        self.nc = bass.Bass("TRN2", target_bir_lowering=False)
        self.es = ExitStack()
        self.queues = {e: [] for e in ENGS}
        self.semh = {}
        self.counts = {}
        self.watermark = {}
        self.input_names = []
        self.output_names = []
        self._sb = {}
        self.sim_trace = {}

    def din(self, name, shape, dtype):
        self.input_names.append(name)
        return self.nc.dram_tensor(name, list(shape), dtype, kind="ExternalInput").ap()

    def dout(self, name, shape, dtype):
        self.output_names.append(name)
        return self.nc.dram_tensor(name, list(shape), dtype, kind="ExternalOutput").ap()

    def sb(self, name, shape, dtype):
        h = self.es.enter_context(self.nc.sbuf_tensor(name, list(shape), dtype))
        self._sb[name] = h
        return h

    def set_arena(self, f32_cols):
        self.arena = self.sb("ARENA", [128, f32_cols], F32)
        self._arena_off = 0
        self._arena_peak = 0

    def begin_phase(self, name):
        self._phase = name
        self._arena_off = 0

    def region(self, name, shape, dtype):
        if getattr(self, "arena", None) is None:
            return self.sb(name, shape, dtype)
        esz = 4 if dtype == F32 else 2
        off = (self._arena_off + 7) // 8 * 8
        self._arena_off = off + shape[1] * esz
        self._arena_peak = max(self._arena_peak, self._arena_off)
        assert self._arena_off <= self.arena.shape[1] * 4, \
            f"arena overflow in {getattr(self, '_phase', '?')}: {self._arena_off}"
        return self.arena.ap().bitcast(dtype)[0:shape[0],
                                              off // esz: off // esz + shape[1]]

    def psum(self):
        self.PS = self.es.enter_context(self.nc.psum_tensor("PS", [128, 4096], F32))
        self.PSB = self.PS.ap().bitcast(BF16)
        return self.PS

    def ps32(self, col0, width, p0=0, pn=128):
        return self.PS.ap()[p0:p0 + pn, col0:col0 + width]

    def psbf(self, col0, width, p0=0, pn=128):
        return self.PSB[p0:p0 + pn, col0:col0 + width]

    def _sem(self, name):
        if name not in self.semh:
            self.semh[name] = self.es.enter_context(self.nc.semaphore(name))
            self.counts[name] = 0
        return self.semh[name]

    def _push(self, eng, fn, flaggable=False):
        entry = {"fn": fn, "inc": None}
        self.queues[eng].append(entry)
        if flaggable:
            self._lastop = getattr(self, "_lastop", {})
            self._lastop[eng] = entry
        return entry

    def raw(self, eng, fn):
        self._push(eng, fn)

    def op(self, eng, opname, *args, **kw):
        if eng == "tensor" and opname in ("matmul", "transpose"):
            lhs = args[1]
            b = lhs.base_partition()
            n = lhs.partition_size()
            key = (b, b + n)
            last = getattr(self, "_pe_rows", None)
            if last is not None and (key[0] >= last[1] or key[1] <= last[0]):
                self._push(eng, lambda e: e.drain())
            self._pe_rows = key
        self._push(eng, lambda e: getattr(e, opname)(*args, **kw), flaggable=True)

    def drain(self, eng):
        if eng == "tensor":
            self._pe_rows = None
        self._push(eng, lambda e: e.drain())

    def signal(self, eng, sem=None, force_drain=False):
        if eng == "tensor":
            self._pe_rows = None
        sem = sem or ("c_" + eng)
        h = self._sem(sem)
        self.counts[sem] += 1
        v = self.counts[sem]
        self.sim_trace.setdefault(eng, []).append(("inc", sem, 1))
        self._push(eng, lambda e: e.drain())
        self._push(eng, lambda e: e.nop())
        self.queues[eng][-1]["inc"] = (h, 1)
        return (sem, v)

    def wait(self, eng, sig):
        sem, v = sig
        if sem == "c_" + eng:
            return
        h = self._sem(sem)
        key = (eng, sem)
        if self.watermark.get(key, 0) >= v:
            return
        self.watermark[key] = v
        self.sim_trace.setdefault(eng, []).append(("wait", sem, v))
        self._push(eng, lambda e: e.wait_ge(h, v))

    def dma(self, eng, sem, out, in_):
        h = self._sem(sem)
        self.counts[sem] += 16
        v = self.counts[sem]
        self.sim_trace.setdefault(eng, []).append(("inc", sem, 16))
        self._push(eng, lambda e: e.dma_start(out=out, in_=in_).then_inc(h, 16))
        return (sem, v)

    def dma_all(self, sem):
        self._sem(sem)
        return (sem, self.counts[sem])

    def finish(self):
        for name in list(self.counts):
            if not name.startswith("c_"):
                self.wait("sync", (name, self.counts[name]))

    def build(self):
        with self.es:
            with self.nc.Block() as block:
                for e in ENGS:
                    q = self.queues[e]
                    if not q:
                        continue
                    def runner(eng, q=q):
                        for item in q:
                            inst = item["fn"](eng)
                            if item["inc"] is not None:
                                h, n = item["inc"]
                                inst.then_inc(h, n)
                    getattr(block, e)(runner)
        return self.nc



def _band_penalty(T):
    LOCAL = 64
    def mask_valid(i, j):
        return (j <= i) and (j >= 0) and ((abs(i - j) <= LOCAL) or (i == 0) or (j == 0))
    pens = {}
    for variant in ("g", "0"):
        pen = np.full((128, 256), NEG, dtype=np.float32)
        qt = 1 if variant == "g" else 0
        base = qt * 128
        for r in range(128):
            i = base + r
            for c in range(192):
                j = base - 64 + c
                if mask_valid(i, j):
                    pen[r, c] = 0.0
            if variant == "g":
                if mask_valid(i, 0):
                    pen[r, 192] = 0.0
        pens[variant] = pen
    return pens["g"], pens["0"]


def _chunk_bias(b):
    return np.ascontiguousarray(b.reshape(CCN, 128).T.astype(np.float32))


def prepare_host(inputs, T):
    A = {}
    f32 = np.float32
    x = np.asarray(inputs["L0_attn_hidden_states"], dtype=f32)
    B = x.shape[0]
    A["__percore_xT"] = [np.ascontiguousarray(x[b].T) for b in range(B)]

    for nm in ("Wq", "Wk", "Wv", "Wo"):
        W = np.asarray(inputs[f"L0_attn_{nm}"], dtype=f32)
        A[f"l0a_{nm}T"] = np.ascontiguousarray(W.T)
    for nm in ("bq", "bk", "bo"):
        A[f"l0a_{nm}"] = _chunk_bias(np.asarray(inputs[f"L0_attn_{nm}"], dtype=f32))
    A["l0a_bv_row"] = np.asarray(inputs["L0_attn_bv"], dtype=f32).reshape(1, C)

    Wq = np.asarray(inputs["L0_ffn_Wq"], dtype=f32)
    Wk = np.asarray(inputs["L0_ffn_Wk"], dtype=f32)
    A["l0f_WqT"] = np.ascontiguousarray(Wq.reshape(C, C).T)
    A["l0f_WkT"] = np.ascontiguousarray(Wk.reshape(C, C).T)
    Ws = np.asarray(inputs["L0_ffn_Ws"], dtype=f32).reshape(128, C)
    Wd = np.asarray(inputs["L0_ffn_Wd"], dtype=f32).reshape(128, C)
    A["l0f_WsdT"] = np.ascontiguousarray(np.concatenate([Ws, Wd], 0).T)
    Wv = np.asarray(inputs["L0_ffn_Wv"], dtype=f32).reshape(512, C)
    A["l0f_WvT"] = np.ascontiguousarray(Wv.T)
    Wo = np.asarray(inputs["L0_ffn_Wo"], dtype=f32)
    A["l0f_WoT"] = np.ascontiguousarray(
        Wo.transpose(0, 2, 1).reshape(512, C))

    perm = np.zeros(C, dtype=np.int64)
    for r in range(512):
        perm[r] = (r // 32) * 64 + 2 * (r % 32)
        perm[512 + r] = (r // 32) * 64 + 2 * (r % 32) + 1
    for nm in ("Wq", "Wk"):
        W = np.asarray(inputs[f"L1_attn_{nm}"], dtype=f32)
        A[f"l1a_{nm}T"] = np.ascontiguousarray(W[:, perm])
    A["l1a_bq"] = _chunk_bias(np.asarray(inputs["L1_attn_bq"], dtype=f32)[perm])
    A["l1a_bk"] = _chunk_bias(np.asarray(inputs["L1_attn_bk"], dtype=f32)[perm])
    A["l1a_WvT"] = np.ascontiguousarray(np.asarray(inputs["L1_attn_Wv"], f32))
    A["l1a_bv_row"] = np.asarray(inputs["L1_attn_bv"], dtype=f32).reshape(1, C)
    A["l1a_WoT"] = np.ascontiguousarray(np.asarray(inputs["L1_attn_Wo"], f32))
    A["l1a_bo"] = _chunk_bias(np.asarray(inputs["L1_attn_bo"], dtype=f32))

    pos = np.arange(T, dtype=np.float32)
    freq = np.arange(0, HD, 2, dtype=np.float32) / HD
    inv_freq = (1.0 / 10000.0 ** freq).astype(np.float32)
    ang = np.outer(pos, inv_freq).astype(np.float32)
    ang2 = np.repeat(ang, 2, axis=-1)
    cos = np.cos(ang2).astype(f32)
    sin = np.sin(ang2).astype(f32)
    for nm, tab in (("cosE", cos[:, 0::2]), ("sinE", sin[:, 0::2]),
                    ("cosO", cos[:, 1::2]), ("sinO", sin[:, 1::2])):
        A[f"rope_{nm}"] = np.ascontiguousarray(np.tile(tab.T, (4, 1)))

    for nm in ("cn_w1", "cn_w2", "cn_w3", "gc_w1", "gc_w2"):
        A[f"l1f_{nm}"] = np.ascontiguousarray(np.asarray(inputs[f"L1_ffn_{nm}"], f32))
    for nm in ("cn_w4", "gc_w3"):
        w = np.asarray(inputs[f"L1_ffn_{nm}"], f32).reshape(-1)
        RC = w.shape[0] // 128
        A[f"l1f_{nm}"] = np.ascontiguousarray(
            w.reshape(RC, 128).T.reshape(1, -1))
    for nm, dim in (("cn_b1", 1024), ("cn_b2", 512), ("cn_b3", 256),
                    ("gc_b1", 512), ("gc_b2", 256)):
        b = np.asarray(inputs[f"L1_ffn_{nm}"], dtype=f32)
        A[f"l1f_{nm}"] = np.ascontiguousarray(b.reshape(dim // 128, 128).T)
    A["l1f_cn_b4"] = np.asarray(inputs["L1_ffn_cn_b4"], f32).reshape(1, 1)
    A["l1f_gc_b3"] = np.asarray(inputs["L1_ffn_gc_b3"], f32).reshape(1, 1)
    A["l1f_ar"] = np.asarray(inputs["L1_ffn_adaptive_range"], f32).reshape(1, 1)

    A["ident_bf"] = np.eye(128, dtype=ml_dtypes.bfloat16)
    A["ones_row_f"] = np.ones((1, 128), dtype=f32)
    A["ones_f128"] = np.ones((128, 64), dtype=f32)
    A["ones_row_bf"] = np.ones((1, 128), dtype=ml_dtypes.bfloat16)
    A["ones_col_bf"] = np.ones((128, 1), dtype=ml_dtypes.bfloat16)
    pen_g, pen_0 = _band_penalty(T)
    A["pen_g"] = pen_g
    A["pen_0"] = pen_0
    tri = np.zeros((128, 128), dtype=f32)
    for r in range(128):
        tri[r, :r] = NEG
    A["pen_tri"] = tri
    iota = np.tile(np.arange(8, dtype=f32), (128, 32))
    A["iota"] = iota
    A["iotap"] = iota + 16384.0
    fc = np.zeros((128, 8), dtype=f32)
    fc[:, 0] = EPS
    fc[:, 1] = -M0
    A["fconsts"] = fc
    return A



class Common:

    def __init__(self, p: Prog, T):
        self.p = p
        self.T = T
        nc = p.nc
        p.psum()
        self.ident = p.sb("s_ident", [128, 128], BF16)
        self.ones_row_f = p.sb("s_ones_row_f", [1, 128], F32)
        self.ones_row_bf = p.sb("s_ones_row_bf", [1, 128], BF16)
        self.ones_col_bf = p.sb("s_ones_col_bf", [128, 1], BF16)
        self.ones_f128 = p.sb("s_ones_f128", [128, 64], F32)
        self.const_loads = []
        for nm, dst in (("ident_bf", self.ident), ("ones_row_f", self.ones_row_f),
                        ("ones_row_bf", self.ones_row_bf), ("ones_col_bf", self.ones_col_bf),
                        ("ones_f128", self.ones_f128)):
            ap = p.din(nm, dst.shape, dst.dtype)
            p.dma("sync", "cst", dst[:], ap)
        self.fc = p.sb("s_fconsts", [128, 8], F32)
        p.dma("sync", "cst", self.fc[:], p.din("fconsts", [128, 8], F32))
        self.stage = p.sb("stage", [128, 1024], F32)
        self.wbuf = p.sb("wbuf", [128, 8192], BF16)
        self.stage_free = None

    def loadc(self, name, shape, dtype):
        p = self.p
        dst = p.region("c_" + name, shape, dtype)
        ap = p.din(name, shape, dtype)
        p.dma("sync", "cst", dst[:], ap)
        return dst

    def wait_consts(self, engs):
        sig = self.p.dma_all("cst")
        for e in engs:
            self.p.wait(e, sig)

    def load_weight(self, wname, hbm, R, O, dst, dst_off=0, war_sig=None,
                    conv="vector"):
        p = self.p
        assert O <= 1024
        RC = R // 128
        cpop = "tensor_copy" if conv == "vector" else "copy"
        if O == 1:
            if war_sig is not None:
                p.wait(conv, war_sig)
            dsig = p.dma("sync", "wstg",
                         self.stage[:, 0:RC],
                         hbm.rearrange("o (pp rc) -> pp (o rc)", pp=128))
            p.wait(conv, dsig)
            p.op(conv, cpop, dst[:, dst_off:dst_off + RC],
                 self.stage[:, 0:RC])
            sig = p.signal(conv)
            p.wait("sync", sig)
            return sig
        hb3 = hbm.rearrange("(rc p) o -> p rc o", p=128)
        per = max(1, 1024 // O)
        if war_sig is not None:
            p.wait(conv, war_sig)
        sig = None
        rc0 = 0
        while rc0 < RC:
            rcn = min(per, RC - rc0)
            n = rcn * O
            st3 = self.stage[:].rearrange("p (rc o) -> p rc o", o=O)
            dsig = p.dma("sync", "wstg", st3[:, 0:rcn], hb3[:, rc0:rc0 + rcn])
            p.wait(conv, dsig)
            p.op(conv, cpop,
                 dst[:, dst_off + rc0 * O: dst_off + rc0 * O + n],
                 self.stage[:, 0:n])
            sig = p.signal(conv)
            p.wait("sync", sig)
            rc0 += rcn
        return sig

    def _stage_guard(self, sig):
        self.p.wait("sync", sig)

    def emit_ln_stream(self, xT, xn_bf, scratch_a, scratch_b, fp32_sink=None):
        p, T = self.p, self.T
        av = apof(scratch_a).bitcast(F32)
        bv = apof(scratch_b).bitcast(F32)
        xc = av[:, 0:2 * T]
        tmp = av[:, 2 * T:3 * T]
        sq = av[:, 3 * T:4 * T].bitcast(BF16)
        stat = bv[0:1, 0:2 * T]
        rows = bv[0:1, 2 * T:4 * T]
        xt3 = xT.rearrange("(cc p) t -> p cc t", p=128)
        self.wait_consts(["tensor", "scalar", "vector"])
        QW = min(512, T)
        mmsigs = {}
        sqsigs = {}
        cvsigs = {}
        for cc in range(CCN):
            slot = cc % 2
            semn = "lnx%d" % slot
            if cc >= 2:
                p.wait("sync", cvsigs[cc - 2])
                p.wait("sync", sqsigs[cc - 2])
            dsig = p.dma("sync", semn, xc[:, slot * T:(slot + 1) * T], xt3[:, cc])
            p.wait("vector", dsig)
            p.op("vector", "tensor_copy", xn_bf[:, cc * T:(cc + 1) * T],
                 xc[:, slot * T:(slot + 1) * T])
            cvsigs[cc] = p.signal("vector")
            p.wait("scalar", dsig)
            if cc >= 2:
                p.wait("scalar", mmsigs[cc - 2])
            p.op("scalar", "activation", sq[:, slot * T:slot * T + T],
                 xc[:, slot * T:(slot + 1) * T], AF.Square)
            sqsigs[cc] = p.signal("scalar")
            p.wait("tensor", sqsigs[cc])
            p.wait("tensor", cvsigs[cc])
            for q in range(T // QW):
                p.op("tensor", "matmul", p.ps32(q * QW, QW, 0, 1),
                     self.ones_col_bf[:], xn_bf[:, cc * T + q * QW: cc * T + (q + 1) * QW],
                     start=(cc == 0), stop=(cc == CCN - 1))
                p.op("tensor", "matmul", p.ps32(2048 + q * QW, QW, 0, 1),
                     self.ones_col_bf[:], sq[:, slot * T + q * QW: slot * T + (q + 1) * QW],
                     start=(cc == 0), stop=(cc == CCN - 1))
            mmsigs[cc] = p.signal("tensor")
        mmsig = mmsigs[CCN - 1]
        p.wait("scalar", mmsig)
        p.op("scalar", "copy", stat[0:1, 0:T], p.ps32(0, T, 0, 1))
        p.op("scalar", "copy", stat[0:1, T:2 * T], p.ps32(2048, T, 0, 1))
        evsig = p.signal("scalar")
        p.wait("vector", evsig)
        p.op("vector", "tensor_scalar", rows[0:1, 0:T], stat[0:1, 0:T],
             1.0 / C, None, ALU.mult)
        p.drain("vector")
        p.op("vector", "tensor_tensor", rows[0:1, T:2 * T], rows[0:1, 0:T],
             rows[0:1, 0:T], ALU.mult)
        p.drain("vector")
        p.op("vector", "scalar_tensor_tensor", rows[0:1, T:2 * T], stat[0:1, T:2 * T],
             1.0 / C, rows[0:1, T:2 * T], ALU.mult, ALU.subtract)
        vsig = p.signal("vector")
        p.wait("scalar", vsig)
        p.op("scalar", "activation", stat[0:1, 0:T], rows[0:1, T:2 * T], AF.Ln,
             bias=self.fc[0:1, 0:1])
        p.drain("scalar")
        p.op("scalar", "activation", stat[0:1, 0:T], stat[0:1, 0:T], AF.Exp, scale=-0.5)
        rsig = p.signal("scalar")
        p.wait("vector", rsig)
        p.op("vector", "scalar_tensor_tensor", rows[0:1, T:2 * T], rows[0:1, 0:T],
             -1.0, stat[0:1, 0:T], ALU.mult, ALU.mult)
        bsig = p.signal("vector")
        p.wait("tensor", rsig)
        p.wait("tensor", bsig)
        for q in range(T // QW):
            p.op("tensor", "matmul", p.ps32(q * QW, QW), self.ones_row_f[:],
                 stat[0:1, q * QW:(q + 1) * QW], start=True, stop=True)
            p.op("tensor", "matmul", p.ps32(2048 + q * QW, QW), self.ones_row_f[:],
                 rows[0:1, T + q * QW: T + (q + 1) * QW], start=True, stop=True)
        bcsig = p.signal("tensor")
        p.wait("vector", bcsig)
        outsigs = {}
        for s in (sqsigs[CCN - 1], sqsigs[CCN - 2], cvsigs[CCN - 1], cvsigs[CCN - 2]):
            p.wait("sync", s)
        sink_sig = None
        for cc in range(CCN):
            slot = cc % 2
            semn = "lnn%d" % slot
            if cc >= 2:
                p.wait("sync", outsigs[cc - 2])
            dsig = p.dma("sync", semn, xc[:, slot * T:(slot + 1) * T], xt3[:, cc])
            p.wait("vector", dsig)
            if sink_sig is not None:
                p.wait("vector", sink_sig)
            p.drain("vector")
            p.op("vector", "tensor_tensor", tmp, xc[:, slot * T:(slot + 1) * T],
                 p.ps32(0, T), ALU.mult)
            p.drain("vector")
            if fp32_sink is not None:
                p.op("vector", "tensor_tensor", tmp, tmp, p.ps32(2048, T), ALU.add)
                p.drain("vector")
                p.op("vector", "tensor_copy", xn_bf[:, cc * T:(cc + 1) * T], tmp)
                sig = p.signal("vector")
                sink_sig = fp32_sink(cc, tmp, sig)
            else:
                p.op("vector", "tensor_tensor", xn_bf[:, cc * T:(cc + 1) * T],
                     tmp, p.ps32(2048, T), ALU.add)
                sig = p.signal("vector")
            outsigs[cc] = sig
        return outsigs[CCN - 1]

    def emit_proj_fm(self, wbuf, woff, src_bf, dst_bf, OC, bias=None, act=AF.Identity,
                     wait_sigs=(), evac_engine="scalar", evac_cb=None, Odim=None,
                     CC=CCN, pre_mm=None):
        p, T = self.p, self.T
        Odim = Odim or OC * 128
        QW = min(512, T)
        NT = T // QW
        for s in wait_sigs:
            p.wait("tensor", s)
        evsigs = {}
        k = 0
        pair_ok = (NT % 2 == 0) and evac_cb is None
        for oc in range(OC):
            tc = 0
            while tc < NT:
                width = 2 if pair_ok else 1
                bank = k % 4
                for kk in range(k, k + width):
                    if kk - 4 in evsigs:
                        p.wait("tensor", evsigs[kk - 4])
                if pre_mm is not None:
                    for w_ in range(width):
                        pre_mm(oc, tc + w_)
                for w_ in range(width):
                    ps = p.ps32((bank + w_) * 512, QW)
                    for cc in range(CC):
                        p.op("tensor", "matmul", ps,
                             wbuf[:, woff + cc * Odim + oc * 128: woff + cc * Odim + (oc + 1) * 128],
                             src_bf[:, cc * T + (tc + w_) * QW: cc * T + (tc + w_ + 1) * QW],
                             start=(cc == 0), stop=(cc == CC - 1))
                msig = p.signal("tensor")
                p.wait(evac_engine, msig)
                if evac_cb is not None:
                    evac_cb(oc, tc, p.ps32(bank * 512, QW))
                else:
                    kw = {}
                    if bias is not None:
                        kw["bias"] = bias[:, oc:oc + 1]
                    p.op(evac_engine, "activation",
                         dst_bf[:, oc * T + tc * QW: oc * T + (tc + width) * QW],
                         p.ps32(bank * 512, width * QW), act, **kw)
                sig = p.signal(evac_engine)
                for kk in range(k, k + width):
                    evsigs[kk] = sig
                k += width
                tc += width
        return evsigs[k - 1]

    def emit_proj_residual(self, wbuf, src_bf, x_hbm, out_hbm, bias=None,
                           wait_sigs=(), OC=CCN, CC=CCN, Odim=None,
                           xbuf=None, obuf=None):
        p, T = self.p, self.T
        QW = min(512, T)
        xin3 = x_hbm.rearrange("(cc p) t -> p cc t", p=128)
        out3 = out_hbm.rearrange("(cc p) t -> p cc t", p=128)
        if xbuf is None:
            xbuf = p.region("res_x", [128, 2 * QW], F32)
        if obuf is None:
            obuf = p.region("res_o", [128, 2 * QW], F32)
        stsigs = {}
        dsigs = {}
        k = 0
        NT = T // QW

        def pre(oc, tc):
            kk = oc * NT + tc
            slot = kk % 2
            if kk >= 2:
                p.wait("sync", stsigs[kk - 2])
            dsigs[kk] = p.dma("sync", "resx%d" % slot,
                              xbuf[:, slot * QW:(slot + 1) * QW],
                              xin3[:, oc, tc * QW:(tc + 1) * QW])

        def evac(oc, tc, ps):
            kk = oc * NT + tc
            slot = kk % 2
            p.wait("vector", dsigs[kk])
            if kk >= 2:
                p.wait("vector", stsigs[kk - 2])
            if bias is not None:
                p.op("vector", "scalar_tensor_tensor",
                     obuf[:, slot * QW:(slot + 1) * QW], ps, bias[:, oc:oc + 1],
                     xbuf[:, slot * QW:(slot + 1) * QW], ALU.add, ALU.add)
            else:
                p.op("vector", "tensor_tensor",
                     obuf[:, slot * QW:(slot + 1) * QW], ps,
                     xbuf[:, slot * QW:(slot + 1) * QW], ALU.add)
            esig = p.signal("vector")
            p.wait("sync", esig)
            stsigs[kk] = p.dma("sync", "ro%d" % slot, out3[:, oc, tc * QW:(tc + 1) * QW],
                               obuf[:, slot * QW:(slot + 1) * QW])

        return self.emit_proj_fm(wbuf, 0, src_bf, None, OC, wait_sigs=wait_sigs,
                                 evac_engine="vector", evac_cb=evac, pre_mm=pre,
                                 CC=CC, Odim=Odim)





    def emit_v_tm(self, wbuf, src_bf, v_tm, bvr_bf, wait_sigs=(), stride=1040):
        p, T = self.p, self.T
        NT16 = T // 128
        if stride == 1040:
            vall = v_tm[:, 0:NT16 * 1040].rearrange(
                "p (t h e) -> p t h e", h=NH, e=65)
            p.op("vector", "memset", vall[:, :, :, 64:65], 1.0)
            onesig = p.signal("vector")
            p.wait("scalar", onesig)
        for s in wait_sigs:
            p.wait("tensor", s)
        evsigs = {}
        for tt in range(NT16):
            bk2 = (tt % 2) * 2
            if tt >= 2:
                p.wait("tensor", evsigs[tt - 2])
            for half in range(2):
                ps = p.ps32(bk2 * 512 + half * 512, 512)
                for cc in range(CCN):
                    p.op("tensor", "matmul", ps,
                         src_bf[:, cc * T + tt * 128: cc * T + (tt + 1) * 128],
                         wbuf[:, cc * C + half * 512: cc * C + (half + 1) * 512],
                         start=(cc == 0), stop=(bvr_bf is None and cc == CCN - 1))
                if bvr_bf is not None:
                    p.op("tensor", "matmul", ps, self.ones_row_bf[:],
                         bvr_bf[0:1, half * 512:(half + 1) * 512],
                         start=False, stop=True)
            msig = p.signal("tensor")
            p.wait("scalar", msig)
            if stride == 1040:
                v3 = v_tm[:, tt * 1040:(tt + 1) * 1040].rearrange(
                    "p (h e) -> p h e", e=65)[:, :, 0:64]
                src3 = p.ps32(bk2 * 512, 1024).rearrange("p (h d) -> p h d", d=64)
                p.op("scalar", "activation", v3, src3, AF.Identity)
            else:
                p.op("scalar", "activation", v_tm[:, tt * 1024:(tt + 1) * 1024],
                     p.ps32(bk2 * 512, 1024), AF.Identity)
            evsigs[tt] = p.signal("scalar")
        return evsigs[NT16 - 1]

    def emit_proj_tm(self, wbuf, src_bf, dst, Odim, wait_sigs=(), act=AF.Identity,
                     extra_mm=None, woff=0, ret_all=False):
        p, T = self.p, self.T
        NT16 = T // 128
        for s in wait_sigs:
            p.wait("tensor", s)
        evsigs = {}
        pair = (NT16 % 2 == 0) and Odim <= 512
        step = 2 if pair else 1
        for t0 in range(0, NT16, step):
            bank = t0 % 4
            for tt in range(t0, t0 + step):
                if tt - 4 in evsigs:
                    p.wait("tensor", evsigs[tt - 4])
            for w_ in range(step):
                ps = p.ps32((bank + w_) * 512, Odim)
                for cc in range(CCN):
                    p.op("tensor", "matmul", ps,
                         src_bf[:, cc * T + (t0 + w_) * 128: cc * T + (t0 + w_ + 1) * 128],
                         wbuf[:, woff + cc * Odim: woff + (cc + 1) * Odim],
                         start=(cc == 0), stop=(cc == CCN - 1))
                if extra_mm is not None:
                    extra_mm(ps, Odim)
            msig = p.signal("tensor")
            p.wait("scalar", msig)
            if pair:
                pv2 = p.PS.ap()[:, bank * 512: bank * 512 + 1024].rearrange(
                    "p (w o) -> p w o", o=512)[:, :, 0:Odim]
                dv2 = dst[:, t0 * Odim:(t0 + 2) * Odim].rearrange(
                    "p (w o) -> p w o", o=Odim)
                p.op("scalar", "activation", dv2, pv2, act)
            else:
                p.op("scalar", "activation", dst[:, t0 * Odim:(t0 + 1) * Odim],
                     p.ps32(bank * 512, Odim), act)
            sig = p.signal("scalar")
            for tt in range(t0, t0 + step):
                evsigs[tt] = sig
        return evsigs if ret_all else evsigs[NT16 - 1]

    def emit_top2(self, g, member, scrf, iota, iotap, wait_sig=None):
        p = self.p
        E, G = 8, 32
        t16 = scrf[:, 0:128]
        t8 = scrf[:, 128:192]
        m1 = scrf[:, 192:224]
        idx = scrf[:, 224:256]
        eqb = scrf[:, 256:512]
        zb = scrf[:, 512:768]
        g2 = scrf[:, 768:1024]
        ism1 = scrf[:, 1024:1280]
        if wait_sig is not None:
            p.wait("vector", wait_sig)

        def fold(srcap, mdst, op):
            s3 = srcap.rearrange("p (g e) -> p g e", e=E)
            t16_3 = t16.rearrange("p (g e) -> p g e", e=4)
            t8_3 = t8.rearrange("p (g e) -> p g e", e=2)
            p.op("vector", "tensor_tensor", t16_3, s3[:, :, 0:4], s3[:, :, 4:8], op)
            p.drain("vector")
            p.op("vector", "tensor_tensor", t8_3, t16_3[:, :, 0:2], t16_3[:, :, 2:4], op)
            p.drain("vector")
            p.op("vector", "tensor_tensor", mdst, t8_3[:, :, 0:1].squeeze(2),
                 t8_3[:, :, 1:2].squeeze(2), op)
            p.drain("vector")

        def argfirst(srcap, ismdst):
            s3 = srcap.rearrange("p (g e) -> p g e", e=E)
            mexp = m1[:, :, None].broadcast_to([128, G, E])
            eq3 = eqb.rearrange("p (g e) -> p g e", e=E)
            p.op("vector", "tensor_tensor", eq3, s3, mexp, ALU.is_equal)
            p.drain("vector")
            p.op("vector", "scalar_tensor_tensor", zb, eqb, -16384.0, iotap[:],
                 ALU.mult, ALU.add)
            p.drain("vector")
            fold(zb, idx, ALU.min)
            iexp = idx[:, :, None].broadcast_to([128, G, E])
            im3 = ismdst.rearrange("p (g e) -> p g e", e=E)
            p.op("vector", "tensor_tensor", im3,
                 iota[:].rearrange("p (g e) -> p g e", e=E), iexp, ALU.is_equal)
            p.drain("vector")

        fold(g, m1, ALU.max)
        argfirst(g, ism1)
        p.op("vector", "scalar_tensor_tensor", g2, ism1, -1e30, g, ALU.mult, ALU.add)
        p.drain("vector")
        fold(g2, m1, ALU.max)
        argfirst(g2, member)
        p.op("vector", "tensor_tensor", member, ism1, member, ALU.add)
        return p.signal("vector")



def emit_L0a(p, cm, T, xT, o_xpart, o_xn, stop=None):
    NT16 = T // 128

    q_fm = p.region("a_q_fm", [128, CCN * T], BF16)
    k_fm = p.region("a_k_fm", [128, CCN * T], BF16)
    v_tm = p.region("a_v_tm", [128, NT16 * C], BF16)
    xnO = p.region("a_xnO", [128, CCN * T], BF16)
    wbuf = cm.wbuf
    P_all = p.region("a_P_all", [128, 4096], BF16)
    Pn_all = p.region("a_Pn_all", [128, 4096], BF16)
    PT_all = p.region("a_PT_all", [128, 4096], BF16)
    rsb = p.region("a_rsb", [128, 48], F32)
    bq = cm.loadc("l0a_bq", [128, 8], F32)
    bk = cm.loadc("l0a_bk", [128, 8], F32)
    bo = cm.loadc("l0a_bo", [128, 8], F32)
    bvr = cm.loadc("l0a_bv_row", [1, C], F32)
    bvr_bf = p.region("a_bvr_bf", [1, C], BF16)
    pen_g = cm.loadc("pen_g", [128, 256], F32)
    pen_0 = cm.loadc("pen_0", [128, 256], F32)

    lnsig = cm.emit_ln_stream(xT, xnO, q_fm, k_fm)

    if stop == "ln":
        p.finish()
        return
    cm.wait_consts(["vector"])
    p.op("vector", "tensor_copy", bvr_bf[:], bvr[:])
    bvsig = p.signal("vector")

    wq = p.din("l0a_WqT", [C, C], F32)
    wk = p.din("l0a_WkT", [C, C], F32)
    wv = p.din("l0a_WvT", [C, C], F32)
    wo = p.din("l0a_WoT", [C, C], F32)

    wsig = cm.load_weight("wq", wq, C, C, wbuf)
    qsig = cm.emit_proj_fm(wbuf, 0, xnO, q_fm, CCN, bias=bq,
                           wait_sigs=[lnsig, wsig])
    wsig = cm.load_weight("wk", wk, C, C, wbuf, war_sig=qsig)
    ksig = cm.emit_proj_fm(wbuf, 0, xnO, k_fm, CCN, bias=bk, wait_sigs=[wsig])

    wsig = cm.load_weight("wv", wv, C, C, wbuf, war_sig=ksig)
    vsig = cm.emit_v_tm(wbuf, xnO, v_tm, bvr_bf, wait_sigs=[wsig, bvsig], stride=1024)
    if stop == "vtm":
        p.finish()
        return

    if stop == "proj":
        p.finish()
        return
    if stop == "projq":
        cm.load_weight("wk2", wk, C, C, wbuf, war_sig=qsig)
        p.finish()
        return
    p.wait("sync", lnsig)
    xnstore = p.dma("sync", "out", o_xn.rearrange("(cc p) t -> p cc t", p=128),
                    xnO[:].rearrange("p (cc t) -> p cc t", cc=CCN))
    if stop == "xns":
        p.finish()
        return

    O_fm = xnO
    p.wait("vector", vsig)
    p.wait("tensor", vsig)
    p.wait("vector", p.dma_all("cst"))
    prev_qt_done = None
    s_sig = {}
    NQT = NT16 if stop is None or not str(stop).startswith("qt") else int(str(stop)[2:])
    for qt in range(NQT):
        pen = pen_0 if qt == 0 else pen_g
        p.wait("tensor", qsig)
        if stop == "qw":
            p.signal("tensor")
            p.finish()
            return
        NHEM = NH if stop is None or not str(stop).startswith("qh") else int(str(stop)[2:])
        if prev_qt_done is not None:
            p.wait("tensor", prev_qt_done)
        heads_ord = [hh for hh in range(NHEM) if hh % 2 == 0] + \
                    [hh for hh in range(NHEM) if hh % 2 == 1]
        for h in heads_ord:
            base = (h % 2) * 64
            qh = q_fm[base:base + 64,
                      (h // 2) * T + qt * 128: (h // 2) * T + (qt + 1) * 128]
            kh_all = k_fm[base:base + 64, (h // 2) * T:(h // 2 + 1) * T]
            if qt == 0:
                p.op("tensor", "matmul", p.ps32(h * 256 + 64, 128), qh,
                     kh_all[:, 0:128], start=True, stop=True)
            else:
                p.op("tensor", "matmul", p.ps32(h * 256, 192), qh,
                     kh_all[:, qt * 128 - 64: qt * 128 + 128], start=True, stop=True)
                p.op("tensor", "matmul", p.ps32(h * 256 + 192, 1), qh,
                     kh_all[:, 0:1], start=True, stop=True)
        ssig = p.signal("tensor")
        if stop == "qs" or (stop is not None and str(stop).startswith("qh")):
            p.finish(); return
        p.wait("vector", ssig)
        sview = p.PS.ap()[:, 0:4096].rearrange("p (h c) -> p h c", c=256)[:, :, 0:193]
        if qt == 0:
            full = p.PS.ap()[:, 0:4096].rearrange("p (h c) -> p h c", c=256)
            p.op("vector", "memset", full[:, :, 0:64], 0.0)
            p.op("vector", "memset", full[:, :, 192:193], 0.0)
            p.drain("vector")
        penv = pen[:].rearrange("p (g c) -> p g c", g=1)[:, :, 0:193].broadcast_to([128, NH, 193])
        p.op("vector", "tensor_tensor", sview, sview, penv, ALU.add)
        pensig = p.signal("vector")
        p.wait("scalar", pensig)
        pview = P_all[:].rearrange("p (h c) -> p h c", c=256)[:, :, 0:193]
        p.op("scalar", "activation", pview, sview, AF.Exp, bias=cm.fc[:, 1:2], scale=0.125)
        esig = p.signal("scalar")
        s_sig[qt] = esig
        if stop == "qe":
            p.finish(); return
        p.wait("vector", esig)
        p.op("vector", "tensor_reduce", rsb[:, 0:16], pview, AX.X, ALU.add)
        p.drain("vector")
        p.op("vector", "reciprocal", rsb[:, 16:32], rsb[:, 0:16])
        p.drain("vector")
        for h in range(NH):
            p.op("vector", "tensor_scalar", Pn_all[:, h * 256:h * 256 + 193],
                 P_all[:, h * 256:h * 256 + 193], rsb[:, 16 + h:17 + h], None, ALU.mult)
        nsig = p.signal("vector")
        if stop == "qn":
            p.finish(); return
        p.wait("tensor", esig)
        p.wait("tensor", nsig)
        for h in range(NH):
            p.op("tensor", "transpose", p.psbf(h * 128, 128),
                 Pn_all[:, h * 256 + 64: h * 256 + 192], cm.ident[:])
            p.op("tensor", "transpose", p.psbf(2048 + h * 128, 128, 64, 64),
                 Pn_all[:, h * 256 + 0: h * 256 + 64], cm.ident[:])
            p.op("tensor", "transpose", p.psbf(2048 + h * 128, 128, 0, 1),
                 Pn_all[:, h * 256 + 192: h * 256 + 193], cm.ident[:])
        trsig = p.signal("tensor")
        p.wait("vector", trsig)
        ptb = PT_all[:].rearrange("p (h two c) -> p h two c", two=2, c=128)
        psb_b = p.PSB[:, 0:2048].rearrange("p (h c) -> p h c", c=128)
        psb_ac = p.PSB[:, 2048:4096].rearrange("p (h c) -> p h c", c=128)
        p.op("vector", "tensor_copy", ptb[:, :, 0, :], psb_b)
        p.op("vector", "tensor_copy", ptb[64:128, :, 1, :], psb_ac[64:128])
        p.op("vector", "tensor_copy", ptb[0:1, :, 1, :], psb_ac[0:1])
        ptsig = p.signal("vector")
        if stop == "qtr":
            p.finish(); return
        p.wait("tensor", ptsig)
        for h in range(NH):
            out_ap = p.ps32(2048 + h * 128, 128, (h % 2) * 64, 64)
            mmA = (v_tm[64:128, (qt - 1) * C + h * 64: (qt - 1) * C + (h + 1) * 64],
                   PT_all[64:128, h * 256 + 128: h * 256 + 256]) if qt > 0 else None
            mmB = (v_tm[:, qt * C + h * 64: qt * C + (h + 1) * 64],
                   PT_all[:, h * 256: h * 256 + 128])
            mmC = (v_tm[0:1, h * 64:(h + 1) * 64],
                   PT_all[0:1, h * 256 + 128: h * 256 + 256])
            seq = [mmA, mmB, mmC] if h % 2 == 0 else [mmC, mmB, mmA]
            seq = [s for s in seq if s is not None]
            for i, (lh, rh) in enumerate(seq):
                p.op("tensor", "matmul", out_ap, lh, rh,
                     start=(i == 0), stop=(i == len(seq) - 1))
        pvsig = p.signal("tensor")
        p.wait("vector", pvsig)
        if qt == 0:
            p.wait("vector", xnstore)
        ops = p.PS.ap()[:, 2048:4096].rearrange("p (h c) -> p h c", c=128)
        ofm = O_fm[:].rearrange("p (oc t) -> p oc t", oc=CCN)[:, :, qt * 128:(qt + 1) * 128]
        p.op("vector", "tensor_copy", ofm[0:64], ops[0:64, 0::2, :])
        p.op("vector", "tensor_copy", ofm[64:128], ops[64:128, 1::2, :])
        prev_qt_done = p.signal("vector")
    if stop is not None and str(stop).startswith("qt"):
        p.finish()
        return

    if stop == "band":
        p.finish()
        return
    wsig = cm.load_weight("wo", wo, C, C, wbuf, war_sig=vsig)
    p.wait("tensor", prev_qt_done)
    cm.emit_proj_residual(wbuf, O_fm, xT, o_xpart, bias=bo, wait_sigs=[wsig])
    for s in ("out", "ro0", "ro1"):
        p.wait("sync", p.dma_all(s))
    return p



ARENA_F32 = 46750


def build_fused(T):
    p = Prog("FUSED")
    cm = Common(p, T)
    p.set_arena(ARENA_F32)
    xT = p.din("xT", [C, T], F32)
    o_y = p.dout("o_y", [C, T], F32)

    def ih(name, dt):
        return p.nc.dram_tensor(name, [C, T], dt, kind="Internal").ap()

    xpart = ih("h_xpart", F32)
    xn = ih("h_xn", BF16)
    x1 = ih("h_x1", F32)
    xn1 = ih("h_xn1", BF16)
    xn1f = ih("h_xn1f", F32)
    x2p = ih("h_x2p", F32)

    p.begin_phase("L0a")
    emit_L0a(p, cm, T, xT, xpart, xn)
    phase_barrier(p)
    p.begin_phase("L0f")
    emit_L0f(p, cm, T, xn, xpart, x1)
    phase_barrier(p)
    p.begin_phase("L1a")
    emit_L1a(p, cm, T, x1, x2p, xn1, xn1f)
    phase_barrier(p)
    p.begin_phase("L1f")
    emit_L1f(p, cm, T, xn1, xn1f, x2p, o_y)
    p.finish()
    return p


_CACHE = {}


def _prog(T):
    if T not in _CACHE:
        p = build_fused(T)
        nc = p.build()
        _CACHE[T] = (p, nc)
    return _CACHE[T]


def kernel(**inputs):
    x = np.asarray(inputs["L0_attn_hidden_states"])
    B, T, _ = x.shape
    A = prepare_host(inputs, T)
    p, nc = _prog(T)
    in_maps = []
    for b in range(B):
        m = {}
        for nm in p.input_names:
            m[nm] = A["__percore_xT"][b] if nm == "xT" else A[nm]
        in_maps.append(m)
    res = run_bass_kernel_spmd(nc, in_maps, core_ids=list(range(B)))
    out = np.stack([np.ascontiguousarray(res.results[b]["o_y"].T)
                    for b in range(B)], 0)
    return out.astype(np.float32)


def emit_wide_sum(p, eng, slots, terms, out):
    n = len(slots)
    acc = None
    i = 0
    while i < len(terms):
        free = [s for s in range(n) if s != acc]
        take = min(len(free), len(terms) - i)
        free = free[:take]
        for s, (a, b) in zip(free, terms[i:i + take]):
            p.op(eng, "tensor_tensor", slots[s], a, b, ALU.mult)
        p.drain(eng)
        cur = free
        while len(cur) > 1:
            nxt = []
            for j in range(0, len(cur) - 1, 2):
                p.op(eng, "tensor_tensor", slots[cur[j]], slots[cur[j]],
                     slots[cur[j + 1]], ALU.add)
                nxt.append(cur[j])
            if len(cur) % 2 == 1:
                nxt.append(cur[-1])
            p.drain(eng)
            cur = nxt
        if acc is None:
            acc = cur[0]
        else:
            p.op(eng, "tensor_tensor", slots[acc], slots[acc], slots[cur[0]], ALU.add)
            p.drain(eng)
        i += take
    p.op(eng, "tensor_copy", out, slots[acc])
    p.drain(eng)


def emit_L0f(p, cm, T, xnH, xpart, o_x1, stop=None):
    NT16 = T // 128
    QH = min(1024, T)
    NQH = T // QH
    NTH = QH // 128
    E = 8

    xn = p.region("f_xn", [128, CCN * T], BF16)
    qf = p.region("f_qf", [128, CCN * T], BF16)
    kf = p.region("f_kf", [128, CCN * T], BF16)
    gU = p.region("f_gU", [128, NT16 * 256], F32)
    U_fm = apof(gU).bitcast(BF16)
    pvn = max(NT16 * 512, 8 * QH)
    pvR = p.region("f_pvR", [128, pvn], BF16)
    v_tm = p.region("f_v_tm", [128, NT16 * 65 * NH], BF16)
    mix_v = p.region("f_mix_v", [128, NT16 * 128], BF16)
    mask_o = p.region("f_mask_o", [128, NT16 * 128], BF16)
    scr = p.region("f_scr", [128, 3072], F32)
    scrb = apof(scr).bitcast(BF16)
    wbuf = cm.wbuf
    iota = cm.loadc("iota", [128, 256], F32)
    iotap = cm.loadc("iotap", [128, 256], F32)

    pvf = apof(pvR).bitcast(F32)
    def Pslot(i):
        return pvR[:, i * QH:(i + 1) * QH]
    Ou = pvf[0:65, (3 * QH) // 2:(3 * QH) // 2 + QH]
    Rb = pvf[0:64, (3 * QH) // 2 + QH:(3 * QH) // 2 + 2 * QH]
    On = pvR[0:64, 7 * QH:8 * QH]

    xnsig = p.dma("sync", "xn", xn[:].rearrange("p (cc t) -> p cc t", cc=CCN),
                  xnH.rearrange("(cc p) t -> p cc t", p=128))

    wWq = p.din("l0f_WqT", [C, C], F32)
    wWk = p.din("l0f_WkT", [C, C], F32)
    wWsd = p.din("l0f_WsdT", [C, 256], F32)
    wWv = p.din("l0f_WvT", [C, 512], F32)
    wWo = p.din("l0f_WoT", [512, C], F32)

    wsig_v = cm.load_weight("wv", wWv, C, 512, wbuf)
    wsig_sd = cm.load_weight("wsd", wWsd, C, 256, wbuf, dst_off=4096)
    gsigs = cm.emit_proj_tm(wbuf, xn, gU, 256, wait_sigs=[xnsig, wsig_sd],
                            woff=4096, ret_all=True)
    pvsigs = cm.emit_proj_tm(wbuf, xn, pvR, 512,
                             wait_sigs=[wsig_v, gsigs[NT16 - 1]],
                             ret_all=True)

    member = scr[:, 1280:1536]
    cm.wait_consts(["vector", "scalar"])
    sgsig = {}
    for tt in range(NT16):
        p.wait("scalar", gsigs[tt])
        p.op("scalar", "activation", mix_v[:, tt * 128:(tt + 1) * 128],
             gU[:, tt * 256: tt * 256 + 128], AF.Sigmoid)
        sgsig[tt] = p.signal("scalar")
    tksigs = {}
    for tt in range(NT16):
        gt = gU[:, tt * 256:(tt + 1) * 256]
        p.wait("vector", gsigs[tt])
        cm.emit_top2(gt, member, scr, iota, iotap)
        p.wait("vector", sgsig[tt])
        p.drain("vector")
        p.op("vector", "tensor_tensor", mix_v[:, tt * 128:(tt + 1) * 128],
             mix_v[:, tt * 128:(tt + 1) * 128], member[:, 0:128], ALU.mult)
        p.op("vector", "tensor_copy", mask_o[:, tt * 128:(tt + 1) * 128],
             member[:, 128:256])
        tksigs[tt] = p.signal("vector")

    wsig = cm.load_weight("wq", wWq, C, C, wbuf, war_sig=pvsigs[NT16 - 1],
                          conv="scalar")
    qsig = cm.emit_proj_fm(wbuf, 0, xn, qf, CCN, wait_sigs=[wsig])
    wsig = cm.load_weight("wk", wWk, C, C, wbuf, war_sig=qsig, conv="scalar")
    ksig = cm.emit_proj_fm(wbuf, 0, xn, kf, CCN, wait_sigs=[wsig])

    vsig_tile = {}
    for tt in range(NT16):
        eng = "vector" if tt % 2 == 0 else "gpsimd"
        p.wait(eng, pvsigs[tt])
        p.wait(eng, tksigs[tt])
        p.drain(eng)
        v3 = v_tm[:, tt * 1040:(tt + 1) * 1040].rearrange("p (h e) -> p h e", e=65)
        p.op(eng, "memset", v3[:, :, 64:65], 1.0)
        p.drain(eng)
        if eng == "vector":
            slots = [scrb[:, i * 1024:(i + 1) * 1024]
                     .rearrange("p (h d) -> p h d", d=64) for i in range(4)]
        else:
            slots = [scrb[:, 4096 + i * 1024: 4096 + (i + 1) * 1024]
                     .rearrange("p (h d) -> p h d", d=64) for i in range(2)]
        terms = []
        for e in range(E):
            pvap = pvR[:, tt * 512 + e * 64: tt * 512 + (e + 1) * 64]
            a = pvap[:, None, :].broadcast_to([128, NH, 64])
            mx = mix_v[:, tt * 128 + e: tt * 128 + 120 + e + 1: 8]
            b = mx[:, :, None].broadcast_to([128, NH, 64])
            terms.append((a, b))
        emit_wide_sum(p, eng, slots, terms, v3[:, :, 0:64])
        vsig_tile[tt] = p.signal(eng)

    if stop == "front":
        p.finish()
        return
    wosig = cm.load_weight("wo", wWo, 512, C, wbuf, war_sig=ksig)
    if stop == "wo":
        p.finish()
        return

    p.wait("tensor", qsig)
    p.wait("tensor", ksig)
    prev_head_evac = None
    prev_head_exp = None
    usig0 = {}
    for half in range(NQH):
        O_tm = xn[:, half * NTH * 1024:(half + 1) * NTH * 1024]
        for h in range(NH):
            base = (h % 2) * 64
            qh = qf[base:base + 64,
                    (h // 2) * T + half * QH:(h // 2) * T + (half + 1) * QH]
            expsig = {}
            tsig = {}

            SOFF = (0, 1024, 3072)

            def smm(kt):
                kh = kf[base:base + 64,
                        (h // 2) * T + kt * 128:(h // 2) * T + (kt + 1) * 128]
                slot = kt % 3
                for qq in range(QH // 512 or 1):
                    w = min(512, QH)
                    p.op("tensor", "matmul", p.ps32(SOFF[slot] + qq * 512, w),
                         kh, qh[:, qq * 512: qq * 512 + w], start=True, stop=True)

            def pv(kt):
                slot = kt % 3
                p.wait("tensor", vsig_tile[kt])
                for qq in range(QH // 512 or 1):
                    w = min(512, QH)
                    p.op("tensor", "matmul",
                         p.ps32(2048 + qq * 512, w, 0, 65),
                         v_tm[:, kt * 1040 + h * 65: kt * 1040 + (h + 1) * 65],
                         Pslot(slot)[:, qq * 512: qq * 512 + w],
                         start=(kt == 0), stop=(kt == NT16 - 1))

            def expop(kt):
                slot = kt % 3
                p.wait("scalar", tsig[kt])
                p.wait("scalar", vsig_tile[2 * slot])
                p.wait("scalar", vsig_tile[2 * slot + 1])
                if kt == 0 and prev_head_evac is not None:
                    p.wait("scalar", prev_head_evac)
                p.op("scalar", "activation", Pslot(slot), p.ps32(SOFF[slot], QH),
                     AF.Exp, bias=cm.fc[:, 1:2], scale=0.125)
                expsig[kt] = p.signal("scalar")

            if prev_head_exp is not None:
                p.wait("tensor", prev_head_exp[0])
            smm(0)
            tsig[0] = p.signal("tensor")
            expop(0)
            for kt in range(1, NT16):
                if kt >= 3:
                    p.wait("tensor", expsig[kt - 3])
                elif prev_head_exp is not None:
                    p.wait("tensor", prev_head_exp[kt % 3])
                smm(kt)
                p.wait("tensor", expsig[kt - 1])
                if kt == 1 and prev_head_evac is not None:
                    p.wait("tensor", prev_head_evac)
                pv(kt - 1)
                tsig[kt] = p.signal("tensor")
                expop(kt)
            p.wait("tensor", expsig[NT16 - 1])
            pv(NT16 - 1)
            pvdone = p.signal("tensor")
            prev_head_exp = {max(0, NT16 - 3 + kk) % 3: expsig[max(0, NT16 - 3 + kk)]
                             for kk in range(3)}
            p.wait("scalar", pvdone)
            for t2 in (6, 7, 8, 9):
                p.wait("scalar", vsig_tile[t2])
            p.op("scalar", "activation", Ou, p.ps32(2048, QH, 0, 65), AF.Identity)
            p.drain("scalar")
            p.op("scalar", "activation", Ou[64:65, :], Ou[64:65, :], AF.Ln)
            p.drain("scalar")
            p.op("scalar", "activation", Ou[64:65, :], Ou[64:65, :], AF.Exp,
                 scale=-1.0)
            rsig = p.signal("scalar")
            p.wait("tensor", rsig)
            for qq in range(QH // 512 or 1):
                w = min(512, QH)
                p.op("tensor", "matmul", p.ps32(3072 + qq * 512, w, 0, 64),
                     cm.ones_f128[64:65, 0:64], Ou[64:65, qq * 512: qq * 512 + w],
                     start=True, stop=True)
            zbsig = p.signal("tensor")
            p.wait("vector", zbsig)
            p.wait("vector", vsig_tile[14])
            p.wait("vector", vsig_tile[15])
            p.op("vector", "tensor_tensor", On, Ou[0:64, :], p.ps32(3072, QH, 0, 64),
                 ALU.mult)
            onsig = p.signal("vector")
            p.wait("tensor", onsig)
            for j in range(NTH):
                p.op("tensor", "transpose", p.psbf(4096 + j * 64, 64),
                     On[:, j * 128:(j + 1) * 128], cm.ident[0:64, 0:64])
            trsig = p.signal("tensor")
            p.wait("vector", trsig)
            pview = p.PSB[:, 4096:4096 + NTH * 64].rearrange("p (j d) -> p j d", d=64)
            oview = O_tm.rearrange("p (j hd) -> p j hd", hd=1024)[:, :, h * 64:(h + 1) * 64]
            p.op("vector", "tensor_copy", oview, pview)
            prev_head_evac = p.signal("vector")

        if half == 0:
            p.wait("gpsimd", prev_head_evac)
            for j in range(NTH):
                p.wait("gpsimd", tksigs[j])
                p.drain("gpsimd")
                slots = [scrb[:, 4096 + i * 1024: 4096 + (i + 1) * 1024]
                         .rearrange("p (x d) -> p x d", d=64)[:, 0:E]
                         for i in range(2)]
                out_sl = U_fm[:, (j // 2) * T + 1024 + (j % 2) * 512:
                              (j // 2) * T + 1024 + (j % 2) * 512 + 512]
                terms = []
                for h2 in range(NH):
                    oap = O_tm[:, j * 1024 + h2 * 64: j * 1024 + (h2 + 1) * 64]
                    a = oap[:, None, :].broadcast_to([128, E, 64])
                    mk = mask_o[:, j * 128 + h2 * 8: j * 128 + (h2 + 1) * 8]
                    b = mk[:, :, None].broadcast_to([128, E, 64])
                    terms.append((a, b))
                emit_wide_sum(p, "gpsimd", slots, terms,
                              out_sl.rearrange("p (e d) -> p e d", d=64))
                usig0[j] = p.signal("gpsimd")

    if stop == "attn":
        p.finish()
        return
    usig1 = {}
    for j in range(0 if (stop or "").startswith(("tx", "tz")) else NTH):
        tt = NTH + j
        eng = "vector" if j % 2 == 0 else "gpsimd"
        p.wait(eng, prev_head_evac)
        p.wait(eng, tksigs[tt])
        p.drain(eng)
        if eng == "vector":
            slots = [scrb[:, i * 1024:(i + 1) * 1024]
                     .rearrange("p (x d) -> p x d", d=64)[:, 0:E]
                     for i in range(4)]
        else:
            slots = [scrb[:, 4096 + i * 1024: 4096 + (i + 1) * 1024]
                     .rearrange("p (x d) -> p x d", d=64)[:, 0:E]
                     for i in range(2)]
        out_sl = qf[:, j * 512:(j + 1) * 512]
        terms = []
        for h2 in range(NH):
            oap = O_tm[:, j * 1024 + h2 * 64: j * 1024 + (h2 + 1) * 64]
            a = oap[:, None, :].broadcast_to([128, E, 64])
            mk = mask_o[:, tt * 128 + h2 * 8: tt * 128 + (h2 + 1) * 8]
            b = mk[:, :, None].broadcast_to([128, E, 64])
            terms.append((a, b))
        emit_wide_sum(p, eng, slots, terms,
                      out_sl.rearrange("p (e d) -> p e d", d=64))
        usig1[j] = p.signal(eng)

    if stop == "u1":
        p.finish()
        return
    utrsig = {}
    NTTR = NT16
    NOEVAC = False
    SAMESRC = False
    if (stop or "").startswith("tr"):
        NTTR = int(stop[2:])
    if (stop or "").startswith("tn"):
        NTTR = int(stop[2:])
        NOEVAC = True
    if (stop or "").startswith(("ts", "tx", "tz")):
        NTTR = int(stop[2:])
        NOEVAC = True
        SAMESRC = True
    for tt0 in range(NTTR):
        tt = tt0
        ss = 0 if SAMESRC else tt
        j = ss % NTH
        if (stop or "").startswith("tz"):
            src = v_tm[:, 0:512]
        elif ss < NTH:
            src = U_fm[:, (j // 2) * T + 1024 + (j % 2) * 512:
                       (j // 2) * T + 1024 + (j % 2) * 512 + 512]
            p.wait("tensor", usig0[j])
        else:
            src = qf[:, j * 512:(j + 1) * 512]
            p.wait("tensor", usig1[j])
        if tt >= 2:
            p.wait("tensor", utrsig[tt - 2])
        rbase = 0 + (tt % 2) * 1024
        for cchunk in range(4):
            p.op("tensor", "transpose", p.psbf(rbase + cchunk * 128, 128),
                 src[:, cchunk * 128:(cchunk + 1) * 128], cm.ident[:])
        trs = p.signal("tensor")
        p.wait("vector", trs)
        if NOEVAC:
            utrsig[tt] = trs
            continue
        pv4 = p.PSB[:, rbase:rbase + 512].rearrange("p (cchunk d) -> p cchunk d", d=128)
        uview = U_fm[:].rearrange("p (cchunk t) -> p cchunk t", t=T)[
            :, :, tt * 128:(tt + 1) * 128]
        p.op("vector", "tensor_copy", uview, pv4)
        utrsig[tt] = p.signal("vector")

    if stop == "u" or (stop or "").startswith(("tr", "tn", "ts", "tx", "tz")):
        p.finish()
        return
    p.wait("sync", utrsig[NT16 - 1])
    cm.emit_proj_residual(wbuf, U_fm, xpart, o_x1, bias=None,
                          wait_sigs=[wosig, utrsig[NT16 - 1]], OC=CCN, CC=4,
                          Odim=C, xbuf=scr[:, 0:1024], obuf=scr[:, 1024:2048])
    for s in ("ro0", "ro1"):
        p.wait("sync", p.dma_all(s))


def build_L0f(T, stop=None):
    p = Prog("L0f")
    cm = Common(p, T)
    xnH = p.din("i_xn", [C, T], BF16)
    xpart = p.din("i_xpart", [C, T], F32)
    o_x1 = p.dout("o_x1", [C, T], F32)
    emit_L0f(p, cm, T, xnH, xpart, o_x1, stop=stop)
    return p


def emit_L1a(p, cm, T, x1, o_x2p, o_xn1, o_xn1f):
    NT16 = T // 128
    QH = min(1024, T)
    NQH = T // QH

    HW = min(1024, T)
    QW = min(512, QH)
    xn_need = max(CCN * T, 2 * (3 * HW + 4 * QW) + 2 * QH)
    xn = p.region("b_xn", [128, xn_need], BF16)
    q_fm = p.region("b_q_fm", [128, CCN * T], BF16)
    k_fm = p.region("b_k_fm", [128, CCN * T], BF16)
    v_tm = p.region("b_v_tm", [128, NT16 * 65 * NH], BF16)
    wbuf = cm.wbuf
    ropeR = p.region("b_ropeR", [128, max(4 * T, (CCN * QH + 3 * QH) // 2 + 2 * QH)], F32)
    rv = apof(ropeR)
    rbf = rv.bitcast(BF16)
    cosE = rv[:, 0:T] if T <= 2048 else None
    sinE = rv[:, T:2 * T]
    cosO = rv[:, 2 * T:3 * T]
    sinO = rv[:, 3 * T:4 * T]
    for nm, dstv in (("rope_cosE", cosE), ("rope_sinE", sinE),
                     ("rope_cosO", cosO), ("rope_sinO", sinO)):
        p.dma("sync", "cst", dstv, p.din(nm, [128, T], F32))
    O_fm = rbf[:, 0:CCN * QH]
    PP = rbf[:, CCN * QH: CCN * QH + 3 * QH]
    Ou = rv[0:65, (CCN * QH + 3 * QH) // 2: (CCN * QH + 3 * QH) // 2 + QH]
    Rb = rv[0:64, (CCN * QH + 3 * QH) // 2 + QH: (CCN * QH + 3 * QH) // 2 + 2 * QH]
    xnv = apof(xn).bitcast(F32)
    rt = xnv[:, 0:3 * HW]
    xbuf = xnv[:, 3 * HW:3 * HW + 2 * QW]
    obuf = xnv[:, 3 * HW + 2 * QW:3 * HW + 4 * QW]
    OnB = apof(xn)[0:64, 2 * (3 * HW + 4 * QW):2 * (3 * HW + 4 * QW) + QH]
    bq = cm.loadc("l1a_bq", [128, 8], F32)
    bk = cm.loadc("l1a_bk", [128, 8], F32)
    bo = cm.loadc("l1a_bo", [128, 8], F32)
    bvr = cm.loadc("l1a_bv_row", [1, C], F32)
    bvr_bf = p.region("b_bvr_bf", [1, C], BF16)
    ptri = cm.loadc("pen_tri", [128, 128], F32)
    qscr = p.nc.dram_tensor("qscr", [C, T], BF16, kind="Internal").ap()
    kscr = p.nc.dram_tensor("kscr", [C, T], BF16, kind="Internal").ap()

    xn1f3 = o_xn1f.rearrange("(cc p) t -> p cc t", p=128)
    xn1f3 = o_xn1f.rearrange("(cc p) t -> p cc t", p=128)

    def sink(cc, tmp, sig):
        p.wait("sync", sig)
        return p.dma("sync", "spill", xn1f3[:, cc], tmp[:])

    lnsig = cm.emit_ln_stream(x1, xn, q_fm, k_fm, fp32_sink=sink)
    p.wait("sync", lnsig)
    p.dma("sync", "out", o_xn1.rearrange("(cc p) t -> p cc t", p=128),
          apof(xn)[:, 0:CCN * T].rearrange("p (cc t) -> p cc t", cc=CCN))

    cm.wait_consts(["vector"])
    p.op("vector", "tensor_copy", bvr_bf[:], bvr[:])
    bvsig = p.signal("vector")

    wq = p.din("l1a_WqT", [C, C], F32)
    wk = p.din("l1a_WkT", [C, C], F32)
    wv = p.din("l1a_WvT", [C, C], F32)
    wo = p.din("l1a_WoT", [C, C], F32)

    wsig = cm.load_weight("wq", wq, C, C, wbuf)
    qsig = cm.emit_proj_fm(wbuf, 0, xn, q_fm, CCN, bias=bq,
                           wait_sigs=[lnsig, wsig, p.dma_all("spill"),
                                      p.dma_all("out")])
    wsig = cm.load_weight("wk", wk, C, C, wbuf, war_sig=qsig)
    ksig = cm.emit_proj_fm(wbuf, 0, xn, k_fm, CCN, bias=bk, wait_sigs=[wsig])
    wsig = cm.load_weight("wv", wv, C, C, wbuf, war_sig=ksig)
    vsig = cm.emit_v_tm(wbuf, xn, v_tm, bvr_bf, wait_sigs=[wsig, bvsig])

    p.wait("vector", qsig)
    p.wait("vector", ksig)
    p.wait("vector", vsig)
    p.wait("vector", p.dma_all("cst"))
    ropesig = None
    for buf in (q_fm, k_fm):
        for j in range(4):
            for hh in range(T // HW):
                lo = hh * HW
                cE = cosE[:, lo:lo + HW]
                sE = sinE[:, lo:lo + HW]
                cO = cosO[:, lo:lo + HW]
                sO = sinO[:, lo:lo + HW]
                qj = buf[:, j * T + lo: j * T + lo + HW]
                qj4 = buf[:, (j + 4) * T + lo: (j + 4) * T + lo + HW]
                t1, t4, t2 = (rt[:, 0:HW], rt[:, HW:2 * HW], rt[:, 2 * HW:3 * HW])
                p.op("vector", "tensor_tensor", t1, qj, cE, ALU.mult)
                p.op("vector", "tensor_tensor", t4, qj, sO, ALU.mult)
                p.op("vector", "tensor_tensor", t2, qj4, sE, ALU.mult)
                p.drain("vector")
                p.op("vector", "tensor_tensor", qj, t1, t2, ALU.subtract)
                p.drain("vector")
                p.op("vector", "tensor_tensor", t2, qj4, cO, ALU.mult)
                p.drain("vector")
                p.op("vector", "tensor_tensor", qj4, t2, t4, ALU.add)
                p.drain("vector")
    ropesig = p.signal("vector")

    p.wait("sync", ropesig)

    def repack_store(scr, srcbuf, sem):
        for cc in range(4):
            for a in range(2):
                for b in range(2):
                    g = 256 * cc + 128 * a + 64 * b
                    p.dma("sync", sem, scr[g:g + 32, :],
                          srcbuf[64 * a + 32 * b:64 * a + 32 * b + 32,
                                 cc * T:(cc + 1) * T])
                    p.dma("sync", sem, scr[g + 32:g + 64, :],
                          srcbuf[64 * a + 32 * b:64 * a + 32 * b + 32,
                                 (4 + cc) * T:(5 + cc) * T])

    repack_store(qscr, q_fm, "rpq")
    p.wait("sync", p.dma_all("rpq"))
    lq = p.dma("sync", "rpq2", q_fm[:].rearrange("p (cc t) -> p cc t", cc=CCN),
               qscr.rearrange("(cc p) t -> p cc t", p=128))
    repack_store(kscr, k_fm, "rpk")
    p.wait("sync", p.dma_all("rpk"))
    lk = p.dma("sync", "rpk2", k_fm[:].rearrange("p (cc t) -> p cc t", cc=CCN),
               kscr.rearrange("(cc p) t -> p cc t", p=128))
    q2 = q_fm
    k2 = k_fm
    rpsig = lk

    for e2 in ("tensor", "scalar"):
        p.wait(e2, ropesig)
        p.wait(e2, vsig)
        p.wait(e2, lq)
        p.wait(e2, rpsig)
    shiftsigs = []
    prev_head_evac = None
    prev_head_exp = None
    for half in range(NQH):
        qlo = half * QH
        kts = [kt for kt in range(NT16) if kt * 128 < qlo + QH]
        for h in range(NH):
            base = (h % 2) * 64
            qh = q2[base:base + 64, (h // 2) * T + qlo:(h // 2) * T + qlo + QH]
            expsig = {}
            tsig = {}
            nk = len(kts)

            SOFF = (0, 1024, 3072)

            def smm(ki):
                kt = kts[ki]
                delta = kt * 128 - qlo
                ql = max(0, delta)
                slot = ki % 3
                kh = k2[base:base + 64,
                        (h // 2) * T + kt * 128:(h // 2) * T + (kt + 1) * 128]
                qq = ql
                while qq < QH:
                    w = min(512 - (qq % 512), QH - qq)
                    p.op("tensor", "matmul", p.ps32(SOFF[slot] + qq, w),
                         kh, qh[:, qq:qq + w], start=True, stop=True)
                    qq += w

            def pv(ki):
                kt = kts[ki]
                delta = kt * 128 - qlo
                ql = max(0, delta)
                slot = ki % 3
                qq = ql
                while qq < QH:
                    w = min(512 - (qq % 512), QH - qq)
                    p.op("tensor", "matmul",
                         p.ps32(2048 + qq, w, 0, 65),
                         v_tm[:, kt * 1040 + h * 65: kt * 1040 + (h + 1) * 65],
                         PP[:, slot * QH + qq: slot * QH + qq + w],
                         start=(ki == 0), stop=(ki == nk - 1))
                    qq += w

            def expop(ki):
                kt = kts[ki]
                delta = kt * 128 - qlo
                ql = max(0, delta)
                slot = ki % 3
                if 0 <= delta:
                    p.wait("vector", tsig[ki])
                    sap = p.ps32(SOFF[slot] + delta, 128)
                    p.op("vector", "tensor_tensor", sap, sap, ptri[:], ALU.add)
                    pensig = p.signal("vector")
                    p.wait("scalar", pensig)
                p.wait("scalar", tsig[ki])
                if ki == 0 and prev_head_evac is not None:
                    p.wait("scalar", prev_head_evac)
                p.op("scalar", "activation",
                     PP[:, slot * QH + ql: (slot + 1) * QH],
                     p.ps32(SOFF[slot] + ql, QH - ql),
                     AF.Exp, bias=cm.fc[:, 1:2], scale=0.125)
                expsig[ki] = p.signal("scalar")

            if prev_head_exp is not None:
                p.wait("tensor", prev_head_exp[0])
            smm(0)
            tsig[0] = p.signal("tensor")
            expop(0)
            for ki in range(1, nk):
                if ki >= 3:
                    p.wait("tensor", expsig[ki - 3])
                elif prev_head_exp is not None:
                    p.wait("tensor", prev_head_exp.get(ki % 3, prev_head_exp[0]))
                smm(ki)
                p.wait("tensor", expsig[ki - 1])
                if ki == 1 and prev_head_evac is not None:
                    p.wait("tensor", prev_head_evac)
                pv(ki - 1)
                tsig[ki] = p.signal("tensor")
                expop(ki)
            p.wait("tensor", expsig[nk - 1])
            pv(nk - 1)
            pvdone = p.signal("tensor")
            prev_head_exp = {max(0, nk - 3 + kk) % 3: expsig[max(0, nk - 3 + kk)]
                             for kk in range(3)}
            p.wait("scalar", pvdone)
            p.op("scalar", "activation", Ou, p.ps32(2048, QH, 0, 65),
                 AF.Identity)
            p.drain("scalar")
            p.op("scalar", "activation", Ou[64:65, :], Ou[64:65, :], AF.Ln)
            p.drain("scalar")
            p.op("scalar", "activation", Ou[64:65, :], Ou[64:65, :], AF.Exp,
                 scale=-1.0)
            rsig = p.signal("scalar")
            p.wait("tensor", rsig)
            for qq in range(QH // 512 or 1):
                w = min(512, QH)
                p.op("tensor", "matmul", p.ps32(3072 + qq * 512, w, 0, 64),
                     cm.ones_f128[64:65, 0:64], Ou[64:65, qq * 512: qq * 512 + w],
                     start=True, stop=True)
            zbsig = p.signal("tensor")
            p.wait("vector", zbsig)
            dst = O_fm[0:64, (h // 2) * QH:(h // 2 + 1) * QH] if h % 2 == 0 \
                else OnB
            p.op("vector", "tensor_tensor", dst, Ou[0:64, :],
                 p.ps32(3072, QH, 0, 64), ALU.mult)
            onsig = p.signal("vector")
            if h % 2 == 1:
                p.wait("sync", onsig)
                ssig = p.dma("sync", "shift%d" % (h % 4 // 2),
                             O_fm[64:128, (h // 2) * QH:(h // 2 + 1) * QH],
                             OnB)
                shiftsigs.append(ssig)
                p.wait("vector", ssig)
            prev_head_evac = onsig

        wosig = cm.load_weight("wo", wo, C, C, wbuf, war_sig=vsig) if half == 0 else wosig
        for s in shiftsigs:
            p.wait("tensor", s)
        shiftsigs = []
        p.wait("tensor", prev_head_evac)
        x1h = x1.rearrange("(cc p) t -> p cc t", p=128)
        o2h = o_x2p.rearrange("(cc p) t -> p cc t", p=128)
        QW = min(512, QH)
        NTW = QH // QW
        dsigs = {}
        stsigs = {}

        def pre(oc, tc):
            kk = oc * NTW + tc
            slot = kk % 2
            if kk >= 2:
                p.wait("sync", stsigs[kk - 2])
            dsigs[kk] = p.dma("sync", "resx%d" % slot,
                              xbuf[:, slot * QW:(slot + 1) * QW],
                              x1h[:, oc, qlo + tc * QW: qlo + (tc + 1) * QW])

        def evac(oc, tc, ps):
            kk = oc * NTW + tc
            slot = kk % 2
            p.wait("vector", dsigs[kk])
            if kk >= 2:
                p.wait("vector", stsigs[kk - 2])
            p.op("vector", "scalar_tensor_tensor",
                 obuf[:, slot * QW:(slot + 1) * QW], ps, bo[:, oc:oc + 1],
                 xbuf[:, slot * QW:(slot + 1) * QW], ALU.add, ALU.add)
            esig = p.signal("vector")
            p.wait("sync", esig)
            stsigs[kk] = p.dma("sync", "ro%d" % slot,
                               o2h[:, oc, qlo + tc * QW: qlo + (tc + 1) * QW],
                               obuf[:, slot * QW:(slot + 1) * QW])

        class _FakeT:
            pass
        cm2T = cm.T
        cm.T = QH
        wores = cm.emit_proj_fm(wbuf, 0, O_fm, None, CCN, wait_sigs=[wosig],
                                evac_engine="vector", evac_cb=evac, pre_mm=pre)
        cm.T = cm2T
        p.wait("tensor", wores)
    for s in ("out", "spill", "ro0", "ro1"):
        p.wait("sync", p.dma_all(s))


def build_L1a(T):
    p = Prog("L1a")
    cm = Common(p, T)
    x1 = p.din("i_x1", [C, T], F32)
    o_x2p = p.dout("o_x2p", [C, T], F32)
    o_xn1 = p.dout("o_xn1", [C, T], BF16)
    o_xn1f = p.dout("o_xn1f", [C, T], F32)
    emit_L1a(p, cm, T, x1, o_x2p, o_xn1, o_xn1f)
    return p


def emit_L1f(p, cm, T, xn1b, xn1f, x2p, o_y):
    QH = min(1024, T)
    NQH = T // QH
    QW = min(512, QH)

    xnb = p.region("g_xnb", [128, max(CCN * T, 2 * CCN * QH)], BF16)
    OUT = apof(xnb).bitcast(F32)[:, 0:CCN * QH]
    outbf = p.region("g_outbf", [128, CCN * QH], BF16)
    h1b = p.region("g_h1b", [128, CCN * QH], BF16)
    h2b = p.region("g_h2b", [128, 4 * QH], BF16)
    h3b = p.region("g_h3b", [128, 2 * QH], BF16)
    tw = p.region("g_tw", [128, 2 * QH], F32)
    rows = p.region("g_rows", [1, 2 * QH], F32)
    gsml = p.region("g_gsml", [128, 32], F32)
    gsmb = p.region("g_gsmb", [128, 32], BF16)
    gf_sb = p.region("g_gf_sb", [1, 8], F32)
    w1 = p.region("g_w1", [128, CCN * 1024], BF16)
    w2 = p.region("g_w2", [128, CCN * 512], BF16)
    w3 = p.region("g_w3", [128, 4 * 256], BF16)
    w4 = p.region("g_w4", [128, 2 * 1], BF16)
    gw1 = p.region("g_gw1", [128, CCN * 512], BF16)
    gw2 = p.region("g_gw2", [128, 4 * 256], BF16)
    gw3 = p.region("g_gw3", [128, 2 * 1], BF16)
    b1 = cm.loadc("l1f_cn_b1", [128, 8], F32)
    b2 = cm.loadc("l1f_cn_b2", [128, 4], F32)
    b3 = cm.loadc("l1f_cn_b3", [128, 2], F32)
    b4 = cm.loadc("l1f_cn_b4", [1, 1], F32)
    gb1 = cm.loadc("l1f_gc_b1", [128, 4], F32)
    gb2 = cm.loadc("l1f_gc_b2", [128, 2], F32)
    gb3 = cm.loadc("l1f_gc_b3", [1, 1], F32)
    arc = cm.loadc("l1f_ar", [1, 1], F32)

    xnsig = p.dma("sync", "xn2",
                  apof(xnb)[:, 0:CCN * T].rearrange("p (cc t) -> p cc t", cc=CCN),
                  xn1b.rearrange("(cc p) t -> p cc t", p=128))

    wsigs = []
    for nm, hbm_shape, dst, R, O in (
            ("l1f_cn_w1", [C, C], w1, C, 1024),
            ("l1f_cn_w2", [C, 512], w2, C, 512),
            ("l1f_cn_w3", [512, 256], w3, 512, 256),
            ("l1f_cn_w4", [1, 256], w4, 256, 1),
            ("l1f_gc_w1", [C, 512], gw1, C, 512),
            ("l1f_gc_w2", [512, 256], gw2, 512, 256),
            ("l1f_gc_w3", [1, 256], gw3, 256, 1)):
        hbm = p.din(nm, hbm_shape, F32)
        wsigs.append(cm.load_weight(nm, hbm, R, O, dst))
    wsig = wsigs[-1]

    cm.wait_consts(["vector", "scalar", "tensor"])
    p.wait("vector", xnsig)
    for cc in range(CCN):
        p.op("vector", "tensor_reduce", gsml[:, cc:cc + 1],
             xnb[:, cc * T:(cc + 1) * T], AX.X, ALU.add)
    p.drain("vector")
    p.op("vector", "tensor_scalar", gsml[:, 0:8], gsml[:, 0:8], 1.0 / T, None,
         ALU.mult)
    p.drain("vector")
    p.op("vector", "tensor_copy", gsmb[:, 0:8], gsml[:, 0:8])
    gcsig = p.signal("vector")
    for s in wsigs:
        p.wait("tensor", s)
    p.wait("tensor", gcsig)
    for oc in range(4):
        ps = p.ps32(3584, 1)
        for cc in range(CCN):
            p.op("tensor", "matmul", ps, gw1[:, cc * 512 + oc * 128:
                                             cc * 512 + (oc + 1) * 128],
                 gsmb[:, cc:cc + 1], start=(cc == 0), stop=(cc == CCN - 1))
        msig = p.signal("tensor")
        p.wait("scalar", msig)
        p.op("scalar", "activation", gsml[:, 8 + oc:9 + oc], ps, AF.Gelu,
             bias=gb1[:, oc:oc + 1])
        p.drain("scalar")
        p.op("scalar", "copy", gsmb[:, 8 + oc:9 + oc], gsml[:, 8 + oc:9 + oc])
        esig = p.signal("scalar")
        p.wait("tensor", esig)
    for oc in range(2):
        ps = p.ps32(3584, 1)
        for cc in range(4):
            p.op("tensor", "matmul", ps, gw2[:, cc * 256 + oc * 128:
                                             cc * 256 + (oc + 1) * 128],
                 gsmb[:, 8 + cc:9 + cc], start=(cc == 0), stop=(cc == 3))
        msig = p.signal("tensor")
        p.wait("scalar", msig)
        p.op("scalar", "activation", gsml[:, 12 + oc:13 + oc], ps, AF.Gelu,
             bias=gb2[:, oc:oc + 1])
        p.drain("scalar")
        p.op("scalar", "copy", gsmb[:, 12 + oc:13 + oc], gsml[:, 12 + oc:13 + oc])
        esig = p.signal("scalar")
        p.wait("tensor", esig)
    ps = p.ps32(3584, 1, 0, 1)
    for cc in range(2):
        p.op("tensor", "matmul", ps, gw3[:, cc:cc + 1], gsmb[:, 12 + cc:13 + cc],
             start=(cc == 0), stop=(cc == 1))
    msig = p.signal("tensor")
    p.wait("scalar", msig)
    p.op("scalar", "activation", gf_sb[0:1, 0:1], ps, AF.Sigmoid,
         bias=gb3[0:1, 0:1])
    gfsig = p.signal("scalar")
    p.wait("vector", gfsig)
    p.op("vector", "tensor_scalar", gf_sb[0:1, 1:2], gf_sb[0:1, 0:1], 0.7, None,
         ALU.mult)
    p.op("vector", "tensor_scalar", gf_sb[0:1, 2:3], arc[0:1, 0:1], 2.0, None,
         ALU.mult)
    gf7sig = p.signal("vector")

    xn1f3 = xn1f.rearrange("(cc p) t -> p cc t", p=128)
    x2p3 = x2p.rearrange("(cc p) t -> p cc t", p=128)
    oy3 = o_y.rearrange("(cc p) t -> p cc t", p=128)
    xbuf = p.region("g_res_x", [128, 2 * QW], F32)
    obuf = p.region("g_res_o", [128, 2 * QW], F32)
    tf_row = rows[0:1, 0:QH]
    cp_row = rows[0:1, QH:2 * QH]
    cm2T = cm.T
    cm.T = QH
    for half in range(NQH):
        qlo = half * QH
        p.wait("sync", gcsig)
        osig = p.dma("sync", "o%d" % (half % 2),
                     OUT[:].rearrange("p (cc t) -> p cc t", cc=CCN),
                     xn1f3[:, :, qlo:qlo + QH])
        cvsig = None
        for cyc in range(3):
            p.wait("vector", osig)
            for cc in range(CCN):
                p.op("vector", "tensor_copy", outbf[:, cc * QH:(cc + 1) * QH],
                     OUT[:, cc * QH:(cc + 1) * QH])
            cvsig = p.signal("vector")
            h1sig = cm.emit_proj_fm(apof(w1), 0, apof(outbf), apof(h1b), CCN,
                                    bias=b1, act=AF.Gelu, wait_sigs=[cvsig])
            h2sig = cm.emit_proj_fm(apof(w2), 0, apof(h1b), apof(h2b), 4, bias=b2,
                                    act=AF.Gelu, wait_sigs=[h1sig], Odim=512)
            h3sig = cm.emit_proj_fm(apof(w3), 0, apof(h2b), apof(h3b), 2, bias=b3,
                                    act=AF.Gelu, wait_sigs=[h2sig], CC=4, Odim=256)
            p.wait("tensor", h3sig)
            for qq in range(QH // QW):
                ps2 = p.ps32(2048 + qq * QW, QW, 0, 1)
                for cc in range(2):
                    p.op("tensor", "matmul", ps2, w4[:, cc:cc + 1],
                         h3b[:, cc * QH + qq * QW: cc * QH + (qq + 1) * QW],
                         start=(cc == 0), stop=(cc == 1))
            tfmsig = p.signal("tensor")
            p.wait("scalar", tfmsig)
            p.op("scalar", "activation", tf_row, p.ps32(2048, QH, 0, 1),
                 AF.Sigmoid, bias=b4[0:1, 0:1])
            tfsig = p.signal("scalar")
            p.wait("vector", tfsig)
            p.wait("vector", gf7sig)
            p.op("vector", "tensor_scalar", cp_row, tf_row, 0.3,
                 gf_sb[0:1, 1:2], ALU.mult, ALU.add)
            p.drain("vector")
            p.op("vector", "tensor_scalar", cp_row, cp_row, 0.5,
                 gf_sb[0:1, 2:3], ALU.subtract, ALU.mult)
            p.drain("vector")
            p.op("vector", "tensor_scalar", cp_row, cp_row, 0.1, None, ALU.add)
            cpsig = p.signal("vector")
            p.wait("tensor", cpsig)
            for qq in range(QH // QW):
                p.op("tensor", "matmul", p.ps32(3072 + qq * QW, QW),
                     cm.ones_row_f[:], cp_row[0:1, qq * QW:(qq + 1) * QW],
                     start=True, stop=True)
            cpbsig = p.signal("tensor")
            p.wait("vector", cpbsig)
            CP = p.ps32(3072, QH)
            for pr, sa, sb_ in (((0, 4), 1.0, -1.0), ((1, 5), 1.0, -1.0),
                                ((2, 6), -1.0, 1.0), ((3, 7), -1.0, 1.0)):
                cA, cB = pr
                a_ap = OUT[:, cA * QH:(cA + 1) * QH]
                b_ap = OUT[:, cB * QH:(cB + 1) * QH]
                p.op("vector", "tensor_tensor", tw[:, 0:QH], CP, b_ap, ALU.mult)
                p.op("vector", "tensor_tensor", tw[:, QH:2 * QH], CP, a_ap, ALU.mult)
                p.drain("vector")
                p.op("vector", "scalar_tensor_tensor", a_ap, tw[:, 0:QH], sa,
                     a_ap, ALU.mult, ALU.add)
                p.op("vector", "scalar_tensor_tensor", b_ap, tw[:, QH:2 * QH], sb_,
                     b_ap, ALU.mult, ALU.add)
                p.drain("vector")
            twsig = p.signal("vector")
            p.wait("vector", twsig)
        stsigs = {}
        dsigs = {}
        for k in range(CCN * (QH // QW)):
            oc, tc = divmod(k, QH // QW)
            slot = k % 2
            if k >= 2:
                p.wait("sync", stsigs[k - 2])
            dsigs[k] = p.dma("sync", "resx%d" % slot,
                             xbuf[:, slot * QW:(slot + 1) * QW],
                             x2p3[:, oc, qlo + tc * QW: qlo + (tc + 1) * QW])
            p.wait("vector", dsigs[k])
            if k >= 2:
                p.wait("vector", stsigs[k - 2])
            p.op("vector", "tensor_tensor", obuf[:, slot * QW:(slot + 1) * QW],
                 OUT[:, oc * QH + tc * QW: oc * QH + (tc + 1) * QW],
                 xbuf[:, slot * QW:(slot + 1) * QW], ALU.add)
            esig = p.signal("vector")
            p.wait("sync", esig)
            stsigs[k] = p.dma("sync", "ro%d" % slot,
                              oy3[:, oc, qlo + tc * QW: qlo + (tc + 1) * QW],
                              obuf[:, slot * QW:(slot + 1) * QW])
        p.wait("sync", p.dma_all("ro0"))
        p.wait("sync", p.dma_all("ro1"))
    cm.T = cm2T


def build_L1f(T):
    p = Prog("L1f")
    cm = Common(p, T)
    xn1b = p.din("i_xn1", [C, T], BF16)
    xn1f = p.din("i_xn1f", [C, T], F32)
    x2p = p.din("i_x2p", [C, T], F32)
    o_y = p.dout("o_y", [C, T], F32)
    emit_L1f(p, cm, T, xn1b, xn1f, x2p, o_y)
    return p


# version 17
# speedup vs baseline: 1.2054x; 1.2054x over previous
import numpy as np
import ml_dtypes
from contextlib import ExitStack

import concourse.bass as bass
import concourse.mybir as mybir
from concourse.bass_utils import run_bass_kernel_spmd

F32 = mybir.dt.float32
BF16 = mybir.dt.bfloat16
AF = mybir.ActivationFunctionType
ALU = mybir.AluOpType
AX = mybir.AxisListType

C = 1024
CCN = 8
NH = 16
HD = 64
M0 = 12.0
NEG = -1e9
EPS = 1e-5


ENGS = ("sync", "tensor", "scalar", "vector", "gpsimd")


def apof(x):
    return x.ap() if callable(getattr(x, "ap", None)) else x


def phase_barrier(p):
    for name in list(p.counts):
        if not name.startswith("c_"):
            p.wait("sync", (name, p.counts[name]))
    sigs = {e: p.signal(e) for e in ENGS}
    for e in ENGS:
        for e2, s in sigs.items():
            if e2 != e:
                p.wait(e, s)


class Prog:
    def __init__(self, name):
        self.name = name
        self.nc = bass.Bass("TRN2", target_bir_lowering=False)
        self.es = ExitStack()
        self.queues = {e: [] for e in ENGS}
        self.semh = {}
        self.counts = {}
        self.watermark = {}
        self.input_names = []
        self.output_names = []
        self._sb = {}
        self.sim_trace = {}

    def din(self, name, shape, dtype):
        self.input_names.append(name)
        return self.nc.dram_tensor(name, list(shape), dtype, kind="ExternalInput").ap()

    def dout(self, name, shape, dtype):
        self.output_names.append(name)
        return self.nc.dram_tensor(name, list(shape), dtype, kind="ExternalOutput").ap()

    def sb(self, name, shape, dtype):
        h = self.es.enter_context(self.nc.sbuf_tensor(name, list(shape), dtype))
        self._sb[name] = h
        return h

    def set_arena(self, f32_cols):
        self.arena = self.sb("ARENA", [128, f32_cols], F32)
        self._arena_off = 0
        self._arena_peak = 0

    def begin_phase(self, name):
        self._phase = name
        self._arena_off = 0

    def region(self, name, shape, dtype):
        if getattr(self, "arena", None) is None:
            return self.sb(name, shape, dtype)
        esz = 4 if dtype == F32 else 2
        off = (self._arena_off + 7) // 8 * 8
        self._arena_off = off + shape[1] * esz
        self._arena_peak = max(self._arena_peak, self._arena_off)
        assert self._arena_off <= self.arena.shape[1] * 4, \
            f"arena overflow in {getattr(self, '_phase', '?')}: {self._arena_off}"
        return self.arena.ap().bitcast(dtype)[0:shape[0],
                                              off // esz: off // esz + shape[1]]

    def psum(self):
        self.PS = self.es.enter_context(self.nc.psum_tensor("PS", [128, 4096], F32))
        self.PSB = self.PS.ap().bitcast(BF16)
        return self.PS

    def ps32(self, col0, width, p0=0, pn=128):
        return self.PS.ap()[p0:p0 + pn, col0:col0 + width]

    def psbf(self, col0, width, p0=0, pn=128):
        return self.PSB[p0:p0 + pn, col0:col0 + width]

    def _sem(self, name):
        if name not in self.semh:
            self.semh[name] = self.es.enter_context(self.nc.semaphore(name))
            self.counts[name] = 0
        return self.semh[name]

    def _push(self, eng, fn, flaggable=False):
        entry = {"fn": fn, "inc": None}
        self.queues[eng].append(entry)
        if flaggable:
            self._lastop = getattr(self, "_lastop", {})
            self._lastop[eng] = entry
        return entry

    def raw(self, eng, fn):
        self._push(eng, fn)

    def op(self, eng, opname, *args, **kw):
        if eng == "tensor" and opname in ("matmul", "transpose"):
            lhs = args[1]
            b = lhs.base_partition()
            n = lhs.partition_size()
            key = (b, b + n)
            last = getattr(self, "_pe_rows", None)
            if last is not None and (key[0] >= last[1] or key[1] <= last[0]):
                self._push(eng, lambda e: e.drain())
            self._pe_rows = key
        self._push(eng, lambda e: getattr(e, opname)(*args, **kw), flaggable=True)

    def drain(self, eng):
        if eng == "tensor":
            self._pe_rows = None
        self._push(eng, lambda e: e.drain())

    def signal(self, eng, sem=None, force_drain=False):
        if eng == "tensor":
            self._pe_rows = None
        sem = sem or ("c_" + eng)
        h = self._sem(sem)
        self.counts[sem] += 1
        v = self.counts[sem]
        self.sim_trace.setdefault(eng, []).append(("inc", sem, 1))
        self._push(eng, lambda e: e.drain())
        self._push(eng, lambda e: e.nop())
        self.queues[eng][-1]["inc"] = (h, 1)
        return (sem, v)

    def wait(self, eng, sig):
        sem, v = sig
        if sem == "c_" + eng:
            return
        h = self._sem(sem)
        key = (eng, sem)
        if self.watermark.get(key, 0) >= v:
            return
        self.watermark[key] = v
        self.sim_trace.setdefault(eng, []).append(("wait", sem, v))
        self._push(eng, lambda e: e.wait_ge(h, v))

    def dma(self, eng, sem, out, in_):
        h = self._sem(sem)
        self.counts[sem] += 16
        v = self.counts[sem]
        self.sim_trace.setdefault(eng, []).append(("inc", sem, 16))
        self._push(eng, lambda e: e.dma_start(out=out, in_=in_).then_inc(h, 16))
        return (sem, v)

    def dma_all(self, sem):
        self._sem(sem)
        return (sem, self.counts[sem])

    def finish(self):
        for name in list(self.counts):
            if not name.startswith("c_"):
                self.wait("sync", (name, self.counts[name]))

    def build(self):
        with self.es:
            with self.nc.Block() as block:
                for e in ENGS:
                    q = self.queues[e]
                    if not q:
                        continue
                    def runner(eng, q=q):
                        for item in q:
                            inst = item["fn"](eng)
                            if item["inc"] is not None:
                                h, n = item["inc"]
                                inst.then_inc(h, n)
                    getattr(block, e)(runner)
        return self.nc



def _band_penalty(T):
    LOCAL = 64
    def mask_valid(i, j):
        return (j <= i) and (j >= 0) and ((abs(i - j) <= LOCAL) or (i == 0) or (j == 0))
    pens = {}
    for variant in ("g", "0"):
        pen = np.full((128, 256), NEG, dtype=np.float32)
        qt = 1 if variant == "g" else 0
        base = qt * 128
        for r in range(128):
            i = base + r
            for c in range(192):
                j = base - 64 + c
                if mask_valid(i, j):
                    pen[r, c] = 0.0
            if variant == "g":
                if mask_valid(i, 0):
                    pen[r, 192] = 0.0
        pens[variant] = pen
    return pens["g"], pens["0"]


def _chunk_bias(b):
    return np.ascontiguousarray(b.reshape(CCN, 128).T.astype(np.float32))


def prepare_host(inputs, T):
    A = {}
    f32 = np.float32
    x = np.asarray(inputs["L0_attn_hidden_states"], dtype=f32)
    B = x.shape[0]
    A["__percore_xT"] = [np.ascontiguousarray(x[b].T) for b in range(B)]

    for nm in ("Wq", "Wk", "Wv", "Wo"):
        W = np.asarray(inputs[f"L0_attn_{nm}"], dtype=f32)
        A[f"l0a_{nm}T"] = np.ascontiguousarray(W.T)
    for nm in ("bq", "bk", "bo"):
        A[f"l0a_{nm}"] = _chunk_bias(np.asarray(inputs[f"L0_attn_{nm}"], dtype=f32))
    A["l0a_bv_row"] = np.asarray(inputs["L0_attn_bv"], dtype=f32).reshape(1, C)

    Wq = np.asarray(inputs["L0_ffn_Wq"], dtype=f32)
    Wk = np.asarray(inputs["L0_ffn_Wk"], dtype=f32)
    A["l0f_WqT"] = np.ascontiguousarray(Wq.reshape(C, C).T)
    A["l0f_WkT"] = np.ascontiguousarray(Wk.reshape(C, C).T)
    Ws = np.asarray(inputs["L0_ffn_Ws"], dtype=f32).reshape(128, C)
    Wd = np.asarray(inputs["L0_ffn_Wd"], dtype=f32).reshape(128, C)
    A["l0f_WsdT"] = np.ascontiguousarray(np.concatenate([Ws, Wd], 0).T)
    Wv = np.asarray(inputs["L0_ffn_Wv"], dtype=f32).reshape(512, C)
    A["l0f_WvT"] = np.ascontiguousarray(Wv.T)
    Wo = np.asarray(inputs["L0_ffn_Wo"], dtype=f32)
    A["l0f_WoT"] = np.ascontiguousarray(
        Wo.transpose(0, 2, 1).reshape(512, C))

    perm = np.zeros(C, dtype=np.int64)
    for r in range(512):
        perm[r] = (r // 32) * 64 + 2 * (r % 32)
        perm[512 + r] = (r // 32) * 64 + 2 * (r % 32) + 1
    for nm in ("Wq", "Wk"):
        W = np.asarray(inputs[f"L1_attn_{nm}"], dtype=f32)
        A[f"l1a_{nm}T"] = np.ascontiguousarray(W[:, perm])
    A["l1a_bq"] = _chunk_bias(np.asarray(inputs["L1_attn_bq"], dtype=f32)[perm])
    A["l1a_bk"] = _chunk_bias(np.asarray(inputs["L1_attn_bk"], dtype=f32)[perm])
    A["l1a_WvT"] = np.ascontiguousarray(np.asarray(inputs["L1_attn_Wv"], f32))
    A["l1a_bv_row"] = np.asarray(inputs["L1_attn_bv"], dtype=f32).reshape(1, C)
    A["l1a_WoT"] = np.ascontiguousarray(np.asarray(inputs["L1_attn_Wo"], f32))
    A["l1a_bo"] = _chunk_bias(np.asarray(inputs["L1_attn_bo"], dtype=f32))

    pos = np.arange(T, dtype=np.float32)
    freq = np.arange(0, HD, 2, dtype=np.float32) / HD
    inv_freq = (1.0 / 10000.0 ** freq).astype(np.float32)
    ang = np.outer(pos, inv_freq).astype(np.float32)
    ang2 = np.repeat(ang, 2, axis=-1)
    cos = np.cos(ang2).astype(f32)
    sin = np.sin(ang2).astype(f32)
    for nm, tab in (("cosE", cos[:, 0::2]), ("sinE", sin[:, 0::2]),
                    ("cosO", cos[:, 1::2]), ("sinO", sin[:, 1::2])):
        A[f"rope_{nm}"] = np.ascontiguousarray(np.tile(tab.T, (4, 1)))

    for nm in ("cn_w1", "cn_w2", "cn_w3", "gc_w1", "gc_w2"):
        A[f"l1f_{nm}"] = np.ascontiguousarray(np.asarray(inputs[f"L1_ffn_{nm}"], f32))
    for nm in ("cn_w4", "gc_w3"):
        w = np.asarray(inputs[f"L1_ffn_{nm}"], f32).reshape(-1)
        RC = w.shape[0] // 128
        A[f"l1f_{nm}"] = np.ascontiguousarray(
            w.reshape(RC, 128).T.reshape(1, -1))
    for nm, dim in (("cn_b1", 1024), ("cn_b2", 512), ("cn_b3", 256),
                    ("gc_b1", 512), ("gc_b2", 256)):
        b = np.asarray(inputs[f"L1_ffn_{nm}"], dtype=f32)
        A[f"l1f_{nm}"] = np.ascontiguousarray(b.reshape(dim // 128, 128).T)
    A["l1f_cn_b4"] = np.asarray(inputs["L1_ffn_cn_b4"], f32).reshape(1, 1)
    A["l1f_gc_b3"] = np.asarray(inputs["L1_ffn_gc_b3"], f32).reshape(1, 1)
    A["l1f_ar"] = np.asarray(inputs["L1_ffn_adaptive_range"], f32).reshape(1, 1)

    A["ident_bf"] = np.eye(128, dtype=ml_dtypes.bfloat16)
    A["ones_row_f"] = np.ones((1, 128), dtype=f32)
    A["ones_f128"] = np.ones((128, 64), dtype=f32)
    A["ones_row_bf"] = np.ones((1, 128), dtype=ml_dtypes.bfloat16)
    A["ones_col_bf"] = np.ones((128, 1), dtype=ml_dtypes.bfloat16)
    pen_g, pen_0 = _band_penalty(T)
    A["pen_g"] = pen_g
    A["pen_0"] = pen_0
    tri = np.zeros((128, 128), dtype=f32)
    for r in range(128):
        tri[r, :r] = NEG
    A["pen_tri"] = tri
    iota = np.tile(np.arange(8, dtype=f32), (128, 32))
    A["iota"] = iota
    A["iotap"] = iota + 16384.0
    fc = np.zeros((128, 8), dtype=f32)
    fc[:, 0] = EPS
    fc[:, 1] = -M0
    A["fconsts"] = fc
    return A



class Common:

    def __init__(self, p: Prog, T):
        self.p = p
        self.T = T
        nc = p.nc
        p.psum()
        self.ident = p.sb("s_ident", [128, 128], BF16)
        self.ones_row_f = p.sb("s_ones_row_f", [1, 128], F32)
        self.ones_row_bf = p.sb("s_ones_row_bf", [1, 128], BF16)
        self.ones_col_bf = p.sb("s_ones_col_bf", [128, 1], BF16)
        self.ones_f128 = p.sb("s_ones_f128", [128, 64], F32)
        self.const_loads = []
        for nm, dst in (("ident_bf", self.ident), ("ones_row_f", self.ones_row_f),
                        ("ones_row_bf", self.ones_row_bf), ("ones_col_bf", self.ones_col_bf),
                        ("ones_f128", self.ones_f128)):
            ap = p.din(nm, dst.shape, dst.dtype)
            p.dma("sync", "cst", dst[:], ap)
        self.fc = p.sb("s_fconsts", [128, 8], F32)
        p.dma("sync", "cst", self.fc[:], p.din("fconsts", [128, 8], F32))
        self.stage = p.sb("stage", [128, 1024], F32)
        self.wbuf = p.sb("wbuf", [128, 8192], BF16)
        self.stage_free = None

    def loadc(self, name, shape, dtype):
        p = self.p
        dst = p.region("c_" + name, shape, dtype)
        ap = p.din(name, shape, dtype)
        p.dma("sync", "cst", dst[:], ap)
        return dst

    def wait_consts(self, engs):
        sig = self.p.dma_all("cst")
        for e in engs:
            self.p.wait(e, sig)

    def load_weight(self, wname, hbm, R, O, dst, dst_off=0, war_sig=None,
                    conv="vector"):
        p = self.p
        assert O <= 1024
        RC = R // 128
        cpop = "tensor_copy" if conv == "vector" else "copy"
        if O == 1:
            if war_sig is not None:
                p.wait(conv, war_sig)
            dsig = p.dma("sync", "wstg",
                         self.stage[:, 0:RC],
                         hbm.rearrange("o (pp rc) -> pp (o rc)", pp=128))
            p.wait(conv, dsig)
            p.op(conv, cpop, dst[:, dst_off:dst_off + RC],
                 self.stage[:, 0:RC])
            sig = p.signal(conv)
            p.wait("sync", sig)
            return sig
        hb3 = hbm.rearrange("(rc p) o -> p rc o", p=128)
        if war_sig is not None:
            p.wait(conv, war_sig)
        OB = min(512, O)
        pieces = [(rc, o0) for rc in range(RC) for o0 in range(0, O, OB)]
        sigs = {}
        sig = None
        for i, (rc, o0) in enumerate(pieces):
            slot = i % 2
            if i >= 2:
                p.wait("sync", sigs[i - 2])
            dsig = p.dma("sync", "wstg%d" % slot,
                         self.stage[:, slot * 512: slot * 512 + OB],
                         hb3[:, rc, o0:o0 + OB])
            p.wait(conv, dsig)
            p.op(conv, cpop,
                 dst[:, dst_off + rc * O + o0: dst_off + rc * O + o0 + OB],
                 self.stage[:, slot * 512: slot * 512 + OB])
            sig = sigs[i] = p.signal(conv)
        p.wait("sync", sig)
        return sig

    def _stage_guard(self, sig):
        self.p.wait("sync", sig)

    def emit_ln_stream(self, xT, xn_bf, scratch_a, scratch_b, fp32_sink=None):
        p, T = self.p, self.T
        av = apof(scratch_a).bitcast(F32)
        bv = apof(scratch_b).bitcast(F32)
        xc = av[:, 0:2 * T]
        tmp = av[:, 2 * T:3 * T]
        sq = av[:, 3 * T:4 * T].bitcast(BF16)
        stat = bv[0:1, 0:2 * T]
        rows = bv[0:1, 2 * T:4 * T]
        xt3 = xT.rearrange("(cc p) t -> p cc t", p=128)
        self.wait_consts(["tensor", "scalar", "vector"])
        QW = min(512, T)
        mmsigs = {}
        sqsigs = {}
        cvsigs = {}
        for cc in range(CCN):
            slot = cc % 2
            semn = "lnx%d" % slot
            if cc >= 2:
                p.wait("sync", cvsigs[cc - 2])
                p.wait("sync", sqsigs[cc - 2])
            dsig = p.dma("sync", semn, xc[:, slot * T:(slot + 1) * T], xt3[:, cc])
            p.wait("vector", dsig)
            p.op("vector", "tensor_copy", xn_bf[:, cc * T:(cc + 1) * T],
                 xc[:, slot * T:(slot + 1) * T])
            cvsigs[cc] = p.signal("vector")
            p.wait("scalar", dsig)
            if cc >= 2:
                p.wait("scalar", mmsigs[cc - 2])
            p.op("scalar", "activation", sq[:, slot * T:slot * T + T],
                 xc[:, slot * T:(slot + 1) * T], AF.Square)
            sqsigs[cc] = p.signal("scalar")
            p.wait("tensor", sqsigs[cc])
            p.wait("tensor", cvsigs[cc])
            for q in range(T // QW):
                p.op("tensor", "matmul", p.ps32(q * QW, QW, 0, 1),
                     self.ones_col_bf[:], xn_bf[:, cc * T + q * QW: cc * T + (q + 1) * QW],
                     start=(cc == 0), stop=(cc == CCN - 1))
                p.op("tensor", "matmul", p.ps32(2048 + q * QW, QW, 0, 1),
                     self.ones_col_bf[:], sq[:, slot * T + q * QW: slot * T + (q + 1) * QW],
                     start=(cc == 0), stop=(cc == CCN - 1))
            mmsigs[cc] = p.signal("tensor")
        mmsig = mmsigs[CCN - 1]
        p.wait("scalar", mmsig)
        p.op("scalar", "copy", stat[0:1, 0:T], p.ps32(0, T, 0, 1))
        p.op("scalar", "copy", stat[0:1, T:2 * T], p.ps32(2048, T, 0, 1))
        evsig = p.signal("scalar")
        p.wait("vector", evsig)
        p.op("vector", "tensor_scalar", rows[0:1, 0:T], stat[0:1, 0:T],
             1.0 / C, None, ALU.mult)
        p.drain("vector")
        p.op("vector", "tensor_tensor", rows[0:1, T:2 * T], rows[0:1, 0:T],
             rows[0:1, 0:T], ALU.mult)
        p.drain("vector")
        p.op("vector", "scalar_tensor_tensor", rows[0:1, T:2 * T], stat[0:1, T:2 * T],
             1.0 / C, rows[0:1, T:2 * T], ALU.mult, ALU.subtract)
        vsig = p.signal("vector")
        p.wait("scalar", vsig)
        p.op("scalar", "activation", stat[0:1, 0:T], rows[0:1, T:2 * T], AF.Ln,
             bias=self.fc[0:1, 0:1])
        p.drain("scalar")
        p.op("scalar", "activation", stat[0:1, 0:T], stat[0:1, 0:T], AF.Exp, scale=-0.5)
        rsig = p.signal("scalar")
        p.wait("vector", rsig)
        p.op("vector", "scalar_tensor_tensor", rows[0:1, T:2 * T], rows[0:1, 0:T],
             -1.0, stat[0:1, 0:T], ALU.mult, ALU.mult)
        bsig = p.signal("vector")
        p.wait("tensor", rsig)
        p.wait("tensor", bsig)
        for q in range(T // QW):
            p.op("tensor", "matmul", p.ps32(q * QW, QW), self.ones_row_f[:],
                 stat[0:1, q * QW:(q + 1) * QW], start=True, stop=True)
            p.op("tensor", "matmul", p.ps32(2048 + q * QW, QW), self.ones_row_f[:],
                 rows[0:1, T + q * QW: T + (q + 1) * QW], start=True, stop=True)
        bcsig = p.signal("tensor")
        p.wait("vector", bcsig)
        outsigs = {}
        for s in (sqsigs[CCN - 1], sqsigs[CCN - 2], cvsigs[CCN - 1], cvsigs[CCN - 2]):
            p.wait("sync", s)
        sink_sig = None
        for cc in range(CCN):
            slot = cc % 2
            semn = "lnn%d" % slot
            if cc >= 2:
                p.wait("sync", outsigs[cc - 2])
            dsig = p.dma("sync", semn, xc[:, slot * T:(slot + 1) * T], xt3[:, cc])
            p.wait("vector", dsig)
            if sink_sig is not None:
                p.wait("vector", sink_sig)
            p.drain("vector")
            p.op("vector", "tensor_tensor", tmp, xc[:, slot * T:(slot + 1) * T],
                 p.ps32(0, T), ALU.mult)
            p.drain("vector")
            if fp32_sink is not None:
                p.op("vector", "tensor_tensor", tmp, tmp, p.ps32(2048, T), ALU.add)
                p.drain("vector")
                p.op("vector", "tensor_copy", xn_bf[:, cc * T:(cc + 1) * T], tmp)
                sig = p.signal("vector")
                sink_sig = fp32_sink(cc, tmp, sig)
            else:
                p.op("vector", "tensor_tensor", xn_bf[:, cc * T:(cc + 1) * T],
                     tmp, p.ps32(2048, T), ALU.add)
                sig = p.signal("vector")
            outsigs[cc] = sig
        return outsigs[CCN - 1]

    def emit_proj_fm(self, wbuf, woff, src_bf, dst_bf, OC, bias=None, act=AF.Identity,
                     wait_sigs=(), evac_engine="scalar", evac_cb=None, Odim=None,
                     CC=CCN, pre_mm=None):
        p, T = self.p, self.T
        Odim = Odim or OC * 128
        QW = min(512, T)
        NT = T // QW
        for s in wait_sigs:
            p.wait("tensor", s)
        evsigs = {}
        k = 0
        pair_ok = (NT % 2 == 0) and evac_cb is None
        for oc in range(OC):
            tc = 0
            while tc < NT:
                width = 2 if pair_ok else 1
                bank = k % 4
                for kk in range(k, k + width):
                    if kk - 4 in evsigs:
                        p.wait("tensor", evsigs[kk - 4])
                if pre_mm is not None:
                    for w_ in range(width):
                        pre_mm(oc, tc + w_)
                for w_ in range(width):
                    ps = p.ps32((bank + w_) * 512, QW)
                    for cc in range(CC):
                        p.op("tensor", "matmul", ps,
                             wbuf[:, woff + cc * Odim + oc * 128: woff + cc * Odim + (oc + 1) * 128],
                             src_bf[:, cc * T + (tc + w_) * QW: cc * T + (tc + w_ + 1) * QW],
                             start=(cc == 0), stop=(cc == CC - 1))
                msig = p.signal("tensor")
                p.wait(evac_engine, msig)
                if evac_cb is not None:
                    evac_cb(oc, tc, p.ps32(bank * 512, QW))
                else:
                    kw = {}
                    if bias is not None:
                        kw["bias"] = bias[:, oc:oc + 1]
                    p.op(evac_engine, "activation",
                         dst_bf[:, oc * T + tc * QW: oc * T + (tc + width) * QW],
                         p.ps32(bank * 512, width * QW), act, **kw)
                sig = p.signal(evac_engine)
                for kk in range(k, k + width):
                    evsigs[kk] = sig
                k += width
                tc += width
        return evsigs[k - 1]

    def emit_proj_residual(self, wbuf, src_bf, x_hbm, out_hbm, bias=None,
                           wait_sigs=(), OC=CCN, CC=CCN, Odim=None,
                           xbuf=None, obuf=None):
        p, T = self.p, self.T
        QW = min(512, T)
        xin3 = x_hbm.rearrange("(cc p) t -> p cc t", p=128)
        out3 = out_hbm.rearrange("(cc p) t -> p cc t", p=128)
        if xbuf is None:
            xbuf = p.region("res_x", [128, 2 * QW], F32)
        if obuf is None:
            obuf = p.region("res_o", [128, 2 * QW], F32)
        stsigs = {}
        dsigs = {}
        k = 0
        NT = T // QW

        def pre(oc, tc):
            kk = oc * NT + tc
            slot = kk % 2
            if kk >= 2:
                p.wait("sync", stsigs[kk - 2])
            dsigs[kk] = p.dma("sync", "resx%d" % slot,
                              xbuf[:, slot * QW:(slot + 1) * QW],
                              xin3[:, oc, tc * QW:(tc + 1) * QW])

        def evac(oc, tc, ps):
            kk = oc * NT + tc
            slot = kk % 2
            p.wait("vector", dsigs[kk])
            if kk >= 2:
                p.wait("vector", stsigs[kk - 2])
            if bias is not None:
                p.op("vector", "scalar_tensor_tensor",
                     obuf[:, slot * QW:(slot + 1) * QW], ps, bias[:, oc:oc + 1],
                     xbuf[:, slot * QW:(slot + 1) * QW], ALU.add, ALU.add)
            else:
                p.op("vector", "tensor_tensor",
                     obuf[:, slot * QW:(slot + 1) * QW], ps,
                     xbuf[:, slot * QW:(slot + 1) * QW], ALU.add)
            esig = p.signal("vector")
            p.wait("sync", esig)
            stsigs[kk] = p.dma("sync", "ro%d" % slot, out3[:, oc, tc * QW:(tc + 1) * QW],
                               obuf[:, slot * QW:(slot + 1) * QW])

        return self.emit_proj_fm(wbuf, 0, src_bf, None, OC, wait_sigs=wait_sigs,
                                 evac_engine="vector", evac_cb=evac, pre_mm=pre,
                                 CC=CC, Odim=Odim)





    def emit_v_tm(self, wbuf, src_bf, v_tm, bvr_bf, wait_sigs=(), stride=1040):
        p, T = self.p, self.T
        NT16 = T // 128
        if stride == 1040:
            vall = v_tm[:, 0:NT16 * 1040].rearrange(
                "p (t h e) -> p t h e", h=NH, e=65)
            p.op("vector", "memset", vall[:, :, :, 64:65], 1.0)
            onesig = p.signal("vector")
            p.wait("scalar", onesig)
        for s in wait_sigs:
            p.wait("tensor", s)
        evsigs = {}
        for tt in range(NT16):
            bk2 = (tt % 2) * 2
            if tt >= 2:
                p.wait("tensor", evsigs[tt - 2])
            for half in range(2):
                ps = p.ps32(bk2 * 512 + half * 512, 512)
                for cc in range(CCN):
                    p.op("tensor", "matmul", ps,
                         src_bf[:, cc * T + tt * 128: cc * T + (tt + 1) * 128],
                         wbuf[:, cc * C + half * 512: cc * C + (half + 1) * 512],
                         start=(cc == 0), stop=(bvr_bf is None and cc == CCN - 1))
                if bvr_bf is not None:
                    p.op("tensor", "matmul", ps, self.ones_row_bf[:],
                         bvr_bf[0:1, half * 512:(half + 1) * 512],
                         start=False, stop=True)
            msig = p.signal("tensor")
            p.wait("scalar", msig)
            if stride == 1040:
                v3 = v_tm[:, tt * 1040:(tt + 1) * 1040].rearrange(
                    "p (h e) -> p h e", e=65)[:, :, 0:64]
                src3 = p.ps32(bk2 * 512, 1024).rearrange("p (h d) -> p h d", d=64)
                p.op("scalar", "activation", v3, src3, AF.Identity)
            else:
                p.op("scalar", "activation", v_tm[:, tt * 1024:(tt + 1) * 1024],
                     p.ps32(bk2 * 512, 1024), AF.Identity)
            evsigs[tt] = p.signal("scalar")
        return evsigs[NT16 - 1]

    def emit_proj_tm(self, wbuf, src_bf, dst, Odim, wait_sigs=(), act=AF.Identity,
                     extra_mm=None, woff=0, ret_all=False):
        p, T = self.p, self.T
        NT16 = T // 128
        for s in wait_sigs:
            p.wait("tensor", s)
        evsigs = {}
        pair = (NT16 % 2 == 0) and Odim <= 512
        step = 2 if pair else 1
        for t0 in range(0, NT16, step):
            bank = t0 % 4
            for tt in range(t0, t0 + step):
                if tt - 4 in evsigs:
                    p.wait("tensor", evsigs[tt - 4])
            for w_ in range(step):
                ps = p.ps32((bank + w_) * 512, Odim)
                for cc in range(CCN):
                    p.op("tensor", "matmul", ps,
                         src_bf[:, cc * T + (t0 + w_) * 128: cc * T + (t0 + w_ + 1) * 128],
                         wbuf[:, woff + cc * Odim: woff + (cc + 1) * Odim],
                         start=(cc == 0), stop=(cc == CCN - 1))
                if extra_mm is not None:
                    extra_mm(ps, Odim)
            msig = p.signal("tensor")
            p.wait("scalar", msig)
            if pair:
                pv2 = p.PS.ap()[:, bank * 512: bank * 512 + 1024].rearrange(
                    "p (w o) -> p w o", o=512)[:, :, 0:Odim]
                dv2 = dst[:, t0 * Odim:(t0 + 2) * Odim].rearrange(
                    "p (w o) -> p w o", o=Odim)
                p.op("scalar", "activation", dv2, pv2, act)
            else:
                p.op("scalar", "activation", dst[:, t0 * Odim:(t0 + 1) * Odim],
                     p.ps32(bank * 512, Odim), act)
            sig = p.signal("scalar")
            for tt in range(t0, t0 + step):
                evsigs[tt] = sig
        return evsigs if ret_all else evsigs[NT16 - 1]

    def emit_top2(self, g, member, scrf, iota, iotap, wait_sig=None):
        p = self.p
        E, G = 8, 32
        t16 = scrf[:, 0:128]
        t8 = scrf[:, 128:192]
        m1 = scrf[:, 192:224]
        idx = scrf[:, 224:256]
        eqb = scrf[:, 256:512]
        zb = scrf[:, 512:768]
        g2 = scrf[:, 768:1024]
        ism1 = scrf[:, 1024:1280]
        if wait_sig is not None:
            p.wait("vector", wait_sig)

        def fold(srcap, mdst, op):
            s3 = srcap.rearrange("p (g e) -> p g e", e=E)
            t16_3 = t16.rearrange("p (g e) -> p g e", e=4)
            t8_3 = t8.rearrange("p (g e) -> p g e", e=2)
            p.op("vector", "tensor_tensor", t16_3, s3[:, :, 0:4], s3[:, :, 4:8], op)
            p.drain("vector")
            p.op("vector", "tensor_tensor", t8_3, t16_3[:, :, 0:2], t16_3[:, :, 2:4], op)
            p.drain("vector")
            p.op("vector", "tensor_tensor", mdst, t8_3[:, :, 0:1].squeeze(2),
                 t8_3[:, :, 1:2].squeeze(2), op)
            p.drain("vector")

        def argfirst(srcap, ismdst):
            s3 = srcap.rearrange("p (g e) -> p g e", e=E)
            mexp = m1[:, :, None].broadcast_to([128, G, E])
            eq3 = eqb.rearrange("p (g e) -> p g e", e=E)
            p.op("vector", "tensor_tensor", eq3, s3, mexp, ALU.is_equal)
            p.drain("vector")
            p.op("vector", "scalar_tensor_tensor", zb, eqb, -16384.0, iotap[:],
                 ALU.mult, ALU.add)
            p.drain("vector")
            fold(zb, idx, ALU.min)
            iexp = idx[:, :, None].broadcast_to([128, G, E])
            im3 = ismdst.rearrange("p (g e) -> p g e", e=E)
            p.op("vector", "tensor_tensor", im3,
                 iota[:].rearrange("p (g e) -> p g e", e=E), iexp, ALU.is_equal)
            p.drain("vector")

        fold(g, m1, ALU.max)
        argfirst(g, ism1)
        p.op("vector", "scalar_tensor_tensor", g2, ism1, -1e30, g, ALU.mult, ALU.add)
        p.drain("vector")
        fold(g2, m1, ALU.max)
        argfirst(g2, member)
        p.op("vector", "tensor_tensor", member, ism1, member, ALU.add)
        return p.signal("vector")



def emit_L0a(p, cm, T, xT, o_xpart, o_xn, stop=None):
    NT16 = T // 128

    q_fm = p.region("a_q_fm", [128, CCN * T], BF16)
    k_fm = p.region("a_k_fm", [128, CCN * T], BF16)
    v_tm = p.region("a_v_tm", [128, NT16 * C], BF16)
    xnO = p.region("a_xnO", [128, CCN * T], BF16)
    wbuf = cm.wbuf
    P_all = p.region("a_P_all", [128, 4096], BF16)
    Pn_all = p.region("a_Pn_all", [128, 4096], BF16)
    PT_all = p.region("a_PT_all", [128, 4096], BF16)
    rsb = p.region("a_rsb", [128, 48], F32)
    bq = cm.loadc("l0a_bq", [128, 8], F32)
    bk = cm.loadc("l0a_bk", [128, 8], F32)
    bo = cm.loadc("l0a_bo", [128, 8], F32)
    bvr = cm.loadc("l0a_bv_row", [1, C], F32)
    bvr_bf = p.region("a_bvr_bf", [1, C], BF16)
    pen_g = cm.loadc("pen_g", [128, 256], F32)
    pen_0 = cm.loadc("pen_0", [128, 256], F32)

    lnsig = cm.emit_ln_stream(xT, xnO, q_fm, k_fm)

    if stop == "ln":
        p.finish()
        return
    cm.wait_consts(["vector"])
    p.op("vector", "tensor_copy", bvr_bf[:], bvr[:])
    bvsig = p.signal("vector")

    wq = p.din("l0a_WqT", [C, C], F32)
    wk = p.din("l0a_WkT", [C, C], F32)
    wv = p.din("l0a_WvT", [C, C], F32)
    wo = p.din("l0a_WoT", [C, C], F32)

    wsig = cm.load_weight("wq", wq, C, C, wbuf)
    qsig = cm.emit_proj_fm(wbuf, 0, xnO, q_fm, CCN, bias=bq,
                           wait_sigs=[lnsig, wsig])
    wsig = cm.load_weight("wk", wk, C, C, wbuf, war_sig=qsig)
    ksig = cm.emit_proj_fm(wbuf, 0, xnO, k_fm, CCN, bias=bk, wait_sigs=[wsig])

    wsig = cm.load_weight("wv", wv, C, C, wbuf, war_sig=ksig)
    vsig = cm.emit_v_tm(wbuf, xnO, v_tm, bvr_bf, wait_sigs=[wsig, bvsig], stride=1024)
    if stop == "vtm":
        p.finish()
        return

    if stop == "proj":
        p.finish()
        return
    if stop == "projq":
        cm.load_weight("wk2", wk, C, C, wbuf, war_sig=qsig)
        p.finish()
        return
    p.wait("sync", lnsig)
    xnstore = p.dma("sync", "out", o_xn.rearrange("(cc p) t -> p cc t", p=128),
                    xnO[:].rearrange("p (cc t) -> p cc t", cc=CCN))
    if stop == "xns":
        p.finish()
        return

    O_fm = xnO
    p.wait("vector", vsig)
    p.wait("tensor", vsig)
    p.wait("vector", p.dma_all("cst"))
    prev_qt_done = None
    s_sig = {}
    NQT = NT16 if stop is None or not str(stop).startswith("qt") else int(str(stop)[2:])
    for qt in range(NQT):
        pen = pen_0 if qt == 0 else pen_g
        p.wait("tensor", qsig)
        if stop == "qw":
            p.signal("tensor")
            p.finish()
            return
        NHEM = NH if stop is None or not str(stop).startswith("qh") else int(str(stop)[2:])
        if prev_qt_done is not None:
            p.wait("tensor", prev_qt_done)
        heads_ord = [hh for hh in range(NHEM) if hh % 2 == 0] + \
                    [hh for hh in range(NHEM) if hh % 2 == 1]
        for h in heads_ord:
            base = (h % 2) * 64
            qh = q_fm[base:base + 64,
                      (h // 2) * T + qt * 128: (h // 2) * T + (qt + 1) * 128]
            kh_all = k_fm[base:base + 64, (h // 2) * T:(h // 2 + 1) * T]
            if qt == 0:
                p.op("tensor", "matmul", p.ps32(h * 256 + 64, 128), qh,
                     kh_all[:, 0:128], start=True, stop=True)
            else:
                p.op("tensor", "matmul", p.ps32(h * 256, 192), qh,
                     kh_all[:, qt * 128 - 64: qt * 128 + 128], start=True, stop=True)
                p.op("tensor", "matmul", p.ps32(h * 256 + 192, 1), qh,
                     kh_all[:, 0:1], start=True, stop=True)
        ssig = p.signal("tensor")
        if stop == "qs" or (stop is not None and str(stop).startswith("qh")):
            p.finish(); return
        p.wait("vector", ssig)
        sview = p.PS.ap()[:, 0:4096].rearrange("p (h c) -> p h c", c=256)[:, :, 0:193]
        if qt == 0:
            full = p.PS.ap()[:, 0:4096].rearrange("p (h c) -> p h c", c=256)
            p.op("vector", "memset", full[:, :, 0:64], 0.0)
            p.op("vector", "memset", full[:, :, 192:193], 0.0)
            p.drain("vector")
        penv = pen[:].rearrange("p (g c) -> p g c", g=1)[:, :, 0:193].broadcast_to([128, NH, 193])
        p.op("vector", "tensor_tensor", sview, sview, penv, ALU.add)
        pensig = p.signal("vector")
        p.wait("scalar", pensig)
        pview = P_all[:].rearrange("p (h c) -> p h c", c=256)[:, :, 0:193]
        p.op("scalar", "activation", pview, sview, AF.Exp, bias=cm.fc[:, 1:2], scale=0.125)
        esig = p.signal("scalar")
        s_sig[qt] = esig
        if stop == "qe":
            p.finish(); return
        p.wait("vector", esig)
        p.op("vector", "tensor_reduce", rsb[:, 0:16], pview, AX.X, ALU.add)
        p.drain("vector")
        p.op("vector", "reciprocal", rsb[:, 16:32], rsb[:, 0:16])
        p.drain("vector")
        for h in range(NH):
            p.op("vector", "tensor_scalar", Pn_all[:, h * 256:h * 256 + 193],
                 P_all[:, h * 256:h * 256 + 193], rsb[:, 16 + h:17 + h], None, ALU.mult)
        nsig = p.signal("vector")
        if stop == "qn":
            p.finish(); return
        p.wait("tensor", esig)
        p.wait("tensor", nsig)
        for h in range(NH):
            p.op("tensor", "transpose", p.psbf(h * 128, 128),
                 Pn_all[:, h * 256 + 64: h * 256 + 192], cm.ident[:])
            p.op("tensor", "transpose", p.psbf(2048 + h * 128, 128, 64, 64),
                 Pn_all[:, h * 256 + 0: h * 256 + 64], cm.ident[:])
            p.op("tensor", "transpose", p.psbf(2048 + h * 128, 128, 0, 1),
                 Pn_all[:, h * 256 + 192: h * 256 + 193], cm.ident[:])
        trsig = p.signal("tensor")
        p.wait("vector", trsig)
        ptb = PT_all[:].rearrange("p (h two c) -> p h two c", two=2, c=128)
        psb_b = p.PSB[:, 0:2048].rearrange("p (h c) -> p h c", c=128)
        psb_ac = p.PSB[:, 2048:4096].rearrange("p (h c) -> p h c", c=128)
        p.op("vector", "tensor_copy", ptb[:, :, 0, :], psb_b)
        p.op("vector", "tensor_copy", ptb[64:128, :, 1, :], psb_ac[64:128])
        p.op("vector", "tensor_copy", ptb[0:1, :, 1, :], psb_ac[0:1])
        ptsig = p.signal("vector")
        if stop == "qtr":
            p.finish(); return
        p.wait("tensor", ptsig)
        for h in range(NH):
            out_ap = p.ps32(2048 + h * 128, 128, (h % 2) * 64, 64)
            mmA = (v_tm[64:128, (qt - 1) * C + h * 64: (qt - 1) * C + (h + 1) * 64],
                   PT_all[64:128, h * 256 + 128: h * 256 + 256]) if qt > 0 else None
            mmB = (v_tm[:, qt * C + h * 64: qt * C + (h + 1) * 64],
                   PT_all[:, h * 256: h * 256 + 128])
            mmC = (v_tm[0:1, h * 64:(h + 1) * 64],
                   PT_all[0:1, h * 256 + 128: h * 256 + 256])
            seq = [mmA, mmB, mmC] if h % 2 == 0 else [mmC, mmB, mmA]
            seq = [s for s in seq if s is not None]
            for i, (lh, rh) in enumerate(seq):
                p.op("tensor", "matmul", out_ap, lh, rh,
                     start=(i == 0), stop=(i == len(seq) - 1))
        pvsig = p.signal("tensor")
        p.wait("vector", pvsig)
        if qt == 0:
            p.wait("vector", xnstore)
        ops = p.PS.ap()[:, 2048:4096].rearrange("p (h c) -> p h c", c=128)
        ofm = O_fm[:].rearrange("p (oc t) -> p oc t", oc=CCN)[:, :, qt * 128:(qt + 1) * 128]
        p.op("vector", "tensor_copy", ofm[0:64], ops[0:64, 0::2, :])
        p.op("vector", "tensor_copy", ofm[64:128], ops[64:128, 1::2, :])
        prev_qt_done = p.signal("vector")
    if stop is not None and str(stop).startswith("qt"):
        p.finish()
        return

    if stop == "band":
        p.finish()
        return
    wsig = cm.load_weight("wo", wo, C, C, wbuf, war_sig=vsig)
    p.wait("tensor", prev_qt_done)
    cm.emit_proj_residual(wbuf, O_fm, xT, o_xpart, bias=bo, wait_sigs=[wsig])
    for s in ("out", "ro0", "ro1"):
        p.wait("sync", p.dma_all(s))
    return p



ARENA_F32 = 46750


def build_fused(T):
    p = Prog("FUSED")
    cm = Common(p, T)
    p.set_arena(ARENA_F32)
    xT = p.din("xT", [C, T], F32)
    o_y = p.dout("o_y", [C, T], F32)

    def ih(name, dt):
        return p.nc.dram_tensor(name, [C, T], dt, kind="Internal").ap()

    xpart = ih("h_xpart", F32)
    xn = ih("h_xn", BF16)
    x1 = ih("h_x1", F32)
    xn1 = ih("h_xn1", BF16)
    xn1f = ih("h_xn1f", F32)
    x2p = ih("h_x2p", F32)

    p.begin_phase("L0a")
    emit_L0a(p, cm, T, xT, xpart, xn)
    phase_barrier(p)
    p.begin_phase("L0f")
    emit_L0f(p, cm, T, xn, xpart, x1)
    phase_barrier(p)
    p.begin_phase("L1a")
    emit_L1a(p, cm, T, x1, x2p, xn1, xn1f)
    phase_barrier(p)
    p.begin_phase("L1f")
    emit_L1f(p, cm, T, xn1, xn1f, x2p, o_y)
    p.finish()
    return p


_CACHE = {}


def _prog(T):
    if T not in _CACHE:
        p = build_fused(T)
        nc = p.build()
        _CACHE[T] = (p, nc)
    return _CACHE[T]


def kernel(**inputs):
    x = np.asarray(inputs["L0_attn_hidden_states"])
    B, T, _ = x.shape
    A = prepare_host(inputs, T)
    p, nc = _prog(T)
    in_maps = []
    for b in range(B):
        m = {}
        for nm in p.input_names:
            m[nm] = A["__percore_xT"][b] if nm == "xT" else A[nm]
        in_maps.append(m)
    res = run_bass_kernel_spmd(nc, in_maps, core_ids=list(range(B)))
    out = np.stack([np.ascontiguousarray(res.results[b]["o_y"].T)
                    for b in range(B)], 0)
    return out.astype(np.float32)


def emit_wide_sum(p, eng, slots, terms, out):
    n = len(slots)
    acc = None
    i = 0
    while i < len(terms):
        free = [s for s in range(n) if s != acc]
        take = min(len(free), len(terms) - i)
        free = free[:take]
        for s, (a, b) in zip(free, terms[i:i + take]):
            p.op(eng, "tensor_tensor", slots[s], a, b, ALU.mult)
        p.drain(eng)
        cur = free
        while len(cur) > 1:
            nxt = []
            for j in range(0, len(cur) - 1, 2):
                p.op(eng, "tensor_tensor", slots[cur[j]], slots[cur[j]],
                     slots[cur[j + 1]], ALU.add)
                nxt.append(cur[j])
            if len(cur) % 2 == 1:
                nxt.append(cur[-1])
            p.drain(eng)
            cur = nxt
        if acc is None:
            acc = cur[0]
        else:
            p.op(eng, "tensor_tensor", slots[acc], slots[acc], slots[cur[0]], ALU.add)
            p.drain(eng)
        i += take
    p.op(eng, "tensor_copy", out, slots[acc])
    p.drain(eng)


def emit_L0f(p, cm, T, xnH, xpart, o_x1, stop=None):
    NT16 = T // 128
    QH = min(1024, T)
    NQH = T // QH
    NTH = QH // 128
    E = 8

    xn = p.region("f_xn", [128, CCN * T], BF16)
    qf = p.region("f_qf", [128, CCN * T], BF16)
    kf = p.region("f_kf", [128, CCN * T], BF16)
    gU = p.region("f_gU", [128, NT16 * 256], F32)
    U_fm = apof(gU).bitcast(BF16)
    pvn = max(NT16 * 512, 8 * QH)
    pvR = p.region("f_pvR", [128, pvn], BF16)
    v_tm = p.region("f_v_tm", [128, NT16 * 65 * NH], BF16)
    mix_v = p.region("f_mix_v", [128, NT16 * 128], BF16)
    mask_o = p.region("f_mask_o", [128, NT16 * 128], BF16)
    scr = p.region("f_scr", [128, 3072], F32)
    scrb = apof(scr).bitcast(BF16)
    wbuf = cm.wbuf
    iota = cm.loadc("iota", [128, 256], F32)
    iotap = cm.loadc("iotap", [128, 256], F32)

    pvf = apof(pvR).bitcast(F32)
    def Pslot(i):
        return pvR[:, i * QH:(i + 1) * QH]
    Ou = pvf[0:65, (3 * QH) // 2:(3 * QH) // 2 + QH]
    Rb = pvf[0:64, (3 * QH) // 2 + QH:(3 * QH) // 2 + 2 * QH]
    On = pvR[0:64, 7 * QH:8 * QH]

    xnsig = p.dma("sync", "xn", xn[:].rearrange("p (cc t) -> p cc t", cc=CCN),
                  xnH.rearrange("(cc p) t -> p cc t", p=128))

    wWq = p.din("l0f_WqT", [C, C], F32)
    wWk = p.din("l0f_WkT", [C, C], F32)
    wWsd = p.din("l0f_WsdT", [C, 256], F32)
    wWv = p.din("l0f_WvT", [C, 512], F32)
    wWo = p.din("l0f_WoT", [512, C], F32)

    wsig_v = cm.load_weight("wv", wWv, C, 512, wbuf)
    wsig_sd = cm.load_weight("wsd", wWsd, C, 256, wbuf, dst_off=4096)
    gsigs = cm.emit_proj_tm(wbuf, xn, gU, 256, wait_sigs=[xnsig, wsig_sd],
                            woff=4096, ret_all=True)
    pvsigs = cm.emit_proj_tm(wbuf, xn, pvR, 512,
                             wait_sigs=[wsig_v, gsigs[NT16 - 1]],
                             ret_all=True)

    member = scr[:, 1280:1536]
    cm.wait_consts(["vector", "scalar"])
    sgsig = {}
    for tt in range(NT16):
        p.wait("scalar", gsigs[tt])
        p.op("scalar", "activation", mix_v[:, tt * 128:(tt + 1) * 128],
             gU[:, tt * 256: tt * 256 + 128], AF.Sigmoid)
        sgsig[tt] = p.signal("scalar")
    tksigs = {}
    for tt in range(NT16):
        gt = gU[:, tt * 256:(tt + 1) * 256]
        p.wait("vector", gsigs[tt])
        cm.emit_top2(gt, member, scr, iota, iotap)
        p.wait("vector", sgsig[tt])
        p.drain("vector")
        p.op("vector", "tensor_tensor", mix_v[:, tt * 128:(tt + 1) * 128],
             mix_v[:, tt * 128:(tt + 1) * 128], member[:, 0:128], ALU.mult)
        p.op("vector", "tensor_copy", mask_o[:, tt * 128:(tt + 1) * 128],
             member[:, 128:256])
        tksigs[tt] = p.signal("vector")

    wsig = cm.load_weight("wq", wWq, C, C, wbuf, war_sig=pvsigs[NT16 - 1],
                          conv="scalar")
    qsig = cm.emit_proj_fm(wbuf, 0, xn, qf, CCN, wait_sigs=[wsig])
    wsig = cm.load_weight("wk", wWk, C, C, wbuf, war_sig=qsig, conv="scalar")
    ksig = cm.emit_proj_fm(wbuf, 0, xn, kf, CCN, wait_sigs=[wsig])

    vsig_tile = {}
    for tt in range(NT16):
        eng = "vector" if tt % 2 == 0 else "gpsimd"
        p.wait(eng, pvsigs[tt])
        p.wait(eng, tksigs[tt])
        p.drain(eng)
        v3 = v_tm[:, tt * 1040:(tt + 1) * 1040].rearrange("p (h e) -> p h e", e=65)
        p.op(eng, "memset", v3[:, :, 64:65], 1.0)
        p.drain(eng)
        if eng == "vector":
            slots = [scrb[:, i * 1024:(i + 1) * 1024]
                     .rearrange("p (h d) -> p h d", d=64) for i in range(4)]
        else:
            slots = [scrb[:, 4096 + i * 1024: 4096 + (i + 1) * 1024]
                     .rearrange("p (h d) -> p h d", d=64) for i in range(2)]
        terms = []
        for e in range(E):
            pvap = pvR[:, tt * 512 + e * 64: tt * 512 + (e + 1) * 64]
            a = pvap[:, None, :].broadcast_to([128, NH, 64])
            mx = mix_v[:, tt * 128 + e: tt * 128 + 120 + e + 1: 8]
            b = mx[:, :, None].broadcast_to([128, NH, 64])
            terms.append((a, b))
        emit_wide_sum(p, eng, slots, terms, v3[:, :, 0:64])
        vsig_tile[tt] = p.signal(eng)

    if stop == "front":
        p.finish()
        return
    wosig = cm.load_weight("wo", wWo, 512, C, wbuf, war_sig=ksig)
    if stop == "wo":
        p.finish()
        return

    p.wait("tensor", qsig)
    p.wait("tensor", ksig)
    prev_head_evac = None
    prev_head_exp = None
    usig0 = {}
    for half in range(NQH):
        O_tm = xn[:, half * NTH * 1024:(half + 1) * NTH * 1024]
        for h in range(NH):
            base = (h % 2) * 64
            qh = qf[base:base + 64,
                    (h // 2) * T + half * QH:(h // 2) * T + (half + 1) * QH]
            expsig = {}
            tsig = {}

            SOFF = (0, 1024, 3072)

            def smm(kt):
                kh = kf[base:base + 64,
                        (h // 2) * T + kt * 128:(h // 2) * T + (kt + 1) * 128]
                slot = kt % 3
                for qq in range(QH // 512 or 1):
                    w = min(512, QH)
                    p.op("tensor", "matmul", p.ps32(SOFF[slot] + qq * 512, w),
                         kh, qh[:, qq * 512: qq * 512 + w], start=True, stop=True)

            def pv(kt):
                slot = kt % 3
                p.wait("tensor", vsig_tile[kt])
                for qq in range(QH // 512 or 1):
                    w = min(512, QH)
                    p.op("tensor", "matmul",
                         p.ps32(2048 + qq * 512, w, 0, 65),
                         v_tm[:, kt * 1040 + h * 65: kt * 1040 + (h + 1) * 65],
                         Pslot(slot)[:, qq * 512: qq * 512 + w],
                         start=(kt == 0), stop=(kt == NT16 - 1))

            def expop(kt):
                slot = kt % 3
                p.wait("scalar", tsig[kt])
                p.wait("scalar", vsig_tile[2 * slot])
                p.wait("scalar", vsig_tile[2 * slot + 1])
                if kt == 0 and prev_head_evac is not None:
                    p.wait("scalar", prev_head_evac)
                p.op("scalar", "activation", Pslot(slot), p.ps32(SOFF[slot], QH),
                     AF.Exp, bias=cm.fc[:, 1:2], scale=0.125)
                expsig[kt] = p.signal("scalar")

            if prev_head_exp is not None:
                p.wait("tensor", prev_head_exp[0])
            smm(0)
            tsig[0] = p.signal("tensor")
            expop(0)
            for kt in range(1, NT16):
                if kt >= 3:
                    p.wait("tensor", expsig[kt - 3])
                elif prev_head_exp is not None:
                    p.wait("tensor", prev_head_exp[kt % 3])
                smm(kt)
                p.wait("tensor", expsig[kt - 1])
                if kt == 1 and prev_head_evac is not None:
                    p.wait("tensor", prev_head_evac)
                pv(kt - 1)
                tsig[kt] = p.signal("tensor")
                expop(kt)
            p.wait("tensor", expsig[NT16 - 1])
            pv(NT16 - 1)
            pvdone = p.signal("tensor")
            prev_head_exp = {max(0, NT16 - 3 + kk) % 3: expsig[max(0, NT16 - 3 + kk)]
                             for kk in range(3)}
            p.wait("scalar", pvdone)
            for t2 in (6, 7, 8, 9):
                p.wait("scalar", vsig_tile[t2])
            p.op("scalar", "activation", Ou, p.ps32(2048, QH, 0, 65), AF.Identity)
            p.drain("scalar")
            p.op("scalar", "activation", Ou[64:65, :], Ou[64:65, :], AF.Ln)
            p.drain("scalar")
            p.op("scalar", "activation", Ou[64:65, :], Ou[64:65, :], AF.Exp,
                 scale=-1.0)
            rsig = p.signal("scalar")
            p.wait("tensor", rsig)
            for qq in range(QH // 512 or 1):
                w = min(512, QH)
                p.op("tensor", "matmul", p.ps32(3072 + qq * 512, w, 0, 64),
                     cm.ones_f128[64:65, 0:64], Ou[64:65, qq * 512: qq * 512 + w],
                     start=True, stop=True)
            zbsig = p.signal("tensor")
            p.wait("vector", zbsig)
            p.wait("vector", vsig_tile[14])
            p.wait("vector", vsig_tile[15])
            p.op("vector", "tensor_tensor", On, Ou[0:64, :], p.ps32(3072, QH, 0, 64),
                 ALU.mult)
            onsig = p.signal("vector")
            p.wait("tensor", onsig)
            for j in range(NTH):
                p.op("tensor", "transpose", p.psbf(4096 + j * 64, 64),
                     On[:, j * 128:(j + 1) * 128], cm.ident[0:64, 0:64])
            trsig = p.signal("tensor")
            p.wait("vector", trsig)
            pview = p.PSB[:, 4096:4096 + NTH * 64].rearrange("p (j d) -> p j d", d=64)
            oview = O_tm.rearrange("p (j hd) -> p j hd", hd=1024)[:, :, h * 64:(h + 1) * 64]
            p.op("vector", "tensor_copy", oview, pview)
            prev_head_evac = p.signal("vector")

        if half == 0:
            p.wait("gpsimd", prev_head_evac)
            for j in range(NTH):
                p.wait("gpsimd", tksigs[j])
                p.drain("gpsimd")
                slots = [scrb[:, 4096 + i * 1024: 4096 + (i + 1) * 1024]
                         .rearrange("p (x d) -> p x d", d=64)[:, 0:E]
                         for i in range(2)]
                out_sl = U_fm[:, (j // 2) * T + 1024 + (j % 2) * 512:
                              (j // 2) * T + 1024 + (j % 2) * 512 + 512]
                terms = []
                for h2 in range(NH):
                    oap = O_tm[:, j * 1024 + h2 * 64: j * 1024 + (h2 + 1) * 64]
                    a = oap[:, None, :].broadcast_to([128, E, 64])
                    mk = mask_o[:, j * 128 + h2 * 8: j * 128 + (h2 + 1) * 8]
                    b = mk[:, :, None].broadcast_to([128, E, 64])
                    terms.append((a, b))
                emit_wide_sum(p, "gpsimd", slots, terms,
                              out_sl.rearrange("p (e d) -> p e d", d=64))
                usig0[j] = p.signal("gpsimd")

    if stop == "attn":
        p.finish()
        return
    usig1 = {}
    for j in range(0 if (stop or "").startswith(("tx", "tz")) else NTH):
        tt = NTH + j
        eng = "vector" if j % 2 == 0 else "gpsimd"
        p.wait(eng, prev_head_evac)
        p.wait(eng, tksigs[tt])
        p.drain(eng)
        if eng == "vector":
            slots = [scrb[:, i * 1024:(i + 1) * 1024]
                     .rearrange("p (x d) -> p x d", d=64)[:, 0:E]
                     for i in range(4)]
        else:
            slots = [scrb[:, 4096 + i * 1024: 4096 + (i + 1) * 1024]
                     .rearrange("p (x d) -> p x d", d=64)[:, 0:E]
                     for i in range(2)]
        out_sl = qf[:, j * 512:(j + 1) * 512]
        terms = []
        for h2 in range(NH):
            oap = O_tm[:, j * 1024 + h2 * 64: j * 1024 + (h2 + 1) * 64]
            a = oap[:, None, :].broadcast_to([128, E, 64])
            mk = mask_o[:, tt * 128 + h2 * 8: tt * 128 + (h2 + 1) * 8]
            b = mk[:, :, None].broadcast_to([128, E, 64])
            terms.append((a, b))
        emit_wide_sum(p, eng, slots, terms,
                      out_sl.rearrange("p (e d) -> p e d", d=64))
        usig1[j] = p.signal(eng)

    if stop == "u1":
        p.finish()
        return
    utrsig = {}
    NTTR = NT16
    NOEVAC = False
    SAMESRC = False
    if (stop or "").startswith("tr"):
        NTTR = int(stop[2:])
    if (stop or "").startswith("tn"):
        NTTR = int(stop[2:])
        NOEVAC = True
    if (stop or "").startswith(("ts", "tx", "tz")):
        NTTR = int(stop[2:])
        NOEVAC = True
        SAMESRC = True
    for tt0 in range(NTTR):
        tt = tt0
        ss = 0 if SAMESRC else tt
        j = ss % NTH
        if (stop or "").startswith("tz"):
            src = v_tm[:, 0:512]
        elif ss < NTH:
            src = U_fm[:, (j // 2) * T + 1024 + (j % 2) * 512:
                       (j // 2) * T + 1024 + (j % 2) * 512 + 512]
            p.wait("tensor", usig0[j])
        else:
            src = qf[:, j * 512:(j + 1) * 512]
            p.wait("tensor", usig1[j])
        if tt >= 2:
            p.wait("tensor", utrsig[tt - 2])
        rbase = 0 + (tt % 2) * 1024
        for cchunk in range(4):
            p.op("tensor", "transpose", p.psbf(rbase + cchunk * 128, 128),
                 src[:, cchunk * 128:(cchunk + 1) * 128], cm.ident[:])
        trs = p.signal("tensor")
        p.wait("vector", trs)
        if NOEVAC:
            utrsig[tt] = trs
            continue
        pv4 = p.PSB[:, rbase:rbase + 512].rearrange("p (cchunk d) -> p cchunk d", d=128)
        uview = U_fm[:].rearrange("p (cchunk t) -> p cchunk t", t=T)[
            :, :, tt * 128:(tt + 1) * 128]
        p.op("vector", "tensor_copy", uview, pv4)
        utrsig[tt] = p.signal("vector")

    if stop == "u" or (stop or "").startswith(("tr", "tn", "ts", "tx", "tz")):
        p.finish()
        return
    p.wait("sync", utrsig[NT16 - 1])
    cm.emit_proj_residual(wbuf, U_fm, xpart, o_x1, bias=None,
                          wait_sigs=[wosig, utrsig[NT16 - 1]], OC=CCN, CC=4,
                          Odim=C, xbuf=scr[:, 0:1024], obuf=scr[:, 1024:2048])
    for s in ("ro0", "ro1"):
        p.wait("sync", p.dma_all(s))


def build_L0f(T, stop=None):
    p = Prog("L0f")
    cm = Common(p, T)
    xnH = p.din("i_xn", [C, T], BF16)
    xpart = p.din("i_xpart", [C, T], F32)
    o_x1 = p.dout("o_x1", [C, T], F32)
    emit_L0f(p, cm, T, xnH, xpart, o_x1, stop=stop)
    return p


def emit_L1a(p, cm, T, x1, o_x2p, o_xn1, o_xn1f):
    NT16 = T // 128
    QH = min(1024, T)
    NQH = T // QH

    HW = min(1024, T)
    QW = min(512, QH)
    xn_need = max(CCN * T, 2 * (3 * HW + 4 * QW) + 2 * QH)
    xn = p.region("b_xn", [128, xn_need], BF16)
    q_fm = p.region("b_q_fm", [128, CCN * T], BF16)
    k_fm = p.region("b_k_fm", [128, CCN * T], BF16)
    v_tm = p.region("b_v_tm", [128, NT16 * 65 * NH], BF16)
    wbuf = cm.wbuf
    ropeR = p.region("b_ropeR", [128, max(4 * T, (CCN * QH + 3 * QH) // 2 + 2 * QH)], F32)
    rv = apof(ropeR)
    rbf = rv.bitcast(BF16)
    cosE = rv[:, 0:T] if T <= 2048 else None
    sinE = rv[:, T:2 * T]
    cosO = rv[:, 2 * T:3 * T]
    sinO = rv[:, 3 * T:4 * T]
    for nm, dstv in (("rope_cosE", cosE), ("rope_sinE", sinE),
                     ("rope_cosO", cosO), ("rope_sinO", sinO)):
        p.dma("sync", "cst", dstv, p.din(nm, [128, T], F32))
    O_fm = rbf[:, 0:CCN * QH]
    PP = rbf[:, CCN * QH: CCN * QH + 3 * QH]
    Ou = rv[0:65, (CCN * QH + 3 * QH) // 2: (CCN * QH + 3 * QH) // 2 + QH]
    Rb = rv[0:64, (CCN * QH + 3 * QH) // 2 + QH: (CCN * QH + 3 * QH) // 2 + 2 * QH]
    rtR = p.region("b_rt", [128, 3 * HW], F32)
    rt = apof(rtR)
    xnv = apof(xn).bitcast(F32)
    xbuf = xnv[:, 3 * HW:3 * HW + 2 * QW]
    obuf = xnv[:, 3 * HW + 2 * QW:3 * HW + 4 * QW]
    OnB = apof(xn)[0:64, 2 * (3 * HW + 4 * QW):2 * (3 * HW + 4 * QW) + QH]
    bq = cm.loadc("l1a_bq", [128, 8], F32)
    bk = cm.loadc("l1a_bk", [128, 8], F32)
    bo = cm.loadc("l1a_bo", [128, 8], F32)
    bvr = cm.loadc("l1a_bv_row", [1, C], F32)
    bvr_bf = p.region("b_bvr_bf", [1, C], BF16)
    ptri = cm.loadc("pen_tri", [128, 128], F32)
    qscr = p.nc.dram_tensor("qscr", [C, T], BF16, kind="Internal").ap()
    kscr = p.nc.dram_tensor("kscr", [C, T], BF16, kind="Internal").ap()

    xn1f3 = o_xn1f.rearrange("(cc p) t -> p cc t", p=128)
    xn1f3 = o_xn1f.rearrange("(cc p) t -> p cc t", p=128)

    def sink(cc, tmp, sig):
        p.wait("sync", sig)
        return p.dma("sync", "spill", xn1f3[:, cc], tmp[:])

    lnsig = cm.emit_ln_stream(x1, xn, q_fm, k_fm, fp32_sink=sink)
    p.wait("sync", lnsig)
    p.dma("sync", "out", o_xn1.rearrange("(cc p) t -> p cc t", p=128),
          apof(xn)[:, 0:CCN * T].rearrange("p (cc t) -> p cc t", cc=CCN))

    cm.wait_consts(["vector"])
    p.op("vector", "tensor_copy", bvr_bf[:], bvr[:])
    bvsig = p.signal("vector")

    wq = p.din("l1a_WqT", [C, C], F32)
    wk = p.din("l1a_WkT", [C, C], F32)
    wv = p.din("l1a_WvT", [C, C], F32)
    wo = p.din("l1a_WoT", [C, C], F32)

    wsig = cm.load_weight("wq", wq, C, C, wbuf)
    qsig = cm.emit_proj_fm(wbuf, 0, xn, q_fm, CCN, bias=bq,
                           wait_sigs=[lnsig, wsig, p.dma_all("spill"),
                                      p.dma_all("out")])
    wsig = cm.load_weight("wk", wk, C, C, wbuf, war_sig=qsig)
    ksig = cm.emit_proj_fm(wbuf, 0, xn, k_fm, CCN, bias=bk, wait_sigs=[wsig])
    wsig = cm.load_weight("wv", wv, C, C, wbuf, war_sig=ksig)
    vsig = cm.emit_v_tm(wbuf, xn, v_tm, bvr_bf, wait_sigs=[wsig, bvsig])

    p.wait("vector", p.dma_all("cst"))
    ropesigs = []
    for buf, psig in ((q_fm, qsig), (k_fm, ksig)):
        p.wait("vector", psig)
        for j in range(4):
            for hh in range(T // HW):
                lo = hh * HW
                cE = cosE[:, lo:lo + HW]
                sE = sinE[:, lo:lo + HW]
                cO = cosO[:, lo:lo + HW]
                sO = sinO[:, lo:lo + HW]
                qj = buf[:, j * T + lo: j * T + lo + HW]
                qj4 = buf[:, (j + 4) * T + lo: (j + 4) * T + lo + HW]
                t1, t4, t2 = (rt[:, 0:HW], rt[:, HW:2 * HW], rt[:, 2 * HW:3 * HW])
                p.op("vector", "tensor_tensor", t1, qj, cE, ALU.mult)
                p.op("vector", "tensor_tensor", t4, qj, sO, ALU.mult)
                p.op("vector", "tensor_tensor", t2, qj4, sE, ALU.mult)
                p.drain("vector")
                p.op("vector", "tensor_tensor", qj, t1, t2, ALU.subtract)
                p.drain("vector")
                p.op("vector", "tensor_tensor", t2, qj4, cO, ALU.mult)
                p.drain("vector")
                p.op("vector", "tensor_tensor", qj4, t2, t4, ALU.add)
                p.drain("vector")
        ropesigs.append(p.signal("vector"))
    ropesig_q, ropesig_k = ropesigs
    ropesig = ropesig_k


    def repack_store(scr, srcbuf, sem):
        for cc in range(4):
            for a in range(2):
                for b in range(2):
                    g = 256 * cc + 128 * a + 64 * b
                    p.dma("sync", sem, scr[g:g + 32, :],
                          srcbuf[64 * a + 32 * b:64 * a + 32 * b + 32,
                                 cc * T:(cc + 1) * T])
                    p.dma("sync", sem, scr[g + 32:g + 64, :],
                          srcbuf[64 * a + 32 * b:64 * a + 32 * b + 32,
                                 (4 + cc) * T:(5 + cc) * T])

    p.wait("sync", ropesig_q)
    repack_store(qscr, q_fm, "rpq")
    p.wait("sync", p.dma_all("rpq"))
    lq = p.dma("sync", "rpq2", q_fm[:].rearrange("p (cc t) -> p cc t", cc=CCN),
               qscr.rearrange("(cc p) t -> p cc t", p=128))
    p.wait("sync", ropesig_k)
    repack_store(kscr, k_fm, "rpk")
    p.wait("sync", p.dma_all("rpk"))
    lk = p.dma("sync", "rpk2", k_fm[:].rearrange("p (cc t) -> p cc t", cc=CCN),
               kscr.rearrange("(cc p) t -> p cc t", p=128))
    q2 = q_fm
    k2 = k_fm
    rpsig = lk

    for e2 in ("tensor", "scalar"):
        p.wait(e2, ropesig)
        p.wait(e2, vsig)
        p.wait(e2, lq)
        p.wait(e2, rpsig)
    shiftsigs = []
    prev_head_evac = None
    prev_head_exp = None
    for half in range(NQH):
        qlo = half * QH
        kts = [kt for kt in range(NT16) if kt * 128 < qlo + QH]
        for h in range(NH):
            base = (h % 2) * 64
            qh = q2[base:base + 64, (h // 2) * T + qlo:(h // 2) * T + qlo + QH]
            expsig = {}
            tsig = {}
            nk = len(kts)

            SOFF = (0, 1024, 3072)

            def smm(ki):
                kt = kts[ki]
                delta = kt * 128 - qlo
                ql = max(0, delta)
                slot = ki % 3
                kh = k2[base:base + 64,
                        (h // 2) * T + kt * 128:(h // 2) * T + (kt + 1) * 128]
                qq = ql
                while qq < QH:
                    w = min(512 - (qq % 512), QH - qq)
                    p.op("tensor", "matmul", p.ps32(SOFF[slot] + qq, w),
                         kh, qh[:, qq:qq + w], start=True, stop=True)
                    qq += w

            def pv(ki):
                kt = kts[ki]
                delta = kt * 128 - qlo
                ql = max(0, delta)
                slot = ki % 3
                qq = ql
                while qq < QH:
                    w = min(512 - (qq % 512), QH - qq)
                    p.op("tensor", "matmul",
                         p.ps32(2048 + qq, w, 0, 65),
                         v_tm[:, kt * 1040 + h * 65: kt * 1040 + (h + 1) * 65],
                         PP[:, slot * QH + qq: slot * QH + qq + w],
                         start=(ki == 0), stop=(ki == nk - 1))
                    qq += w

            def expop(ki):
                kt = kts[ki]
                delta = kt * 128 - qlo
                ql = max(0, delta)
                slot = ki % 3
                if 0 <= delta:
                    p.wait("vector", tsig[ki])
                    sap = p.ps32(SOFF[slot] + delta, 128)
                    p.op("vector", "tensor_tensor", sap, sap, ptri[:], ALU.add)
                    pensig = p.signal("vector")
                    p.wait("scalar", pensig)
                p.wait("scalar", tsig[ki])
                if ki == 0 and prev_head_evac is not None:
                    p.wait("scalar", prev_head_evac)
                p.op("scalar", "activation",
                     PP[:, slot * QH + ql: (slot + 1) * QH],
                     p.ps32(SOFF[slot] + ql, QH - ql),
                     AF.Exp, bias=cm.fc[:, 1:2], scale=0.125)
                expsig[ki] = p.signal("scalar")

            if prev_head_exp is not None:
                p.wait("tensor", prev_head_exp[0])
            smm(0)
            tsig[0] = p.signal("tensor")
            expop(0)
            for ki in range(1, nk):
                if ki >= 3:
                    p.wait("tensor", expsig[ki - 3])
                elif prev_head_exp is not None:
                    p.wait("tensor", prev_head_exp.get(ki % 3, prev_head_exp[0]))
                smm(ki)
                p.wait("tensor", expsig[ki - 1])
                if ki == 1 and prev_head_evac is not None:
                    p.wait("tensor", prev_head_evac)
                pv(ki - 1)
                tsig[ki] = p.signal("tensor")
                expop(ki)
            p.wait("tensor", expsig[nk - 1])
            pv(nk - 1)
            pvdone = p.signal("tensor")
            prev_head_exp = {max(0, nk - 3 + kk) % 3: expsig[max(0, nk - 3 + kk)]
                             for kk in range(3)}
            p.wait("scalar", pvdone)
            p.op("scalar", "activation", Ou, p.ps32(2048, QH, 0, 65),
                 AF.Identity)
            p.drain("scalar")
            p.op("scalar", "activation", Ou[64:65, :], Ou[64:65, :], AF.Ln)
            p.drain("scalar")
            p.op("scalar", "activation", Ou[64:65, :], Ou[64:65, :], AF.Exp,
                 scale=-1.0)
            rsig = p.signal("scalar")
            p.wait("tensor", rsig)
            for qq in range(QH // 512 or 1):
                w = min(512, QH)
                p.op("tensor", "matmul", p.ps32(3072 + qq * 512, w, 0, 64),
                     cm.ones_f128[64:65, 0:64], Ou[64:65, qq * 512: qq * 512 + w],
                     start=True, stop=True)
            zbsig = p.signal("tensor")
            p.wait("vector", zbsig)
            dst = O_fm[0:64, (h // 2) * QH:(h // 2 + 1) * QH] if h % 2 == 0 \
                else OnB
            p.op("vector", "tensor_tensor", dst, Ou[0:64, :],
                 p.ps32(3072, QH, 0, 64), ALU.mult)
            onsig = p.signal("vector")
            if h % 2 == 1:
                p.wait("sync", onsig)
                ssig = p.dma("sync", "shift%d" % (h % 4 // 2),
                             O_fm[64:128, (h // 2) * QH:(h // 2 + 1) * QH],
                             OnB)
                shiftsigs.append(ssig)
                p.wait("vector", ssig)
            prev_head_evac = onsig

        wosig = cm.load_weight("wo", wo, C, C, wbuf, war_sig=vsig) if half == 0 else wosig
        for s in shiftsigs:
            p.wait("tensor", s)
        shiftsigs = []
        p.wait("tensor", prev_head_evac)
        x1h = x1.rearrange("(cc p) t -> p cc t", p=128)
        o2h = o_x2p.rearrange("(cc p) t -> p cc t", p=128)
        QW = min(512, QH)
        NTW = QH // QW
        dsigs = {}
        stsigs = {}

        def pre(oc, tc):
            kk = oc * NTW + tc
            slot = kk % 2
            if kk >= 2:
                p.wait("sync", stsigs[kk - 2])
            dsigs[kk] = p.dma("sync", "resx%d" % slot,
                              xbuf[:, slot * QW:(slot + 1) * QW],
                              x1h[:, oc, qlo + tc * QW: qlo + (tc + 1) * QW])

        def evac(oc, tc, ps):
            kk = oc * NTW + tc
            slot = kk % 2
            p.wait("vector", dsigs[kk])
            if kk >= 2:
                p.wait("vector", stsigs[kk - 2])
            p.op("vector", "scalar_tensor_tensor",
                 obuf[:, slot * QW:(slot + 1) * QW], ps, bo[:, oc:oc + 1],
                 xbuf[:, slot * QW:(slot + 1) * QW], ALU.add, ALU.add)
            esig = p.signal("vector")
            p.wait("sync", esig)
            stsigs[kk] = p.dma("sync", "ro%d" % slot,
                               o2h[:, oc, qlo + tc * QW: qlo + (tc + 1) * QW],
                               obuf[:, slot * QW:(slot + 1) * QW])

        class _FakeT:
            pass
        cm2T = cm.T
        cm.T = QH
        wores = cm.emit_proj_fm(wbuf, 0, O_fm, None, CCN, wait_sigs=[wosig],
                                evac_engine="vector", evac_cb=evac, pre_mm=pre)
        cm.T = cm2T
        p.wait("tensor", wores)
    for s in ("out", "spill", "ro0", "ro1"):
        p.wait("sync", p.dma_all(s))


def build_L1a(T):
    p = Prog("L1a")
    cm = Common(p, T)
    x1 = p.din("i_x1", [C, T], F32)
    o_x2p = p.dout("o_x2p", [C, T], F32)
    o_xn1 = p.dout("o_xn1", [C, T], BF16)
    o_xn1f = p.dout("o_xn1f", [C, T], F32)
    emit_L1a(p, cm, T, x1, o_x2p, o_xn1, o_xn1f)
    return p


def emit_L1f(p, cm, T, xn1b, xn1f, x2p, o_y):
    QH = min(1024, T)
    NQH = T // QH
    QW = min(512, QH)

    xnb = p.region("g_xnb", [128, max(CCN * T, 2 * CCN * QH)], BF16)
    OUT = apof(xnb).bitcast(F32)[:, 0:CCN * QH]
    out2r = p.region("g_out2", [128, 2 * CCN * QH], BF16)
    OUT2 = apof(out2r).bitcast(F32)[:, 0:CCN * QH]
    tw2 = p.region("g_tw2", [128, 2 * QH], F32)
    cp_sb = p.region("g_cp_sb", [128, QH], F32)
    outbf = p.region("g_outbf", [128, CCN * QH], BF16)
    h1b = p.region("g_h1b", [128, CCN * QH], BF16)
    h2b = p.region("g_h2b", [128, 4 * QH], BF16)
    h3b = p.region("g_h3b", [128, 2 * QH], BF16)
    tw = p.region("g_tw", [128, 2 * QH], F32)
    rows = p.region("g_rows", [1, 2 * QH], F32)
    gsml = p.region("g_gsml", [128, 32], F32)
    gsmb = p.region("g_gsmb", [128, 32], BF16)
    gf_sb = p.region("g_gf_sb", [1, 8], F32)
    w1 = p.region("g_w1", [128, CCN * 1024], BF16)
    w2 = p.region("g_w2", [128, CCN * 512], BF16)
    w3 = p.region("g_w3", [128, 4 * 256], BF16)
    w4 = p.region("g_w4", [128, 2 * 1], BF16)
    gw1 = p.region("g_gw1", [128, CCN * 512], BF16)
    gw2 = p.region("g_gw2", [128, 4 * 256], BF16)
    gw3 = p.region("g_gw3", [128, 2 * 1], BF16)
    b1 = cm.loadc("l1f_cn_b1", [128, 8], F32)
    b2 = cm.loadc("l1f_cn_b2", [128, 4], F32)
    b3 = cm.loadc("l1f_cn_b3", [128, 2], F32)
    b4 = cm.loadc("l1f_cn_b4", [1, 1], F32)
    gb1 = cm.loadc("l1f_gc_b1", [128, 4], F32)
    gb2 = cm.loadc("l1f_gc_b2", [128, 2], F32)
    gb3 = cm.loadc("l1f_gc_b3", [1, 1], F32)
    arc = cm.loadc("l1f_ar", [1, 1], F32)

    xnsig = p.dma("sync", "xn2",
                  apof(xnb)[:, 0:CCN * T].rearrange("p (cc t) -> p cc t", cc=CCN),
                  xn1b.rearrange("(cc p) t -> p cc t", p=128))

    wsigs = []
    for nm, hbm_shape, dst, R, O in (
            ("l1f_cn_w1", [C, C], w1, C, 1024),
            ("l1f_cn_w2", [C, 512], w2, C, 512),
            ("l1f_cn_w3", [512, 256], w3, 512, 256),
            ("l1f_cn_w4", [1, 256], w4, 256, 1),
            ("l1f_gc_w1", [C, 512], gw1, C, 512),
            ("l1f_gc_w2", [512, 256], gw2, 512, 256),
            ("l1f_gc_w3", [1, 256], gw3, 256, 1)):
        hbm = p.din(nm, hbm_shape, F32)
        wsigs.append(cm.load_weight(nm, hbm, R, O, dst))
    wsig = wsigs[-1]

    cm.wait_consts(["vector", "scalar", "tensor"])
    p.wait("vector", xnsig)
    for cc in range(CCN):
        p.op("vector", "tensor_reduce", gsml[:, cc:cc + 1],
             xnb[:, cc * T:(cc + 1) * T], AX.X, ALU.add)
    p.drain("vector")
    p.op("vector", "tensor_scalar", gsml[:, 0:8], gsml[:, 0:8], 1.0 / T, None,
         ALU.mult)
    p.drain("vector")
    p.op("vector", "tensor_copy", gsmb[:, 0:8], gsml[:, 0:8])
    gcsig = p.signal("vector")
    for s in wsigs:
        p.wait("tensor", s)
    p.wait("tensor", gcsig)
    for oc in range(4):
        ps = p.ps32(3584, 1)
        for cc in range(CCN):
            p.op("tensor", "matmul", ps, gw1[:, cc * 512 + oc * 128:
                                             cc * 512 + (oc + 1) * 128],
                 gsmb[:, cc:cc + 1], start=(cc == 0), stop=(cc == CCN - 1))
        msig = p.signal("tensor")
        p.wait("scalar", msig)
        p.op("scalar", "activation", gsml[:, 8 + oc:9 + oc], ps, AF.Gelu,
             bias=gb1[:, oc:oc + 1])
        p.drain("scalar")
        p.op("scalar", "copy", gsmb[:, 8 + oc:9 + oc], gsml[:, 8 + oc:9 + oc])
        esig = p.signal("scalar")
        p.wait("tensor", esig)
    for oc in range(2):
        ps = p.ps32(3584, 1)
        for cc in range(4):
            p.op("tensor", "matmul", ps, gw2[:, cc * 256 + oc * 128:
                                             cc * 256 + (oc + 1) * 128],
                 gsmb[:, 8 + cc:9 + cc], start=(cc == 0), stop=(cc == 3))
        msig = p.signal("tensor")
        p.wait("scalar", msig)
        p.op("scalar", "activation", gsml[:, 12 + oc:13 + oc], ps, AF.Gelu,
             bias=gb2[:, oc:oc + 1])
        p.drain("scalar")
        p.op("scalar", "copy", gsmb[:, 12 + oc:13 + oc], gsml[:, 12 + oc:13 + oc])
        esig = p.signal("scalar")
        p.wait("tensor", esig)
    ps = p.ps32(3584, 1, 0, 1)
    for cc in range(2):
        p.op("tensor", "matmul", ps, gw3[:, cc:cc + 1], gsmb[:, 12 + cc:13 + cc],
             start=(cc == 0), stop=(cc == 1))
    msig = p.signal("tensor")
    p.wait("scalar", msig)
    p.op("scalar", "activation", gf_sb[0:1, 0:1], ps, AF.Sigmoid,
         bias=gb3[0:1, 0:1])
    gfsig = p.signal("scalar")
    p.wait("vector", gfsig)
    p.op("vector", "tensor_scalar", gf_sb[0:1, 1:2], gf_sb[0:1, 0:1], 0.7, None,
         ALU.mult)
    p.op("vector", "tensor_scalar", gf_sb[0:1, 2:3], arc[0:1, 0:1], 2.0, None,
         ALU.mult)
    gf7sig = p.signal("vector")

    xn1f3 = xn1f.rearrange("(cc p) t -> p cc t", p=128)
    x2p3 = x2p.rearrange("(cc p) t -> p cc t", p=128)
    oy3 = o_y.rearrange("(cc p) t -> p cc t", p=128)
    xbuf = p.region("g_res_x", [128, 2 * QW], F32)
    obuf = p.region("g_res_o", [128, 2 * QW], F32)
    tf_row = rows[0:1, 0:QH]
    cp_row = rows[0:1, QH:2 * QH]
    cm2T = cm.T
    cm.T = QH
    OUTS = [OUT, OUT2]
    fsts = {}
    fdss = {}
    osigs = []
    for half in range(NQH):
        if half == 0:
            p.wait("sync", gcsig)
        osigs.append(p.dma("sync", "o%d" % (half % 2),
                           OUTS[half][:].rearrange("p (cc t) -> p cc t", cc=CCN),
                           xn1f3[:, :, half * QH:(half + 1) * QH]))
    twg = None
    for half in range(NQH):
        qlo = half * QH
        OUTh = OUTS[half]
        osig = osigs[half]
        cvsig = None
        for cyc in range(3):
            p.wait("vector", osig)
            if twg is not None:
                p.wait("vector", twg)
            for cc in range(CCN):
                p.op("vector", "tensor_copy", outbf[:, cc * QH:(cc + 1) * QH],
                     OUTh[:, cc * QH:(cc + 1) * QH])
            cvsig = p.signal("vector")
            h1sig = cm.emit_proj_fm(apof(w1), 0, apof(outbf), apof(h1b), CCN,
                                    bias=b1, act=AF.Gelu, wait_sigs=[cvsig])
            h2sig = cm.emit_proj_fm(apof(w2), 0, apof(h1b), apof(h2b), 4, bias=b2,
                                    act=AF.Gelu, wait_sigs=[h1sig], Odim=512)
            h3sig = cm.emit_proj_fm(apof(w3), 0, apof(h2b), apof(h3b), 2, bias=b3,
                                    act=AF.Gelu, wait_sigs=[h2sig], CC=4, Odim=256)
            p.wait("tensor", h3sig)
            for qq in range(QH // QW):
                ps2 = p.ps32(2048 + qq * QW, QW, 0, 1)
                for cc in range(2):
                    p.op("tensor", "matmul", ps2, w4[:, cc:cc + 1],
                         h3b[:, cc * QH + qq * QW: cc * QH + (qq + 1) * QW],
                         start=(cc == 0), stop=(cc == 1))
            tfmsig = p.signal("tensor")
            p.wait("scalar", tfmsig)
            p.op("scalar", "activation", tf_row, p.ps32(2048, QH, 0, 1),
                 AF.Sigmoid, bias=b4[0:1, 0:1])
            tfsig = p.signal("scalar")
            p.wait("vector", tfsig)
            p.wait("vector", gf7sig)
            p.op("vector", "tensor_scalar", cp_row, tf_row, 0.3,
                 gf_sb[0:1, 1:2], ALU.mult, ALU.add)
            p.drain("vector")
            p.op("vector", "tensor_scalar", cp_row, cp_row, 0.5,
                 gf_sb[0:1, 2:3], ALU.subtract, ALU.mult)
            p.drain("vector")
            p.op("vector", "tensor_scalar", cp_row, cp_row, 0.1, None, ALU.add)
            cpsig = p.signal("vector")
            p.wait("tensor", cpsig)
            for qq in range(QH // QW):
                p.op("tensor", "matmul", p.ps32(3072 + qq * QW, QW),
                     cm.ones_row_f[:], cp_row[0:1, qq * QW:(qq + 1) * QW],
                     start=True, stop=True)
            cpbsig = p.signal("tensor")
            p.wait("scalar", cpbsig)
            p.op("scalar", "activation", cp_sb[:], p.ps32(3072, QH), AF.Identity)
            cpssig = p.signal("scalar")
            tws = {}
            for eng, pairs, twr in (
                    ("vector", (((0, 4), 1.0, -1.0), ((1, 5), 1.0, -1.0)), tw),
                    ("gpsimd", (((2, 6), -1.0, 1.0), ((3, 7), -1.0, 1.0)), tw2)):
                p.wait(eng, cpssig)
                if eng == "gpsimd":
                    p.wait(eng, cvsig)
                p.drain(eng)
                for pr, sa, sb_ in pairs:
                    cA, cB = pr
                    a_ap = OUTh[:, cA * QH:(cA + 1) * QH]
                    b_ap = OUTh[:, cB * QH:(cB + 1) * QH]
                    p.op(eng, "tensor_tensor", twr[:, 0:QH], cp_sb[:], b_ap,
                         ALU.mult)
                    p.op(eng, "tensor_tensor", twr[:, QH:2 * QH], cp_sb[:], a_ap,
                         ALU.mult)
                    p.drain(eng)
                    if sa > 0:
                        p.op(eng, "tensor_tensor", a_ap, twr[:, 0:QH], a_ap,
                             ALU.add)
                    else:
                        p.op(eng, "tensor_tensor", a_ap, a_ap, twr[:, 0:QH],
                             ALU.subtract)
                    if sb_ > 0:
                        p.op(eng, "tensor_tensor", b_ap, twr[:, QH:2 * QH], b_ap,
                             ALU.add)
                    else:
                        p.op(eng, "tensor_tensor", b_ap, b_ap, twr[:, QH:2 * QH],
                             ALU.subtract)
                    p.drain(eng)
                tws[eng] = p.signal(eng)
            twv, twg = tws["vector"], tws["gpsimd"]
        p.wait("gpsimd", twv)
        for k in range(CCN * (QH // QW)):
            oc, tc = divmod(k, QH // QW)
            kk = half * CCN * (QH // QW) + k
            slot = kk % 2
            if kk >= 2:
                p.wait("sync", fsts[kk - 2])
            fdss[kk] = p.dma("sync", "resx%d" % slot,
                             xbuf[:, slot * QW:(slot + 1) * QW],
                             x2p3[:, oc, qlo + tc * QW: qlo + (tc + 1) * QW])
            p.wait("gpsimd", fdss[kk])
            if kk >= 2:
                p.wait("gpsimd", fsts[kk - 2])
            p.drain("gpsimd")
            p.op("gpsimd", "tensor_tensor", obuf[:, slot * QW:(slot + 1) * QW],
                 OUTh[:, oc * QH + tc * QW: oc * QH + (tc + 1) * QW],
                 xbuf[:, slot * QW:(slot + 1) * QW], ALU.add)
            esig = p.signal("gpsimd")
            p.wait("sync", esig)
            fsts[kk] = p.dma("sync", "ro%d" % slot,
                             oy3[:, oc, qlo + tc * QW: qlo + (tc + 1) * QW],
                             obuf[:, slot * QW:(slot + 1) * QW])
    cm.T = cm2T


def build_L1f(T):
    p = Prog("L1f")
    cm = Common(p, T)
    xn1b = p.din("i_xn1", [C, T], BF16)
    xn1f = p.din("i_xn1f", [C, T], F32)
    x2p = p.din("i_x2p", [C, T], F32)
    o_y = p.dout("o_y", [C, T], F32)
    emit_L1f(p, cm, T, xn1b, xn1f, x2p, o_y)
    return p


# version 18
# speedup vs baseline: 1.2075x; 1.0017x over previous
import numpy as np
import ml_dtypes
from contextlib import ExitStack

import concourse.bass as bass
import concourse.mybir as mybir
from concourse.bass_utils import run_bass_kernel_spmd

F32 = mybir.dt.float32
BF16 = mybir.dt.bfloat16
AF = mybir.ActivationFunctionType
ALU = mybir.AluOpType
AX = mybir.AxisListType

C = 1024
CCN = 8
NH = 16
HD = 64
M0 = 12.0
NEG = -1e9
EPS = 1e-5


ENGS = ("sync", "tensor", "scalar", "vector", "gpsimd")


def apof(x):
    return x.ap() if callable(getattr(x, "ap", None)) else x


def phase_barrier(p):
    for name in list(p.counts):
        if not name.startswith("c_"):
            p.wait("sync", (name, p.counts[name]))
    sigs = {e: p.signal(e) for e in ENGS}
    for e in ENGS:
        for e2, s in sigs.items():
            if e2 != e:
                p.wait(e, s)


class Prog:
    def __init__(self, name):
        self.name = name
        self.nc = bass.Bass("TRN2", target_bir_lowering=False)
        self.es = ExitStack()
        self.queues = {e: [] for e in ENGS}
        self.semh = {}
        self.counts = {}
        self.watermark = {}
        self.input_names = []
        self.output_names = []
        self._sb = {}
        self.sim_trace = {}

    def din(self, name, shape, dtype):
        self.input_names.append(name)
        return self.nc.dram_tensor(name, list(shape), dtype, kind="ExternalInput").ap()

    def dout(self, name, shape, dtype):
        self.output_names.append(name)
        return self.nc.dram_tensor(name, list(shape), dtype, kind="ExternalOutput").ap()

    def sb(self, name, shape, dtype):
        h = self.es.enter_context(self.nc.sbuf_tensor(name, list(shape), dtype))
        self._sb[name] = h
        return h

    def set_arena(self, f32_cols):
        self.arena = self.sb("ARENA", [128, f32_cols], F32)
        self._arena_off = 0
        self._arena_peak = 0

    def begin_phase(self, name):
        self._phase = name
        self._arena_off = 0

    def region(self, name, shape, dtype):
        if getattr(self, "arena", None) is None:
            return self.sb(name, shape, dtype)
        esz = 4 if dtype == F32 else 2
        off = (self._arena_off + 7) // 8 * 8
        self._arena_off = off + shape[1] * esz
        self._arena_peak = max(self._arena_peak, self._arena_off)
        assert self._arena_off <= self.arena.shape[1] * 4, \
            f"arena overflow in {getattr(self, '_phase', '?')}: {self._arena_off}"
        return self.arena.ap().bitcast(dtype)[0:shape[0],
                                              off // esz: off // esz + shape[1]]

    def psum(self):
        self.PS = self.es.enter_context(self.nc.psum_tensor("PS", [128, 4096], F32))
        self.PSB = self.PS.ap().bitcast(BF16)
        return self.PS

    def ps32(self, col0, width, p0=0, pn=128):
        return self.PS.ap()[p0:p0 + pn, col0:col0 + width]

    def psbf(self, col0, width, p0=0, pn=128):
        return self.PSB[p0:p0 + pn, col0:col0 + width]

    def _sem(self, name):
        if name not in self.semh:
            self.semh[name] = self.es.enter_context(self.nc.semaphore(name))
            self.counts[name] = 0
        return self.semh[name]

    def _push(self, eng, fn, flaggable=False):
        entry = {"fn": fn, "inc": None}
        self.queues[eng].append(entry)
        if flaggable:
            self._lastop = getattr(self, "_lastop", {})
            self._lastop[eng] = entry
        return entry

    def raw(self, eng, fn):
        self._push(eng, fn)

    def op(self, eng, opname, *args, **kw):
        if eng == "tensor" and opname in ("matmul", "transpose"):
            lhs = args[1]
            b = lhs.base_partition()
            n = lhs.partition_size()
            key = (b, b + n)
            last = getattr(self, "_pe_rows", None)
            if last is not None and (key[0] >= last[1] or key[1] <= last[0]):
                self._push(eng, lambda e: e.drain())
            self._pe_rows = key
        self._push(eng, lambda e: getattr(e, opname)(*args, **kw), flaggable=True)

    def drain(self, eng):
        if eng == "tensor":
            self._pe_rows = None
        self._push(eng, lambda e: e.drain())

    def signal(self, eng, sem=None, force_drain=False):
        if eng == "tensor":
            self._pe_rows = None
        sem = sem or ("c_" + eng)
        h = self._sem(sem)
        self.counts[sem] += 1
        v = self.counts[sem]
        self.sim_trace.setdefault(eng, []).append(("inc", sem, 1))
        self._push(eng, lambda e: e.drain())
        self._push(eng, lambda e: e.nop())
        self.queues[eng][-1]["inc"] = (h, 1)
        return (sem, v)

    def wait(self, eng, sig):
        sem, v = sig
        if sem == "c_" + eng:
            return
        h = self._sem(sem)
        key = (eng, sem)
        if self.watermark.get(key, 0) >= v:
            return
        self.watermark[key] = v
        self.sim_trace.setdefault(eng, []).append(("wait", sem, v))
        self._push(eng, lambda e: e.wait_ge(h, v))

    def dma(self, eng, sem, out, in_):
        h = self._sem(sem)
        self.counts[sem] += 16
        v = self.counts[sem]
        self.sim_trace.setdefault(eng, []).append(("inc", sem, 16))
        self._push(eng, lambda e: e.dma_start(out=out, in_=in_).then_inc(h, 16))
        return (sem, v)

    def dma_all(self, sem):
        self._sem(sem)
        return (sem, self.counts[sem])

    def finish(self):
        for name in list(self.counts):
            if not name.startswith("c_"):
                self.wait("sync", (name, self.counts[name]))

    def build(self):
        with self.es:
            with self.nc.Block() as block:
                for e in ENGS:
                    q = self.queues[e]
                    if not q:
                        continue
                    def runner(eng, q=q):
                        for item in q:
                            inst = item["fn"](eng)
                            if item["inc"] is not None:
                                h, n = item["inc"]
                                inst.then_inc(h, n)
                    getattr(block, e)(runner)
        return self.nc



def _band_penalty(T):
    LOCAL = 64
    def mask_valid(i, j):
        return (j <= i) and (j >= 0) and ((abs(i - j) <= LOCAL) or (i == 0) or (j == 0))
    pens = {}
    for variant in ("g", "0"):
        pen = np.full((128, 256), NEG, dtype=np.float32)
        qt = 1 if variant == "g" else 0
        base = qt * 128
        for r in range(128):
            i = base + r
            for c in range(192):
                j = base - 64 + c
                if mask_valid(i, j):
                    pen[r, c] = 0.0
            if variant == "g":
                if mask_valid(i, 0):
                    pen[r, 192] = 0.0
        pens[variant] = pen
    return pens["g"], pens["0"]


def _chunk_bias(b):
    return np.ascontiguousarray(b.reshape(CCN, 128).T.astype(np.float32))


def prepare_host(inputs, T):
    A = {}
    f32 = np.float32
    x = np.asarray(inputs["L0_attn_hidden_states"], dtype=f32)
    B = x.shape[0]
    A["__percore_xT"] = [np.ascontiguousarray(x[b].T) for b in range(B)]

    for nm in ("Wq", "Wk", "Wv", "Wo"):
        W = np.asarray(inputs[f"L0_attn_{nm}"], dtype=f32)
        A[f"l0a_{nm}T"] = np.ascontiguousarray(W.T)
    for nm in ("bq", "bk", "bo"):
        A[f"l0a_{nm}"] = _chunk_bias(np.asarray(inputs[f"L0_attn_{nm}"], dtype=f32))
    A["l0a_bv_row"] = np.asarray(inputs["L0_attn_bv"], dtype=f32).reshape(1, C)

    Wq = np.asarray(inputs["L0_ffn_Wq"], dtype=f32)
    Wk = np.asarray(inputs["L0_ffn_Wk"], dtype=f32)
    A["l0f_WqT"] = np.ascontiguousarray(Wq.reshape(C, C).T)
    A["l0f_WkT"] = np.ascontiguousarray(Wk.reshape(C, C).T)
    Ws = np.asarray(inputs["L0_ffn_Ws"], dtype=f32).reshape(128, C)
    Wd = np.asarray(inputs["L0_ffn_Wd"], dtype=f32).reshape(128, C)
    A["l0f_WsdT"] = np.ascontiguousarray(np.concatenate([Ws, Wd], 0).T)
    Wv = np.asarray(inputs["L0_ffn_Wv"], dtype=f32).reshape(512, C)
    A["l0f_WvT"] = np.ascontiguousarray(Wv.T)
    Wo = np.asarray(inputs["L0_ffn_Wo"], dtype=f32)
    A["l0f_WoT"] = np.ascontiguousarray(
        Wo.transpose(0, 2, 1).reshape(512, C))

    perm = np.zeros(C, dtype=np.int64)
    for r in range(512):
        perm[r] = (r // 32) * 64 + 2 * (r % 32)
        perm[512 + r] = (r // 32) * 64 + 2 * (r % 32) + 1
    for nm in ("Wq", "Wk"):
        W = np.asarray(inputs[f"L1_attn_{nm}"], dtype=f32)
        A[f"l1a_{nm}T"] = np.ascontiguousarray(W[:, perm])
    A["l1a_bq"] = _chunk_bias(np.asarray(inputs["L1_attn_bq"], dtype=f32)[perm])
    A["l1a_bk"] = _chunk_bias(np.asarray(inputs["L1_attn_bk"], dtype=f32)[perm])
    A["l1a_WvT"] = np.ascontiguousarray(np.asarray(inputs["L1_attn_Wv"], f32))
    A["l1a_bv_row"] = np.asarray(inputs["L1_attn_bv"], dtype=f32).reshape(1, C)
    A["l1a_WoT"] = np.ascontiguousarray(np.asarray(inputs["L1_attn_Wo"], f32))
    A["l1a_bo"] = _chunk_bias(np.asarray(inputs["L1_attn_bo"], dtype=f32))

    pos = np.arange(T, dtype=np.float32)
    freq = np.arange(0, HD, 2, dtype=np.float32) / HD
    inv_freq = (1.0 / 10000.0 ** freq).astype(np.float32)
    ang = np.outer(pos, inv_freq).astype(np.float32)
    ang2 = np.repeat(ang, 2, axis=-1)
    cos = np.cos(ang2).astype(f32)
    sin = np.sin(ang2).astype(f32)
    for nm, tab in (("cosE", cos[:, 0::2]), ("sinE", sin[:, 0::2]),
                    ("cosO", cos[:, 1::2]), ("sinO", sin[:, 1::2])):
        A[f"rope_{nm}"] = np.ascontiguousarray(np.tile(tab.T, (4, 1)))

    for nm in ("cn_w1", "cn_w2", "cn_w3", "gc_w1", "gc_w2"):
        A[f"l1f_{nm}"] = np.ascontiguousarray(np.asarray(inputs[f"L1_ffn_{nm}"], f32))
    for nm in ("cn_w4", "gc_w3"):
        w = np.asarray(inputs[f"L1_ffn_{nm}"], f32).reshape(-1)
        RC = w.shape[0] // 128
        A[f"l1f_{nm}"] = np.ascontiguousarray(
            w.reshape(RC, 128).T.reshape(1, -1))
    for nm, dim in (("cn_b1", 1024), ("cn_b2", 512), ("cn_b3", 256),
                    ("gc_b1", 512), ("gc_b2", 256)):
        b = np.asarray(inputs[f"L1_ffn_{nm}"], dtype=f32)
        A[f"l1f_{nm}"] = np.ascontiguousarray(b.reshape(dim // 128, 128).T)
    A["l1f_cn_b4"] = np.asarray(inputs["L1_ffn_cn_b4"], f32).reshape(1, 1)
    A["l1f_gc_b3"] = np.asarray(inputs["L1_ffn_gc_b3"], f32).reshape(1, 1)
    A["l1f_ar"] = np.asarray(inputs["L1_ffn_adaptive_range"], f32).reshape(1, 1)

    A["ident_bf"] = np.eye(128, dtype=ml_dtypes.bfloat16)
    A["ones_row_f"] = np.ones((1, 128), dtype=f32)
    A["ones_f128"] = np.ones((128, 64), dtype=f32)
    A["ones_row_bf"] = np.ones((1, 128), dtype=ml_dtypes.bfloat16)
    A["ones_col_bf"] = np.ones((128, 1), dtype=ml_dtypes.bfloat16)
    pen_g, pen_0 = _band_penalty(T)
    A["pen_g"] = pen_g
    A["pen_0"] = pen_0
    tri = np.zeros((128, 128), dtype=f32)
    for r in range(128):
        tri[r, :r] = NEG
    A["pen_tri"] = tri
    iota = np.tile(np.arange(8, dtype=f32), (128, 32))
    A["iota"] = iota
    A["iotap"] = iota + 16384.0
    fc = np.zeros((128, 8), dtype=f32)
    fc[:, 0] = EPS
    fc[:, 1] = -M0
    A["fconsts"] = fc
    return A



class Common:

    def __init__(self, p: Prog, T):
        self.p = p
        self.T = T
        nc = p.nc
        p.psum()
        self.ident = p.sb("s_ident", [128, 128], BF16)
        self.ones_row_f = p.sb("s_ones_row_f", [1, 128], F32)
        self.ones_row_bf = p.sb("s_ones_row_bf", [1, 128], BF16)
        self.ones_col_bf = p.sb("s_ones_col_bf", [128, 1], BF16)
        self.ones_f128 = p.sb("s_ones_f128", [128, 64], F32)
        self.const_loads = []
        for nm, dst in (("ident_bf", self.ident), ("ones_row_f", self.ones_row_f),
                        ("ones_row_bf", self.ones_row_bf), ("ones_col_bf", self.ones_col_bf),
                        ("ones_f128", self.ones_f128)):
            ap = p.din(nm, dst.shape, dst.dtype)
            p.dma("sync", "cst", dst[:], ap)
        self.fc = p.sb("s_fconsts", [128, 8], F32)
        p.dma("sync", "cst", self.fc[:], p.din("fconsts", [128, 8], F32))
        self.stage = p.sb("stage", [128, 1024], F32)
        self.wbuf = p.sb("wbuf", [128, 8192], BF16)
        self.stage_free = None

    def loadc(self, name, shape, dtype):
        p = self.p
        dst = p.region("c_" + name, shape, dtype)
        ap = p.din(name, shape, dtype)
        p.dma("sync", "cst", dst[:], ap)
        return dst

    def wait_consts(self, engs):
        sig = self.p.dma_all("cst")
        for e in engs:
            self.p.wait(e, sig)

    def load_weight(self, wname, hbm, R, O, dst, dst_off=0, war_sig=None,
                    conv="vector"):
        p = self.p
        assert O <= 1024
        RC = R // 128
        cpop = "tensor_copy" if conv == "vector" else "copy"
        if O == 1:
            if war_sig is not None:
                p.wait(conv, war_sig)
            dsig = p.dma("sync", "wstg",
                         self.stage[:, 0:RC],
                         hbm.rearrange("o (pp rc) -> pp (o rc)", pp=128))
            p.wait(conv, dsig)
            p.op(conv, cpop, dst[:, dst_off:dst_off + RC],
                 self.stage[:, 0:RC])
            sig = p.signal(conv)
            p.wait("sync", sig)
            return sig
        hb3 = hbm.rearrange("(rc p) o -> p rc o", p=128)
        if war_sig is not None:
            p.wait(conv, war_sig)
        OB = min(512, O)
        pieces = [(rc, o0) for rc in range(RC) for o0 in range(0, O, OB)]
        sigs = {}
        sig = None
        for i, (rc, o0) in enumerate(pieces):
            slot = i % 2
            if i >= 2:
                p.wait("sync", sigs[i - 2])
            dsig = p.dma("sync", "wstg%d" % slot,
                         self.stage[:, slot * 512: slot * 512 + OB],
                         hb3[:, rc, o0:o0 + OB])
            p.wait(conv, dsig)
            p.op(conv, cpop,
                 dst[:, dst_off + rc * O + o0: dst_off + rc * O + o0 + OB],
                 self.stage[:, slot * 512: slot * 512 + OB])
            sig = sigs[i] = p.signal(conv)
        p.wait("sync", sig)
        return sig

    def _stage_guard(self, sig):
        self.p.wait("sync", sig)

    def emit_ln_stream(self, xT, xn_bf, scratch_a, scratch_b, fp32_sink=None):
        p, T = self.p, self.T
        av = apof(scratch_a).bitcast(F32)
        bv = apof(scratch_b).bitcast(F32)
        xc = av[:, 0:2 * T]
        tmp = av[:, 2 * T:3 * T]
        sq = av[:, 3 * T:4 * T].bitcast(BF16)
        stat = bv[0:1, 0:2 * T]
        rows = bv[0:1, 2 * T:4 * T]
        xt3 = xT.rearrange("(cc p) t -> p cc t", p=128)
        self.wait_consts(["tensor", "scalar", "vector"])
        QW = min(512, T)
        mmsigs = {}
        sqsigs = {}
        cvsigs = {}
        for cc in range(CCN):
            slot = cc % 2
            semn = "lnx%d" % slot
            if cc >= 2:
                p.wait("sync", cvsigs[cc - 2])
                p.wait("sync", sqsigs[cc - 2])
            dsig = p.dma("sync", semn, xc[:, slot * T:(slot + 1) * T], xt3[:, cc])
            p.wait("vector", dsig)
            p.op("vector", "tensor_copy", xn_bf[:, cc * T:(cc + 1) * T],
                 xc[:, slot * T:(slot + 1) * T])
            cvsigs[cc] = p.signal("vector")
            p.wait("scalar", dsig)
            if cc >= 2:
                p.wait("scalar", mmsigs[cc - 2])
            p.op("scalar", "activation", sq[:, slot * T:slot * T + T],
                 xc[:, slot * T:(slot + 1) * T], AF.Square)
            sqsigs[cc] = p.signal("scalar")
            p.wait("tensor", sqsigs[cc])
            p.wait("tensor", cvsigs[cc])
            for q in range(T // QW):
                p.op("tensor", "matmul", p.ps32(q * QW, QW, 0, 1),
                     self.ones_col_bf[:], xn_bf[:, cc * T + q * QW: cc * T + (q + 1) * QW],
                     start=(cc == 0), stop=(cc == CCN - 1))
                p.op("tensor", "matmul", p.ps32(2048 + q * QW, QW, 0, 1),
                     self.ones_col_bf[:], sq[:, slot * T + q * QW: slot * T + (q + 1) * QW],
                     start=(cc == 0), stop=(cc == CCN - 1))
            mmsigs[cc] = p.signal("tensor")
        mmsig = mmsigs[CCN - 1]
        p.wait("scalar", mmsig)
        p.op("scalar", "copy", stat[0:1, 0:T], p.ps32(0, T, 0, 1))
        p.op("scalar", "copy", stat[0:1, T:2 * T], p.ps32(2048, T, 0, 1))
        evsig = p.signal("scalar")
        p.wait("vector", evsig)
        p.op("vector", "tensor_scalar", rows[0:1, 0:T], stat[0:1, 0:T],
             1.0 / C, None, ALU.mult)
        p.drain("vector")
        p.op("vector", "tensor_tensor", rows[0:1, T:2 * T], rows[0:1, 0:T],
             rows[0:1, 0:T], ALU.mult)
        p.drain("vector")
        p.op("vector", "scalar_tensor_tensor", rows[0:1, T:2 * T], stat[0:1, T:2 * T],
             1.0 / C, rows[0:1, T:2 * T], ALU.mult, ALU.subtract)
        vsig = p.signal("vector")
        p.wait("scalar", vsig)
        p.op("scalar", "activation", stat[0:1, 0:T], rows[0:1, T:2 * T], AF.Ln,
             bias=self.fc[0:1, 0:1])
        p.drain("scalar")
        p.op("scalar", "activation", stat[0:1, 0:T], stat[0:1, 0:T], AF.Exp, scale=-0.5)
        rsig = p.signal("scalar")
        p.wait("vector", rsig)
        p.op("vector", "scalar_tensor_tensor", rows[0:1, T:2 * T], rows[0:1, 0:T],
             -1.0, stat[0:1, 0:T], ALU.mult, ALU.mult)
        bsig = p.signal("vector")
        p.wait("tensor", rsig)
        p.wait("tensor", bsig)
        for q in range(T // QW):
            p.op("tensor", "matmul", p.ps32(q * QW, QW), self.ones_row_f[:],
                 stat[0:1, q * QW:(q + 1) * QW], start=True, stop=True)
            p.op("tensor", "matmul", p.ps32(2048 + q * QW, QW), self.ones_row_f[:],
                 rows[0:1, T + q * QW: T + (q + 1) * QW], start=True, stop=True)
        bcsig = p.signal("tensor")
        p.wait("vector", bcsig)
        outsigs = {}
        for s in (sqsigs[CCN - 1], sqsigs[CCN - 2], cvsigs[CCN - 1], cvsigs[CCN - 2]):
            p.wait("sync", s)
        sink_sig = None
        for cc in range(CCN):
            slot = cc % 2
            semn = "lnn%d" % slot
            if cc >= 2:
                p.wait("sync", outsigs[cc - 2])
            dsig = p.dma("sync", semn, xc[:, slot * T:(slot + 1) * T], xt3[:, cc])
            p.wait("vector", dsig)
            if sink_sig is not None:
                p.wait("vector", sink_sig)
            p.drain("vector")
            p.op("vector", "tensor_tensor", tmp, xc[:, slot * T:(slot + 1) * T],
                 p.ps32(0, T), ALU.mult)
            p.drain("vector")
            if fp32_sink is not None:
                p.op("vector", "tensor_tensor", tmp, tmp, p.ps32(2048, T), ALU.add)
                p.drain("vector")
                p.op("vector", "tensor_copy", xn_bf[:, cc * T:(cc + 1) * T], tmp)
                sig = p.signal("vector")
                sink_sig = fp32_sink(cc, tmp, sig)
            else:
                p.op("vector", "tensor_tensor", xn_bf[:, cc * T:(cc + 1) * T],
                     tmp, p.ps32(2048, T), ALU.add)
                sig = p.signal("vector")
            outsigs[cc] = sig
        return outsigs[CCN - 1]

    def emit_proj_fm(self, wbuf, woff, src_bf, dst_bf, OC, bias=None, act=AF.Identity,
                     wait_sigs=(), evac_engine="scalar", evac_cb=None, Odim=None,
                     CC=CCN, pre_mm=None):
        p, T = self.p, self.T
        Odim = Odim or OC * 128
        QW = min(512, T)
        NT = T // QW
        for s in wait_sigs:
            p.wait("tensor", s)
        evsigs = {}
        k = 0
        pair_ok = (NT % 2 == 0) and evac_cb is None
        for oc in range(OC):
            tc = 0
            while tc < NT:
                width = 2 if pair_ok else 1
                bank = k % 4
                for kk in range(k, k + width):
                    if kk - 4 in evsigs:
                        p.wait("tensor", evsigs[kk - 4])
                if pre_mm is not None:
                    for w_ in range(width):
                        pre_mm(oc, tc + w_)
                for w_ in range(width):
                    ps = p.ps32((bank + w_) * 512, QW)
                    for cc in range(CC):
                        p.op("tensor", "matmul", ps,
                             wbuf[:, woff + cc * Odim + oc * 128: woff + cc * Odim + (oc + 1) * 128],
                             src_bf[:, cc * T + (tc + w_) * QW: cc * T + (tc + w_ + 1) * QW],
                             start=(cc == 0), stop=(cc == CC - 1))
                msig = p.signal("tensor")
                p.wait(evac_engine, msig)
                if evac_cb is not None:
                    evac_cb(oc, tc, p.ps32(bank * 512, QW))
                else:
                    kw = {}
                    if bias is not None:
                        kw["bias"] = bias[:, oc:oc + 1]
                    p.op(evac_engine, "activation",
                         dst_bf[:, oc * T + tc * QW: oc * T + (tc + width) * QW],
                         p.ps32(bank * 512, width * QW), act, **kw)
                sig = p.signal(evac_engine)
                for kk in range(k, k + width):
                    evsigs[kk] = sig
                k += width
                tc += width
        return evsigs[k - 1]

    def emit_proj_residual(self, wbuf, src_bf, x_hbm, out_hbm, bias=None,
                           wait_sigs=(), OC=CCN, CC=CCN, Odim=None,
                           xbuf=None, obuf=None):
        p, T = self.p, self.T
        QW = min(512, T)
        xin3 = x_hbm.rearrange("(cc p) t -> p cc t", p=128)
        out3 = out_hbm.rearrange("(cc p) t -> p cc t", p=128)
        if xbuf is None:
            xbuf = p.region("res_x", [128, 2 * QW], F32)
        if obuf is None:
            obuf = p.region("res_o", [128, 2 * QW], F32)
        stsigs = {}
        dsigs = {}
        k = 0
        NT = T // QW

        def pre(oc, tc):
            kk = oc * NT + tc
            slot = kk % 2
            if kk >= 2:
                p.wait("sync", stsigs[kk - 2])
            dsigs[kk] = p.dma("sync", "resx%d" % slot,
                              xbuf[:, slot * QW:(slot + 1) * QW],
                              xin3[:, oc, tc * QW:(tc + 1) * QW])

        def evac(oc, tc, ps):
            kk = oc * NT + tc
            slot = kk % 2
            p.wait("vector", dsigs[kk])
            if kk >= 2:
                p.wait("vector", stsigs[kk - 2])
            if bias is not None:
                p.op("vector", "scalar_tensor_tensor",
                     obuf[:, slot * QW:(slot + 1) * QW], ps, bias[:, oc:oc + 1],
                     xbuf[:, slot * QW:(slot + 1) * QW], ALU.add, ALU.add)
            else:
                p.op("vector", "tensor_tensor",
                     obuf[:, slot * QW:(slot + 1) * QW], ps,
                     xbuf[:, slot * QW:(slot + 1) * QW], ALU.add)
            esig = p.signal("vector")
            p.wait("sync", esig)
            stsigs[kk] = p.dma("sync", "ro%d" % slot, out3[:, oc, tc * QW:(tc + 1) * QW],
                               obuf[:, slot * QW:(slot + 1) * QW])

        return self.emit_proj_fm(wbuf, 0, src_bf, None, OC, wait_sigs=wait_sigs,
                                 evac_engine="vector", evac_cb=evac, pre_mm=pre,
                                 CC=CC, Odim=Odim)





    def emit_v_tm(self, wbuf, src_bf, v_tm, bvr_bf, wait_sigs=(), stride=1040):
        p, T = self.p, self.T
        NT16 = T // 128
        if stride == 1040:
            vall = v_tm[:, 0:NT16 * 1040].rearrange(
                "p (t h e) -> p t h e", h=NH, e=65)
            p.op("vector", "memset", vall[:, :, :, 64:65], 1.0)
            onesig = p.signal("vector")
            p.wait("scalar", onesig)
        for s in wait_sigs:
            p.wait("tensor", s)
        evsigs = {}
        for tt in range(NT16):
            bk2 = (tt % 2) * 2
            if tt >= 2:
                p.wait("tensor", evsigs[tt - 2])
            for half in range(2):
                ps = p.ps32(bk2 * 512 + half * 512, 512)
                for cc in range(CCN):
                    p.op("tensor", "matmul", ps,
                         src_bf[:, cc * T + tt * 128: cc * T + (tt + 1) * 128],
                         wbuf[:, cc * C + half * 512: cc * C + (half + 1) * 512],
                         start=(cc == 0), stop=(bvr_bf is None and cc == CCN - 1))
                if bvr_bf is not None:
                    p.op("tensor", "matmul", ps, self.ones_row_bf[:],
                         bvr_bf[0:1, half * 512:(half + 1) * 512],
                         start=False, stop=True)
            msig = p.signal("tensor")
            p.wait("scalar", msig)
            if stride == 1040:
                v3 = v_tm[:, tt * 1040:(tt + 1) * 1040].rearrange(
                    "p (h e) -> p h e", e=65)[:, :, 0:64]
                src3 = p.ps32(bk2 * 512, 1024).rearrange("p (h d) -> p h d", d=64)
                p.op("scalar", "activation", v3, src3, AF.Identity)
            else:
                p.op("scalar", "activation", v_tm[:, tt * 1024:(tt + 1) * 1024],
                     p.ps32(bk2 * 512, 1024), AF.Identity)
            evsigs[tt] = p.signal("scalar")
        return evsigs[NT16 - 1]

    def emit_proj_tm(self, wbuf, src_bf, dst, Odim, wait_sigs=(), act=AF.Identity,
                     extra_mm=None, woff=0, ret_all=False):
        p, T = self.p, self.T
        NT16 = T // 128
        for s in wait_sigs:
            p.wait("tensor", s)
        evsigs = {}
        pair = (NT16 % 2 == 0) and Odim <= 512
        step = 2 if pair else 1
        for t0 in range(0, NT16, step):
            bank = t0 % 4
            for tt in range(t0, t0 + step):
                if tt - 4 in evsigs:
                    p.wait("tensor", evsigs[tt - 4])
            for w_ in range(step):
                ps = p.ps32((bank + w_) * 512, Odim)
                for cc in range(CCN):
                    p.op("tensor", "matmul", ps,
                         src_bf[:, cc * T + (t0 + w_) * 128: cc * T + (t0 + w_ + 1) * 128],
                         wbuf[:, woff + cc * Odim: woff + (cc + 1) * Odim],
                         start=(cc == 0), stop=(cc == CCN - 1))
                if extra_mm is not None:
                    extra_mm(ps, Odim)
            msig = p.signal("tensor")
            p.wait("scalar", msig)
            if pair:
                pv2 = p.PS.ap()[:, bank * 512: bank * 512 + 1024].rearrange(
                    "p (w o) -> p w o", o=512)[:, :, 0:Odim]
                dv2 = dst[:, t0 * Odim:(t0 + 2) * Odim].rearrange(
                    "p (w o) -> p w o", o=Odim)
                p.op("scalar", "activation", dv2, pv2, act)
            else:
                p.op("scalar", "activation", dst[:, t0 * Odim:(t0 + 1) * Odim],
                     p.ps32(bank * 512, Odim), act)
            sig = p.signal("scalar")
            for tt in range(t0, t0 + step):
                evsigs[tt] = sig
        return evsigs if ret_all else evsigs[NT16 - 1]

    def emit_top2(self, g, member, scrf, iota, iotap, wait_sig=None):
        p = self.p
        E, G = 8, 32
        t16 = scrf[:, 0:128]
        t8 = scrf[:, 128:192]
        m1 = scrf[:, 192:224]
        idx = scrf[:, 224:256]
        eqb = scrf[:, 256:512]
        zb = scrf[:, 512:768]
        g2 = scrf[:, 768:1024]
        ism1 = scrf[:, 1024:1280]
        if wait_sig is not None:
            p.wait("vector", wait_sig)

        def fold(srcap, mdst, op):
            s3 = srcap.rearrange("p (g e) -> p g e", e=E)
            t16_3 = t16.rearrange("p (g e) -> p g e", e=4)
            t8_3 = t8.rearrange("p (g e) -> p g e", e=2)
            p.op("vector", "tensor_tensor", t16_3, s3[:, :, 0:4], s3[:, :, 4:8], op)
            p.drain("vector")
            p.op("vector", "tensor_tensor", t8_3, t16_3[:, :, 0:2], t16_3[:, :, 2:4], op)
            p.drain("vector")
            p.op("vector", "tensor_tensor", mdst, t8_3[:, :, 0:1].squeeze(2),
                 t8_3[:, :, 1:2].squeeze(2), op)
            p.drain("vector")

        def argfirst(srcap, ismdst):
            s3 = srcap.rearrange("p (g e) -> p g e", e=E)
            mexp = m1[:, :, None].broadcast_to([128, G, E])
            eq3 = eqb.rearrange("p (g e) -> p g e", e=E)
            p.op("vector", "tensor_tensor", eq3, s3, mexp, ALU.is_equal)
            p.drain("vector")
            p.op("vector", "scalar_tensor_tensor", zb, eqb, -16384.0, iotap[:],
                 ALU.mult, ALU.add)
            p.drain("vector")
            fold(zb, idx, ALU.min)
            iexp = idx[:, :, None].broadcast_to([128, G, E])
            im3 = ismdst.rearrange("p (g e) -> p g e", e=E)
            p.op("vector", "tensor_tensor", im3,
                 iota[:].rearrange("p (g e) -> p g e", e=E), iexp, ALU.is_equal)
            p.drain("vector")

        fold(g, m1, ALU.max)
        argfirst(g, ism1)
        p.op("vector", "scalar_tensor_tensor", g2, ism1, -1e30, g, ALU.mult, ALU.add)
        p.drain("vector")
        fold(g2, m1, ALU.max)
        argfirst(g2, member)
        p.op("vector", "tensor_tensor", member, ism1, member, ALU.add)
        return p.signal("vector")



def emit_L0a(p, cm, T, xT, o_xpart, o_xn, stop=None):
    NT16 = T // 128

    q_fm = p.region("a_q_fm", [128, CCN * T], BF16)
    k_fm = p.region("a_k_fm", [128, CCN * T], BF16)
    v_tm = p.region("a_v_tm", [128, NT16 * C], BF16)
    xnO = p.region("a_xnO", [128, CCN * T], BF16)
    wbuf = cm.wbuf
    P_all = p.region("a_P_all", [128, 4096], BF16)
    Pn_all = p.region("a_Pn_all", [128, 4096], BF16)
    PT_all = p.region("a_PT_all", [128, 4096], BF16)
    rsb = p.region("a_rsb", [128, 48], F32)
    bq = cm.loadc("l0a_bq", [128, 8], F32)
    bk = cm.loadc("l0a_bk", [128, 8], F32)
    bo = cm.loadc("l0a_bo", [128, 8], F32)
    bvr = cm.loadc("l0a_bv_row", [1, C], F32)
    bvr_bf = p.region("a_bvr_bf", [1, C], BF16)
    pen_g = cm.loadc("pen_g", [128, 256], F32)
    pen_0 = cm.loadc("pen_0", [128, 256], F32)

    lnsig = cm.emit_ln_stream(xT, xnO, q_fm, k_fm)

    if stop == "ln":
        p.finish()
        return
    cm.wait_consts(["vector"])
    p.op("vector", "tensor_copy", bvr_bf[:], bvr[:])
    bvsig = p.signal("vector")

    wq = p.din("l0a_WqT", [C, C], F32)
    wk = p.din("l0a_WkT", [C, C], F32)
    wv = p.din("l0a_WvT", [C, C], F32)
    wo = p.din("l0a_WoT", [C, C], F32)

    wsig = cm.load_weight("wq", wq, C, C, wbuf)
    qsig = cm.emit_proj_fm(wbuf, 0, xnO, q_fm, CCN, bias=bq,
                           wait_sigs=[lnsig, wsig])
    wsig = cm.load_weight("wk", wk, C, C, wbuf, war_sig=qsig)
    ksig = cm.emit_proj_fm(wbuf, 0, xnO, k_fm, CCN, bias=bk, wait_sigs=[wsig])

    wsig = cm.load_weight("wv", wv, C, C, wbuf, war_sig=ksig)
    vsig = cm.emit_v_tm(wbuf, xnO, v_tm, bvr_bf, wait_sigs=[wsig, bvsig], stride=1024)
    if stop == "vtm":
        p.finish()
        return

    if stop == "proj":
        p.finish()
        return
    if stop == "projq":
        cm.load_weight("wk2", wk, C, C, wbuf, war_sig=qsig)
        p.finish()
        return
    p.wait("sync", lnsig)
    xnstore = p.dma("sync", "out", o_xn.rearrange("(cc p) t -> p cc t", p=128),
                    xnO[:].rearrange("p (cc t) -> p cc t", cc=CCN))
    if stop == "xns":
        p.finish()
        return

    O_fm = xnO
    p.wait("vector", vsig)
    p.wait("tensor", vsig)
    p.wait("vector", p.dma_all("cst"))
    prev_qt_done = None
    s_sig = {}
    NQT = NT16 if stop is None or not str(stop).startswith("qt") else int(str(stop)[2:])
    for qt in range(NQT):
        pen = pen_0 if qt == 0 else pen_g
        p.wait("tensor", qsig)
        if stop == "qw":
            p.signal("tensor")
            p.finish()
            return
        NHEM = NH if stop is None or not str(stop).startswith("qh") else int(str(stop)[2:])
        if prev_qt_done is not None:
            p.wait("tensor", prev_qt_done)
        heads_ord = [hh for hh in range(NHEM) if hh % 2 == 0] + \
                    [hh for hh in range(NHEM) if hh % 2 == 1]
        for h in heads_ord:
            base = (h % 2) * 64
            qh = q_fm[base:base + 64,
                      (h // 2) * T + qt * 128: (h // 2) * T + (qt + 1) * 128]
            kh_all = k_fm[base:base + 64, (h // 2) * T:(h // 2 + 1) * T]
            if qt == 0:
                p.op("tensor", "matmul", p.ps32(h * 256 + 64, 128), qh,
                     kh_all[:, 0:128], start=True, stop=True)
            else:
                p.op("tensor", "matmul", p.ps32(h * 256, 192), qh,
                     kh_all[:, qt * 128 - 64: qt * 128 + 128], start=True, stop=True)
                p.op("tensor", "matmul", p.ps32(h * 256 + 192, 1), qh,
                     kh_all[:, 0:1], start=True, stop=True)
        ssig = p.signal("tensor")
        if stop == "qs" or (stop is not None and str(stop).startswith("qh")):
            p.finish(); return
        p.wait("vector", ssig)
        sview = p.PS.ap()[:, 0:4096].rearrange("p (h c) -> p h c", c=256)[:, :, 0:193]
        if qt == 0:
            full = p.PS.ap()[:, 0:4096].rearrange("p (h c) -> p h c", c=256)
            p.op("vector", "memset", full[:, :, 0:64], 0.0)
            p.op("vector", "memset", full[:, :, 192:193], 0.0)
            p.drain("vector")
        penv = pen[:].rearrange("p (g c) -> p g c", g=1)[:, :, 0:193].broadcast_to([128, NH, 193])
        p.op("vector", "tensor_tensor", sview, sview, penv, ALU.add)
        pensig = p.signal("vector")
        p.wait("scalar", pensig)
        pview = P_all[:].rearrange("p (h c) -> p h c", c=256)[:, :, 0:193]
        p.op("scalar", "activation", pview, sview, AF.Exp, bias=cm.fc[:, 1:2], scale=0.125)
        esig = p.signal("scalar")
        s_sig[qt] = esig
        if stop == "qe":
            p.finish(); return
        p.wait("vector", esig)
        p.op("vector", "tensor_reduce", rsb[:, 0:16], pview, AX.X, ALU.add)
        p.drain("vector")
        p.op("vector", "reciprocal", rsb[:, 16:32], rsb[:, 0:16])
        p.drain("vector")
        for h in range(NH):
            p.op("vector", "tensor_scalar", Pn_all[:, h * 256:h * 256 + 193],
                 P_all[:, h * 256:h * 256 + 193], rsb[:, 16 + h:17 + h], None, ALU.mult)
        nsig = p.signal("vector")
        if stop == "qn":
            p.finish(); return
        p.wait("tensor", esig)
        p.wait("tensor", nsig)
        for h in range(NH):
            p.op("tensor", "transpose", p.psbf(h * 128, 128),
                 Pn_all[:, h * 256 + 64: h * 256 + 192], cm.ident[:])
            p.op("tensor", "transpose", p.psbf(2048 + h * 128, 128, 64, 64),
                 Pn_all[:, h * 256 + 0: h * 256 + 64], cm.ident[:])
            p.op("tensor", "transpose", p.psbf(2048 + h * 128, 128, 0, 1),
                 Pn_all[:, h * 256 + 192: h * 256 + 193], cm.ident[:])
        trsig = p.signal("tensor")
        p.wait("vector", trsig)
        ptb = PT_all[:].rearrange("p (h two c) -> p h two c", two=2, c=128)
        psb_b = p.PSB[:, 0:2048].rearrange("p (h c) -> p h c", c=128)
        psb_ac = p.PSB[:, 2048:4096].rearrange("p (h c) -> p h c", c=128)
        p.op("vector", "tensor_copy", ptb[:, :, 0, :], psb_b)
        p.op("vector", "tensor_copy", ptb[64:128, :, 1, :], psb_ac[64:128])
        p.op("vector", "tensor_copy", ptb[0:1, :, 1, :], psb_ac[0:1])
        ptsig = p.signal("vector")
        if stop == "qtr":
            p.finish(); return
        p.wait("tensor", ptsig)
        for h in range(NH):
            out_ap = p.ps32(2048 + h * 128, 128, (h % 2) * 64, 64)
            mmA = (v_tm[64:128, (qt - 1) * C + h * 64: (qt - 1) * C + (h + 1) * 64],
                   PT_all[64:128, h * 256 + 128: h * 256 + 256]) if qt > 0 else None
            mmB = (v_tm[:, qt * C + h * 64: qt * C + (h + 1) * 64],
                   PT_all[:, h * 256: h * 256 + 128])
            mmC = (v_tm[0:1, h * 64:(h + 1) * 64],
                   PT_all[0:1, h * 256 + 128: h * 256 + 256])
            seq = [mmA, mmB, mmC] if h % 2 == 0 else [mmC, mmB, mmA]
            seq = [s for s in seq if s is not None]
            for i, (lh, rh) in enumerate(seq):
                p.op("tensor", "matmul", out_ap, lh, rh,
                     start=(i == 0), stop=(i == len(seq) - 1))
        pvsig = p.signal("tensor")
        p.wait("vector", pvsig)
        if qt == 0:
            p.wait("vector", xnstore)
        ops = p.PS.ap()[:, 2048:4096].rearrange("p (h c) -> p h c", c=128)
        ofm = O_fm[:].rearrange("p (oc t) -> p oc t", oc=CCN)[:, :, qt * 128:(qt + 1) * 128]
        p.op("vector", "tensor_copy", ofm[0:64], ops[0:64, 0::2, :])
        p.op("vector", "tensor_copy", ofm[64:128], ops[64:128, 1::2, :])
        prev_qt_done = p.signal("vector")
    if stop is not None and str(stop).startswith("qt"):
        p.finish()
        return

    if stop == "band":
        p.finish()
        return
    wsig = cm.load_weight("wo", wo, C, C, wbuf, war_sig=vsig)
    p.wait("tensor", prev_qt_done)
    cm.emit_proj_residual(wbuf, O_fm, xT, o_xpart, bias=bo, wait_sigs=[wsig])
    for s in ("out", "ro0", "ro1"):
        p.wait("sync", p.dma_all(s))
    return p



ARENA_F32 = 46750


def build_fused(T):
    p = Prog("FUSED")
    cm = Common(p, T)
    p.set_arena(ARENA_F32)
    xT = p.din("xT", [C, T], F32)
    o_y = p.dout("o_y", [C, T], F32)

    def ih(name, dt):
        return p.nc.dram_tensor(name, [C, T], dt, kind="Internal").ap()

    xpart = ih("h_xpart", F32)
    xn = ih("h_xn", BF16)
    x1 = ih("h_x1", F32)
    xn1 = ih("h_xn1", BF16)
    xn1f = ih("h_xn1f", F32)
    x2p = ih("h_x2p", F32)

    p.begin_phase("L0a")
    emit_L0a(p, cm, T, xT, xpart, xn)
    phase_barrier(p)
    p.begin_phase("L0f")
    emit_L0f(p, cm, T, xn, xpart, x1)
    phase_barrier(p)
    p.begin_phase("L1a")
    emit_L1a(p, cm, T, x1, x2p, xn1, xn1f)
    phase_barrier(p)
    p.begin_phase("L1f")
    emit_L1f(p, cm, T, xn1, xn1f, x2p, o_y)
    p.finish()
    return p


_CACHE = {}


def _prog(T):
    if T not in _CACHE:
        p = build_fused(T)
        nc = p.build()
        _CACHE[T] = (p, nc)
    return _CACHE[T]


def kernel(**inputs):
    x = np.asarray(inputs["L0_attn_hidden_states"])
    B, T, _ = x.shape
    A = prepare_host(inputs, T)
    p, nc = _prog(T)
    in_maps = []
    for b in range(B):
        m = {}
        for nm in p.input_names:
            m[nm] = A["__percore_xT"][b] if nm == "xT" else A[nm]
        in_maps.append(m)
    res = run_bass_kernel_spmd(nc, in_maps, core_ids=list(range(B)))
    out = np.stack([np.ascontiguousarray(res.results[b]["o_y"].T)
                    for b in range(B)], 0)
    return out.astype(np.float32)


def emit_wide_sum(p, eng, slots, terms, out):
    n = len(slots)
    acc = None
    i = 0
    while i < len(terms):
        free = [s for s in range(n) if s != acc]
        take = min(len(free), len(terms) - i)
        free = free[:take]
        for s, (a, b) in zip(free, terms[i:i + take]):
            p.op(eng, "tensor_tensor", slots[s], a, b, ALU.mult)
        p.drain(eng)
        cur = free
        while len(cur) > 1:
            nxt = []
            for j in range(0, len(cur) - 1, 2):
                p.op(eng, "tensor_tensor", slots[cur[j]], slots[cur[j]],
                     slots[cur[j + 1]], ALU.add)
                nxt.append(cur[j])
            if len(cur) % 2 == 1:
                nxt.append(cur[-1])
            p.drain(eng)
            cur = nxt
        if acc is None:
            acc = cur[0]
        else:
            p.op(eng, "tensor_tensor", slots[acc], slots[acc], slots[cur[0]], ALU.add)
            p.drain(eng)
        i += take
    p.op(eng, "tensor_copy", out, slots[acc])
    p.drain(eng)


def emit_L0f(p, cm, T, xnH, xpart, o_x1, stop=None):
    NT16 = T // 128
    QH = min(1024, T)
    NQH = T // QH
    NTH = QH // 128
    E = 8

    xn = p.region("f_xn", [128, CCN * T], BF16)
    qf = p.region("f_qf", [128, CCN * T], BF16)
    kf = p.region("f_kf", [128, CCN * T], BF16)
    gU = p.region("f_gU", [128, NT16 * 256], F32)
    U_fm = apof(gU).bitcast(BF16)
    pvn = max(NT16 * 512, 8 * QH)
    pvR = p.region("f_pvR", [128, pvn], BF16)
    v_tm = p.region("f_v_tm", [128, NT16 * 65 * NH], BF16)
    mix_v = p.region("f_mix_v", [128, NT16 * 128], BF16)
    mask_o = p.region("f_mask_o", [128, NT16 * 128], BF16)
    scr = p.region("f_scr", [128, 3072], F32)
    scrb = apof(scr).bitcast(BF16)
    wbuf = cm.wbuf
    iota = cm.loadc("iota", [128, 256], F32)
    iotap = cm.loadc("iotap", [128, 256], F32)

    pvf = apof(pvR).bitcast(F32)
    def Pslot(i):
        return pvR[:, i * QH:(i + 1) * QH]
    Ou = pvf[0:65, (3 * QH) // 2:(3 * QH) // 2 + QH]
    Rb = pvf[0:64, (3 * QH) // 2 + QH:(3 * QH) // 2 + 2 * QH]
    On = pvR[0:64, 7 * QH:8 * QH]

    xnsig = p.dma("sync", "xn", xn[:].rearrange("p (cc t) -> p cc t", cc=CCN),
                  xnH.rearrange("(cc p) t -> p cc t", p=128))

    wWq = p.din("l0f_WqT", [C, C], F32)
    wWk = p.din("l0f_WkT", [C, C], F32)
    wWsd = p.din("l0f_WsdT", [C, 256], F32)
    wWv = p.din("l0f_WvT", [C, 512], F32)
    wWo = p.din("l0f_WoT", [512, C], F32)

    wsig_v = cm.load_weight("wv", wWv, C, 512, wbuf)
    wsig_sd = cm.load_weight("wsd", wWsd, C, 256, wbuf, dst_off=4096)
    gsigs = cm.emit_proj_tm(wbuf, xn, gU, 256, wait_sigs=[xnsig, wsig_sd],
                            woff=4096, ret_all=True)
    pvsigs = cm.emit_proj_tm(wbuf, xn, pvR, 512,
                             wait_sigs=[wsig_v, gsigs[NT16 - 1]],
                             ret_all=True)

    member = scr[:, 1280:1536]
    cm.wait_consts(["vector", "scalar"])
    sgsig = {}
    for tt in range(NT16):
        p.wait("scalar", gsigs[tt])
        p.op("scalar", "activation", mix_v[:, tt * 128:(tt + 1) * 128],
             gU[:, tt * 256: tt * 256 + 128], AF.Sigmoid)
        sgsig[tt] = p.signal("scalar")
    tksigs = {}
    for tt in range(NT16):
        gt = gU[:, tt * 256:(tt + 1) * 256]
        p.wait("vector", gsigs[tt])
        cm.emit_top2(gt, member, scr, iota, iotap)
        p.wait("vector", sgsig[tt])
        p.drain("vector")
        p.op("vector", "tensor_tensor", mix_v[:, tt * 128:(tt + 1) * 128],
             mix_v[:, tt * 128:(tt + 1) * 128], member[:, 0:128], ALU.mult)
        p.op("vector", "tensor_copy", mask_o[:, tt * 128:(tt + 1) * 128],
             member[:, 128:256])
        tksigs[tt] = p.signal("vector")

    wsig = cm.load_weight("wq", wWq, C, C, wbuf, war_sig=pvsigs[NT16 - 1],
                          conv="scalar")
    qsig = cm.emit_proj_fm(wbuf, 0, xn, qf, CCN, wait_sigs=[wsig])
    wsig = cm.load_weight("wk", wWk, C, C, wbuf, war_sig=qsig, conv="scalar")
    ksig = cm.emit_proj_fm(wbuf, 0, xn, kf, CCN, wait_sigs=[wsig])

    vsig_tile = {}
    for tt in range(NT16):
        eng = "vector" if tt % 2 == 0 else "gpsimd"
        p.wait(eng, pvsigs[tt])
        p.wait(eng, tksigs[tt])
        p.drain(eng)
        v3 = v_tm[:, tt * 1040:(tt + 1) * 1040].rearrange("p (h e) -> p h e", e=65)
        p.op(eng, "memset", v3[:, :, 64:65], 1.0)
        p.drain(eng)
        if eng == "vector":
            slots = [scrb[:, i * 1024:(i + 1) * 1024]
                     .rearrange("p (h d) -> p h d", d=64) for i in range(4)]
        else:
            slots = [scrb[:, 4096 + i * 1024: 4096 + (i + 1) * 1024]
                     .rearrange("p (h d) -> p h d", d=64) for i in range(2)]
        terms = []
        for e in range(E):
            pvap = pvR[:, tt * 512 + e * 64: tt * 512 + (e + 1) * 64]
            a = pvap[:, None, :].broadcast_to([128, NH, 64])
            mx = mix_v[:, tt * 128 + e: tt * 128 + 120 + e + 1: 8]
            b = mx[:, :, None].broadcast_to([128, NH, 64])
            terms.append((a, b))
        emit_wide_sum(p, eng, slots, terms, v3[:, :, 0:64])
        vsig_tile[tt] = p.signal(eng)

    if stop == "front":
        p.finish()
        return
    wosig = cm.load_weight("wo", wWo, 512, C, wbuf, war_sig=ksig)
    if stop == "wo":
        p.finish()
        return

    p.wait("tensor", qsig)
    p.wait("tensor", ksig)
    prev_head_evac = None
    prev_head_exp = None
    usig0 = {}
    for half in range(NQH):
        O_tm = xn[:, half * NTH * 1024:(half + 1) * NTH * 1024]
        for h in range(NH):
            base = (h % 2) * 64
            qh = qf[base:base + 64,
                    (h // 2) * T + half * QH:(h // 2) * T + (half + 1) * QH]
            expsig = {}
            tsig = {}

            SOFF = (0, 1024, 3072)

            def smm(kt):
                kh = kf[base:base + 64,
                        (h // 2) * T + kt * 128:(h // 2) * T + (kt + 1) * 128]
                slot = kt % 3
                for qq in range(QH // 512 or 1):
                    w = min(512, QH)
                    p.op("tensor", "matmul", p.ps32(SOFF[slot] + qq * 512, w),
                         kh, qh[:, qq * 512: qq * 512 + w], start=True, stop=True)

            def pv(kt):
                slot = kt % 3
                p.wait("tensor", vsig_tile[kt])
                for qq in range(QH // 512 or 1):
                    w = min(512, QH)
                    p.op("tensor", "matmul",
                         p.ps32(2048 + qq * 512, w, 0, 65),
                         v_tm[:, kt * 1040 + h * 65: kt * 1040 + (h + 1) * 65],
                         Pslot(slot)[:, qq * 512: qq * 512 + w],
                         start=(kt == 0), stop=(kt == NT16 - 1))

            def expop(kt):
                slot = kt % 3
                p.wait("scalar", tsig[kt])
                p.wait("scalar", vsig_tile[2 * slot])
                p.wait("scalar", vsig_tile[2 * slot + 1])
                if kt == 0 and prev_head_evac is not None:
                    p.wait("scalar", prev_head_evac)
                p.op("scalar", "activation", Pslot(slot), p.ps32(SOFF[slot], QH),
                     AF.Exp, bias=cm.fc[:, 1:2], scale=0.125)
                expsig[kt] = p.signal("scalar")

            if prev_head_exp is not None:
                p.wait("tensor", prev_head_exp[0])
            smm(0)
            tsig[0] = p.signal("tensor")
            expop(0)
            for kt in range(1, NT16):
                if kt >= 3:
                    p.wait("tensor", expsig[kt - 3])
                elif prev_head_exp is not None:
                    p.wait("tensor", prev_head_exp[kt % 3])
                smm(kt)
                p.wait("tensor", expsig[kt - 1])
                if kt == 1 and prev_head_evac is not None:
                    p.wait("tensor", prev_head_evac)
                pv(kt - 1)
                tsig[kt] = p.signal("tensor")
                expop(kt)
            p.wait("tensor", expsig[NT16 - 1])
            pv(NT16 - 1)
            pvdone = p.signal("tensor")
            prev_head_exp = {max(0, NT16 - 3 + kk) % 3: expsig[max(0, NT16 - 3 + kk)]
                             for kk in range(3)}
            p.wait("scalar", pvdone)
            for t2 in (6, 7, 8, 9):
                p.wait("scalar", vsig_tile[t2])
            p.op("scalar", "activation", Ou, p.ps32(2048, QH, 0, 65), AF.Identity)
            p.drain("scalar")
            p.op("scalar", "activation", Ou[64:65, :], Ou[64:65, :], AF.Ln)
            p.drain("scalar")
            p.op("scalar", "activation", Ou[64:65, :], Ou[64:65, :], AF.Exp,
                 scale=-1.0)
            rsig = p.signal("scalar")
            p.wait("tensor", rsig)
            for qq in range(QH // 512 or 1):
                w = min(512, QH)
                p.op("tensor", "matmul", p.ps32(3072 + qq * 512, w, 0, 64),
                     cm.ones_f128[64:65, 0:64], Ou[64:65, qq * 512: qq * 512 + w],
                     start=True, stop=True)
            zbsig = p.signal("tensor")
            p.wait("vector", zbsig)
            p.wait("vector", vsig_tile[14])
            p.wait("vector", vsig_tile[15])
            p.op("vector", "tensor_tensor", On, Ou[0:64, :], p.ps32(3072, QH, 0, 64),
                 ALU.mult)
            onsig = p.signal("vector")
            p.wait("tensor", onsig)
            for j in range(NTH):
                p.op("tensor", "transpose", p.psbf(4096 + j * 64, 64),
                     On[:, j * 128:(j + 1) * 128], cm.ident[0:64, 0:64])
            trsig = p.signal("tensor")
            p.wait("vector", trsig)
            pview = p.PSB[:, 4096:4096 + NTH * 64].rearrange("p (j d) -> p j d", d=64)
            oview = O_tm.rearrange("p (j hd) -> p j hd", hd=1024)[:, :, h * 64:(h + 1) * 64]
            p.op("vector", "tensor_copy", oview, pview)
            prev_head_evac = p.signal("vector")

        if half == 0:
            p.wait("gpsimd", prev_head_evac)
            for j in range(NTH):
                p.wait("gpsimd", tksigs[j])
                p.drain("gpsimd")
                slots = [scrb[:, 4096 + i * 1024: 4096 + (i + 1) * 1024]
                         .rearrange("p (x d) -> p x d", d=64)[:, 0:E]
                         for i in range(2)]
                out_sl = U_fm[:, (j // 2) * T + 1024 + (j % 2) * 512:
                              (j // 2) * T + 1024 + (j % 2) * 512 + 512]
                terms = []
                for h2 in range(NH):
                    oap = O_tm[:, j * 1024 + h2 * 64: j * 1024 + (h2 + 1) * 64]
                    a = oap[:, None, :].broadcast_to([128, E, 64])
                    mk = mask_o[:, j * 128 + h2 * 8: j * 128 + (h2 + 1) * 8]
                    b = mk[:, :, None].broadcast_to([128, E, 64])
                    terms.append((a, b))
                emit_wide_sum(p, "gpsimd", slots, terms,
                              out_sl.rearrange("p (e d) -> p e d", d=64))
                usig0[j] = p.signal("gpsimd")

    if stop == "attn":
        p.finish()
        return
    usig1 = {}
    for j in range(0 if (stop or "").startswith(("tx", "tz")) else NTH):
        tt = NTH + j
        eng = "vector" if j % 2 == 0 else "gpsimd"
        p.wait(eng, prev_head_evac)
        p.wait(eng, tksigs[tt])
        p.drain(eng)
        if eng == "vector":
            slots = [scrb[:, i * 1024:(i + 1) * 1024]
                     .rearrange("p (x d) -> p x d", d=64)[:, 0:E]
                     for i in range(4)]
        else:
            slots = [scrb[:, 4096 + i * 1024: 4096 + (i + 1) * 1024]
                     .rearrange("p (x d) -> p x d", d=64)[:, 0:E]
                     for i in range(2)]
        out_sl = qf[:, j * 512:(j + 1) * 512]
        terms = []
        for h2 in range(NH):
            oap = O_tm[:, j * 1024 + h2 * 64: j * 1024 + (h2 + 1) * 64]
            a = oap[:, None, :].broadcast_to([128, E, 64])
            mk = mask_o[:, tt * 128 + h2 * 8: tt * 128 + (h2 + 1) * 8]
            b = mk[:, :, None].broadcast_to([128, E, 64])
            terms.append((a, b))
        emit_wide_sum(p, eng, slots, terms,
                      out_sl.rearrange("p (e d) -> p e d", d=64))
        usig1[j] = p.signal(eng)

    if stop == "u1":
        p.finish()
        return
    utrsig = {}
    NTTR = NT16
    NOEVAC = False
    SAMESRC = False
    if (stop or "").startswith("tr"):
        NTTR = int(stop[2:])
    if (stop or "").startswith("tn"):
        NTTR = int(stop[2:])
        NOEVAC = True
    if (stop or "").startswith(("ts", "tx", "tz")):
        NTTR = int(stop[2:])
        NOEVAC = True
        SAMESRC = True
    for tt0 in range(NTTR):
        tt = tt0
        ss = 0 if SAMESRC else tt
        j = ss % NTH
        if (stop or "").startswith("tz"):
            src = v_tm[:, 0:512]
        elif ss < NTH:
            src = U_fm[:, (j // 2) * T + 1024 + (j % 2) * 512:
                       (j // 2) * T + 1024 + (j % 2) * 512 + 512]
            p.wait("tensor", usig0[j])
        else:
            src = qf[:, j * 512:(j + 1) * 512]
            p.wait("tensor", usig1[j])
        if tt >= 2:
            p.wait("tensor", utrsig[tt - 2])
        rbase = 0 + (tt % 2) * 1024
        for cchunk in range(4):
            p.op("tensor", "transpose", p.psbf(rbase + cchunk * 128, 128),
                 src[:, cchunk * 128:(cchunk + 1) * 128], cm.ident[:])
        trs = p.signal("tensor")
        p.wait("vector", trs)
        if NOEVAC:
            utrsig[tt] = trs
            continue
        pv4 = p.PSB[:, rbase:rbase + 512].rearrange("p (cchunk d) -> p cchunk d", d=128)
        uview = U_fm[:].rearrange("p (cchunk t) -> p cchunk t", t=T)[
            :, :, tt * 128:(tt + 1) * 128]
        p.op("vector", "tensor_copy", uview, pv4)
        utrsig[tt] = p.signal("vector")

    if stop == "u" or (stop or "").startswith(("tr", "tn", "ts", "tx", "tz")):
        p.finish()
        return
    p.wait("sync", utrsig[NT16 - 1])
    cm.emit_proj_residual(wbuf, U_fm, xpart, o_x1, bias=None,
                          wait_sigs=[wosig, utrsig[NT16 - 1]], OC=CCN, CC=4,
                          Odim=C, xbuf=scr[:, 0:1024], obuf=scr[:, 1024:2048])
    for s in ("ro0", "ro1"):
        p.wait("sync", p.dma_all(s))


def build_L0f(T, stop=None):
    p = Prog("L0f")
    cm = Common(p, T)
    xnH = p.din("i_xn", [C, T], BF16)
    xpart = p.din("i_xpart", [C, T], F32)
    o_x1 = p.dout("o_x1", [C, T], F32)
    emit_L0f(p, cm, T, xnH, xpart, o_x1, stop=stop)
    return p


def emit_L1a(p, cm, T, x1, o_x2p, o_xn1, o_xn1f):
    NT16 = T // 128
    QH = min(1024, T)
    NQH = T // QH

    HW = min(1024, T)
    QW = min(512, QH)
    xn_need = max(CCN * T, 2 * (3 * HW + 4 * QW) + 2 * QH)
    xn = p.region("b_xn", [128, xn_need], BF16)
    q_fm = p.region("b_q_fm", [128, CCN * T], BF16)
    k_fm = p.region("b_k_fm", [128, CCN * T], BF16)
    v_tm = p.region("b_v_tm", [128, NT16 * 65 * NH], BF16)
    wbuf = cm.wbuf
    ropeR = p.region("b_ropeR", [128, max(4 * T, (CCN * QH + 3 * QH) // 2 + 2 * QH)], F32)
    rv = apof(ropeR)
    rbf = rv.bitcast(BF16)
    cosE = rv[:, 0:T] if T <= 2048 else None
    sinE = rv[:, T:2 * T]
    cosO = rv[:, 2 * T:3 * T]
    sinO = rv[:, 3 * T:4 * T]
    for nm, dstv in (("rope_cosE", cosE), ("rope_sinE", sinE),
                     ("rope_cosO", cosO), ("rope_sinO", sinO)):
        p.dma("sync", "cst", dstv, p.din(nm, [128, T], F32))
    O_fm = rbf[:, 0:CCN * QH]
    PP = rbf[:, CCN * QH: CCN * QH + 3 * QH]
    Ou = rv[0:65, (CCN * QH + 3 * QH) // 2: (CCN * QH + 3 * QH) // 2 + QH]
    Rb = rv[0:64, (CCN * QH + 3 * QH) // 2 + QH: (CCN * QH + 3 * QH) // 2 + 2 * QH]
    rtR = p.region("b_rt", [128, 3 * HW], F32)
    rt = apof(rtR)
    xnv = apof(xn).bitcast(F32)
    xbuf = xnv[:, 3 * HW:3 * HW + 2 * QW]
    obuf = xnv[:, 3 * HW + 2 * QW:3 * HW + 4 * QW]
    OnB = apof(xn)[0:64, 2 * (3 * HW + 4 * QW):2 * (3 * HW + 4 * QW) + QH]
    bq = cm.loadc("l1a_bq", [128, 8], F32)
    bk = cm.loadc("l1a_bk", [128, 8], F32)
    bo = cm.loadc("l1a_bo", [128, 8], F32)
    bvr = cm.loadc("l1a_bv_row", [1, C], F32)
    bvr_bf = p.region("b_bvr_bf", [1, C], BF16)
    ptri = cm.loadc("pen_tri", [128, 128], F32)
    qscr = p.nc.dram_tensor("qscr", [C, T], BF16, kind="Internal").ap()
    kscr = p.nc.dram_tensor("kscr", [C, T], BF16, kind="Internal").ap()

    xn1f3 = o_xn1f.rearrange("(cc p) t -> p cc t", p=128)
    xn1f3 = o_xn1f.rearrange("(cc p) t -> p cc t", p=128)

    def sink(cc, tmp, sig):
        p.wait("sync", sig)
        return p.dma("sync", "spill", xn1f3[:, cc], tmp[:])

    lnsig = cm.emit_ln_stream(x1, xn, q_fm, k_fm, fp32_sink=sink)
    p.wait("sync", lnsig)
    p.dma("sync", "out", o_xn1.rearrange("(cc p) t -> p cc t", p=128),
          apof(xn)[:, 0:CCN * T].rearrange("p (cc t) -> p cc t", cc=CCN))

    cm.wait_consts(["vector"])
    p.op("vector", "tensor_copy", bvr_bf[:], bvr[:])
    bvsig = p.signal("vector")

    wq = p.din("l1a_WqT", [C, C], F32)
    wk = p.din("l1a_WkT", [C, C], F32)
    wv = p.din("l1a_WvT", [C, C], F32)
    wo = p.din("l1a_WoT", [C, C], F32)

    wsig = cm.load_weight("wq", wq, C, C, wbuf)
    qsig = cm.emit_proj_fm(wbuf, 0, xn, q_fm, CCN, bias=bq,
                           wait_sigs=[lnsig, wsig, p.dma_all("spill"),
                                      p.dma_all("out")])
    wsig = cm.load_weight("wk", wk, C, C, wbuf, war_sig=qsig)
    ksig = cm.emit_proj_fm(wbuf, 0, xn, k_fm, CCN, bias=bk, wait_sigs=[wsig])
    wsig = cm.load_weight("wv", wv, C, C, wbuf, war_sig=ksig)
    vsig = cm.emit_v_tm(wbuf, xn, v_tm, bvr_bf, wait_sigs=[wsig, bvsig])

    p.wait("vector", p.dma_all("cst"))
    ropesigs = []
    for buf, psig in ((q_fm, qsig), (k_fm, ksig)):
        p.wait("vector", psig)
        for j in range(4):
            for hh in range(T // HW):
                lo = hh * HW
                cE = cosE[:, lo:lo + HW]
                sE = sinE[:, lo:lo + HW]
                cO = cosO[:, lo:lo + HW]
                sO = sinO[:, lo:lo + HW]
                qj = buf[:, j * T + lo: j * T + lo + HW]
                qj4 = buf[:, (j + 4) * T + lo: (j + 4) * T + lo + HW]
                t1, t4, t2 = (rt[:, 0:HW], rt[:, HW:2 * HW], rt[:, 2 * HW:3 * HW])
                p.op("vector", "tensor_tensor", t1, qj, cE, ALU.mult)
                p.op("vector", "tensor_tensor", t4, qj, sO, ALU.mult)
                p.op("vector", "tensor_tensor", t2, qj4, sE, ALU.mult)
                p.drain("vector")
                p.op("vector", "tensor_tensor", qj, t1, t2, ALU.subtract)
                p.drain("vector")
                p.op("vector", "tensor_tensor", t2, qj4, cO, ALU.mult)
                p.drain("vector")
                p.op("vector", "tensor_tensor", qj4, t2, t4, ALU.add)
                p.drain("vector")
        ropesigs.append(p.signal("vector"))
    ropesig_q, ropesig_k = ropesigs
    ropesig = ropesig_k


    def repack_store(scr, srcbuf, sem):
        for cc in range(4):
            for a in range(2):
                for b in range(2):
                    g = 256 * cc + 128 * a + 64 * b
                    p.dma("sync", sem, scr[g:g + 32, :],
                          srcbuf[64 * a + 32 * b:64 * a + 32 * b + 32,
                                 cc * T:(cc + 1) * T])
                    p.dma("sync", sem, scr[g + 32:g + 64, :],
                          srcbuf[64 * a + 32 * b:64 * a + 32 * b + 32,
                                 (4 + cc) * T:(5 + cc) * T])

    p.wait("sync", ropesig_q)
    repack_store(qscr, q_fm, "rpq")
    p.wait("sync", p.dma_all("rpq"))
    lq = p.dma("sync", "rpq2", q_fm[:].rearrange("p (cc t) -> p cc t", cc=CCN),
               qscr.rearrange("(cc p) t -> p cc t", p=128))
    p.wait("sync", ropesig_k)
    repack_store(kscr, k_fm, "rpk")
    p.wait("sync", p.dma_all("rpk"))
    lk = p.dma("sync", "rpk2", k_fm[:].rearrange("p (cc t) -> p cc t", cc=CCN),
               kscr.rearrange("(cc p) t -> p cc t", p=128))
    q2 = q_fm
    k2 = k_fm
    rpsig = lk

    for e2 in ("tensor", "scalar"):
        p.wait(e2, ropesig)
        p.wait(e2, vsig)
        p.wait(e2, lq)
        p.wait(e2, rpsig)
    shiftsigs = []
    prev_head_evac = None
    prev_head_exp = None
    for half in range(NQH):
        qlo = half * QH
        kts = [kt for kt in range(NT16) if kt * 128 < qlo + QH]
        for h in range(NH):
            base = (h % 2) * 64
            qh = q2[base:base + 64, (h // 2) * T + qlo:(h // 2) * T + qlo + QH]
            expsig = {}
            tsig = {}
            nk = len(kts)

            SOFF = (0, 1024, 3072)

            def smm(ki):
                kt = kts[ki]
                delta = kt * 128 - qlo
                ql = max(0, delta)
                slot = ki % 3
                kh = k2[base:base + 64,
                        (h // 2) * T + kt * 128:(h // 2) * T + (kt + 1) * 128]
                qq = ql
                while qq < QH:
                    w = min(512 - (qq % 512), QH - qq)
                    p.op("tensor", "matmul", p.ps32(SOFF[slot] + qq, w),
                         kh, qh[:, qq:qq + w], start=True, stop=True)
                    qq += w

            def pv(ki):
                kt = kts[ki]
                delta = kt * 128 - qlo
                ql = max(0, delta)
                slot = ki % 3
                qq = ql
                while qq < QH:
                    w = min(512 - (qq % 512), QH - qq)
                    p.op("tensor", "matmul",
                         p.ps32(2048 + qq, w, 0, 65),
                         v_tm[:, kt * 1040 + h * 65: kt * 1040 + (h + 1) * 65],
                         PP[:, slot * QH + qq: slot * QH + qq + w],
                         start=(ki == 0), stop=(ki == nk - 1))
                    qq += w

            def expop(ki):
                kt = kts[ki]
                delta = kt * 128 - qlo
                ql = max(0, delta)
                slot = ki % 3
                if 0 <= delta:
                    p.wait("vector", tsig[ki])
                    sap = p.ps32(SOFF[slot] + delta, 128)
                    p.op("vector", "tensor_tensor", sap, sap, ptri[:], ALU.add)
                    pensig = p.signal("vector")
                    p.wait("scalar", pensig)
                p.wait("scalar", tsig[ki])
                if ki == 0 and prev_head_evac is not None:
                    p.wait("scalar", prev_head_evac)
                p.op("scalar", "activation",
                     PP[:, slot * QH + ql: (slot + 1) * QH],
                     p.ps32(SOFF[slot] + ql, QH - ql),
                     AF.Exp, bias=cm.fc[:, 1:2], scale=0.125)
                expsig[ki] = p.signal("scalar")

            if prev_head_exp is not None:
                p.wait("tensor", prev_head_exp[0])
            smm(0)
            tsig[0] = p.signal("tensor")
            expop(0)
            for ki in range(1, nk):
                if ki >= 3:
                    p.wait("tensor", expsig[ki - 3])
                elif prev_head_exp is not None:
                    p.wait("tensor", prev_head_exp.get(ki % 3, prev_head_exp[0]))
                smm(ki)
                p.wait("tensor", expsig[ki - 1])
                if ki == 1 and prev_head_evac is not None:
                    p.wait("tensor", prev_head_evac)
                pv(ki - 1)
                tsig[ki] = p.signal("tensor")
                expop(ki)
            p.wait("tensor", expsig[nk - 1])
            pv(nk - 1)
            pvdone = p.signal("tensor")
            prev_head_exp = {max(0, nk - 3 + kk) % 3: expsig[max(0, nk - 3 + kk)]
                             for kk in range(3)}
            p.wait("scalar", pvdone)
            p.op("scalar", "activation", Ou, p.ps32(2048, QH, 0, 65),
                 AF.Identity)
            p.drain("scalar")
            p.op("scalar", "activation", Ou[64:65, :], Ou[64:65, :], AF.Ln)
            p.drain("scalar")
            p.op("scalar", "activation", Ou[64:65, :], Ou[64:65, :], AF.Exp,
                 scale=-1.0)
            rsig = p.signal("scalar")
            p.wait("tensor", rsig)
            for qq in range(QH // 512 or 1):
                w = min(512, QH)
                p.op("tensor", "matmul", p.ps32(3072 + qq * 512, w, 0, 64),
                     cm.ones_f128[64:65, 0:64], Ou[64:65, qq * 512: qq * 512 + w],
                     start=True, stop=True)
            zbsig = p.signal("tensor")
            p.wait("vector", zbsig)
            dst = O_fm[0:64, (h // 2) * QH:(h // 2 + 1) * QH] if h % 2 == 0 \
                else OnB
            p.op("vector", "tensor_tensor", dst, Ou[0:64, :],
                 p.ps32(3072, QH, 0, 64), ALU.mult)
            onsig = p.signal("vector")
            if h % 2 == 1:
                p.wait("sync", onsig)
                ssig = p.dma("sync", "shift%d" % (h % 4 // 2),
                             O_fm[64:128, (h // 2) * QH:(h // 2 + 1) * QH],
                             OnB)
                shiftsigs.append(ssig)
                p.wait("vector", ssig)
            prev_head_evac = onsig

        wosig = cm.load_weight("wo", wo, C, C, wbuf, war_sig=vsig) if half == 0 else wosig
        for s in shiftsigs:
            p.wait("tensor", s)
        shiftsigs = []
        p.wait("tensor", prev_head_evac)
        x1h = x1.rearrange("(cc p) t -> p cc t", p=128)
        o2h = o_x2p.rearrange("(cc p) t -> p cc t", p=128)
        QW = min(512, QH)
        NTW = QH // QW
        dsigs = {}
        stsigs = {}

        def pre(oc, tc):
            kk = oc * NTW + tc
            slot = kk % 2
            if kk >= 2:
                p.wait("sync", stsigs[kk - 2])
            dsigs[kk] = p.dma("sync", "resx%d" % slot,
                              xbuf[:, slot * QW:(slot + 1) * QW],
                              x1h[:, oc, qlo + tc * QW: qlo + (tc + 1) * QW])

        def evac(oc, tc, ps):
            kk = oc * NTW + tc
            slot = kk % 2
            p.wait("vector", dsigs[kk])
            if kk >= 2:
                p.wait("vector", stsigs[kk - 2])
            p.op("vector", "scalar_tensor_tensor",
                 obuf[:, slot * QW:(slot + 1) * QW], ps, bo[:, oc:oc + 1],
                 xbuf[:, slot * QW:(slot + 1) * QW], ALU.add, ALU.add)
            esig = p.signal("vector")
            p.wait("sync", esig)
            stsigs[kk] = p.dma("sync", "ro%d" % slot,
                               o2h[:, oc, qlo + tc * QW: qlo + (tc + 1) * QW],
                               obuf[:, slot * QW:(slot + 1) * QW])

        class _FakeT:
            pass
        cm2T = cm.T
        cm.T = QH
        wores = cm.emit_proj_fm(wbuf, 0, O_fm, None, CCN, wait_sigs=[wosig],
                                evac_engine="vector", evac_cb=evac, pre_mm=pre)
        cm.T = cm2T
        p.wait("tensor", wores)
    for s in ("out", "spill", "ro0", "ro1"):
        p.wait("sync", p.dma_all(s))


def build_L1a(T):
    p = Prog("L1a")
    cm = Common(p, T)
    x1 = p.din("i_x1", [C, T], F32)
    o_x2p = p.dout("o_x2p", [C, T], F32)
    o_xn1 = p.dout("o_xn1", [C, T], BF16)
    o_xn1f = p.dout("o_xn1f", [C, T], F32)
    emit_L1a(p, cm, T, x1, o_x2p, o_xn1, o_xn1f)
    return p


def emit_L1f(p, cm, T, xn1b, xn1f, x2p, o_y):
    QH = min(1024, T)
    NQH = T // QH
    QW = min(512, QH)

    xnb = p.region("g_xnb", [128, max(CCN * T, 2 * CCN * QH)], BF16)
    OUT = apof(xnb).bitcast(F32)[:, 0:CCN * QH]
    out2r = p.region("g_out2", [128, 2 * CCN * QH], BF16)
    OUT2 = apof(out2r).bitcast(F32)[:, 0:CCN * QH]
    tw2 = p.region("g_tw2", [128, 2 * QH], F32)
    cp_sb = p.region("g_cp_sb", [128, QH], F32)
    outbf = p.region("g_outbf", [128, CCN * QH], BF16)
    h1b = p.region("g_h1b", [128, CCN * QH], BF16)
    h2b = p.region("g_h2b", [128, 4 * QH], BF16)
    h3b = p.region("g_h3b", [128, 2 * QH], BF16)
    tw = p.region("g_tw", [128, 2 * QH], F32)
    rows = p.region("g_rows", [1, 2 * QH], F32)
    gsml = p.region("g_gsml", [128, 32], F32)
    gsmb = p.region("g_gsmb", [128, 32], BF16)
    gf_sb = p.region("g_gf_sb", [1, 8], F32)
    w1 = p.region("g_w1", [128, CCN * 1024], BF16)
    w2 = p.region("g_w2", [128, CCN * 512], BF16)
    w3 = p.region("g_w3", [128, 4 * 256], BF16)
    w4 = p.region("g_w4", [128, 2 * 1], BF16)
    gw1 = p.region("g_gw1", [128, CCN * 512], BF16)
    gw2 = p.region("g_gw2", [128, 4 * 256], BF16)
    gw3 = p.region("g_gw3", [128, 2 * 1], BF16)
    b1 = cm.loadc("l1f_cn_b1", [128, 8], F32)
    b2 = cm.loadc("l1f_cn_b2", [128, 4], F32)
    b3 = cm.loadc("l1f_cn_b3", [128, 2], F32)
    b4 = cm.loadc("l1f_cn_b4", [1, 1], F32)
    gb1 = cm.loadc("l1f_gc_b1", [128, 4], F32)
    gb2 = cm.loadc("l1f_gc_b2", [128, 2], F32)
    gb3 = cm.loadc("l1f_gc_b3", [1, 1], F32)
    arc = cm.loadc("l1f_ar", [1, 1], F32)

    xn1f3 = xn1f.rearrange("(cc p) t -> p cc t", p=128)
    OUTS = [OUT, OUT2]
    osigs = []
    for half in range(NQH):
        osigs.append(p.dma("sync", "o%d" % (half % 2),
                           OUTS[half][:].rearrange("p (cc t) -> p cc t", cc=CCN),
                           xn1f3[:, :, half * QH:(half + 1) * QH]))

    wsigs = []
    for nm, hbm_shape, dst, R, O in (
            ("l1f_cn_w1", [C, C], w1, C, 1024),
            ("l1f_cn_w2", [C, 512], w2, C, 512),
            ("l1f_cn_w3", [512, 256], w3, 512, 256),
            ("l1f_cn_w4", [1, 256], w4, 256, 1),
            ("l1f_gc_w1", [C, 512], gw1, C, 512),
            ("l1f_gc_w2", [512, 256], gw2, 512, 256),
            ("l1f_gc_w3", [1, 256], gw3, 256, 1)):
        hbm = p.din(nm, hbm_shape, F32)
        wsigs.append(cm.load_weight(nm, hbm, R, O, dst))
    wsig = wsigs[-1]

    cm.wait_consts(["vector", "scalar", "tensor"])
    p.wait("vector", osigs[0])
    p.wait("vector", osigs[-1])
    for cc in range(CCN):
        p.op("vector", "tensor_reduce", gsml[:, 16 + cc:17 + cc],
             OUT[:, cc * QH:(cc + 1) * QH], AX.X, ALU.add)
        p.op("vector", "tensor_reduce", gsml[:, 24 + cc:25 + cc],
             OUT2[:, cc * QH:(cc + 1) * QH], AX.X, ALU.add)
    p.drain("vector")
    p.op("vector", "tensor_tensor", gsml[:, 0:8], gsml[:, 16:24],
         gsml[:, 24:32], ALU.add)
    p.drain("vector")
    p.op("vector", "tensor_scalar", gsml[:, 0:8], gsml[:, 0:8], 1.0 / T, None,
         ALU.mult)
    p.drain("vector")
    p.op("vector", "tensor_copy", gsmb[:, 0:8], gsml[:, 0:8])
    gcsig = p.signal("vector")
    for s in wsigs:
        p.wait("tensor", s)
    p.wait("tensor", gcsig)
    for oc in range(4):
        ps = p.ps32(3584, 1)
        for cc in range(CCN):
            p.op("tensor", "matmul", ps, gw1[:, cc * 512 + oc * 128:
                                             cc * 512 + (oc + 1) * 128],
                 gsmb[:, cc:cc + 1], start=(cc == 0), stop=(cc == CCN - 1))
        msig = p.signal("tensor")
        p.wait("scalar", msig)
        p.op("scalar", "activation", gsml[:, 8 + oc:9 + oc], ps, AF.Gelu,
             bias=gb1[:, oc:oc + 1])
        p.drain("scalar")
        p.op("scalar", "copy", gsmb[:, 8 + oc:9 + oc], gsml[:, 8 + oc:9 + oc])
        esig = p.signal("scalar")
        p.wait("tensor", esig)
    for oc in range(2):
        ps = p.ps32(3584, 1)
        for cc in range(4):
            p.op("tensor", "matmul", ps, gw2[:, cc * 256 + oc * 128:
                                             cc * 256 + (oc + 1) * 128],
                 gsmb[:, 8 + cc:9 + cc], start=(cc == 0), stop=(cc == 3))
        msig = p.signal("tensor")
        p.wait("scalar", msig)
        p.op("scalar", "activation", gsml[:, 12 + oc:13 + oc], ps, AF.Gelu,
             bias=gb2[:, oc:oc + 1])
        p.drain("scalar")
        p.op("scalar", "copy", gsmb[:, 12 + oc:13 + oc], gsml[:, 12 + oc:13 + oc])
        esig = p.signal("scalar")
        p.wait("tensor", esig)
    ps = p.ps32(3584, 1, 0, 1)
    for cc in range(2):
        p.op("tensor", "matmul", ps, gw3[:, cc:cc + 1], gsmb[:, 12 + cc:13 + cc],
             start=(cc == 0), stop=(cc == 1))
    msig = p.signal("tensor")
    p.wait("scalar", msig)
    p.op("scalar", "activation", gf_sb[0:1, 0:1], ps, AF.Sigmoid,
         bias=gb3[0:1, 0:1])
    gfsig = p.signal("scalar")
    p.wait("vector", gfsig)
    p.op("vector", "tensor_scalar", gf_sb[0:1, 1:2], gf_sb[0:1, 0:1], 0.7, None,
         ALU.mult)
    p.op("vector", "tensor_scalar", gf_sb[0:1, 2:3], arc[0:1, 0:1], 2.0, None,
         ALU.mult)
    gf7sig = p.signal("vector")

    xn1f3 = xn1f.rearrange("(cc p) t -> p cc t", p=128)
    x2p3 = x2p.rearrange("(cc p) t -> p cc t", p=128)
    oy3 = o_y.rearrange("(cc p) t -> p cc t", p=128)
    xbuf = p.region("g_res_x", [128, 2 * QW], F32)
    obuf = p.region("g_res_o", [128, 2 * QW], F32)
    tf_row = rows[0:1, 0:QH]
    cp_row = rows[0:1, QH:2 * QH]
    cm2T = cm.T
    cm.T = QH
    fsts = {}
    fdss = {}
    twg = None
    for half in range(NQH):
        qlo = half * QH
        OUTh = OUTS[half]
        osig = osigs[half]
        cvsig = None
        for cyc in range(3):
            p.wait("vector", osig)
            if twg is not None:
                p.wait("vector", twg)
            for cc in range(CCN):
                p.op("vector", "tensor_copy", outbf[:, cc * QH:(cc + 1) * QH],
                     OUTh[:, cc * QH:(cc + 1) * QH])
            cvsig = p.signal("vector")
            h1sig = cm.emit_proj_fm(apof(w1), 0, apof(outbf), apof(h1b), CCN,
                                    bias=b1, act=AF.Gelu, wait_sigs=[cvsig])
            h2sig = cm.emit_proj_fm(apof(w2), 0, apof(h1b), apof(h2b), 4, bias=b2,
                                    act=AF.Gelu, wait_sigs=[h1sig], Odim=512)
            h3sig = cm.emit_proj_fm(apof(w3), 0, apof(h2b), apof(h3b), 2, bias=b3,
                                    act=AF.Gelu, wait_sigs=[h2sig], CC=4, Odim=256)
            p.wait("tensor", h3sig)
            for qq in range(QH // QW):
                ps2 = p.ps32(2048 + qq * QW, QW, 0, 1)
                for cc in range(2):
                    p.op("tensor", "matmul", ps2, w4[:, cc:cc + 1],
                         h3b[:, cc * QH + qq * QW: cc * QH + (qq + 1) * QW],
                         start=(cc == 0), stop=(cc == 1))
            tfmsig = p.signal("tensor")
            p.wait("scalar", tfmsig)
            p.op("scalar", "activation", tf_row, p.ps32(2048, QH, 0, 1),
                 AF.Sigmoid, bias=b4[0:1, 0:1])
            tfsig = p.signal("scalar")
            p.wait("vector", tfsig)
            p.wait("vector", gf7sig)
            p.op("vector", "tensor_scalar", cp_row, tf_row, 0.3,
                 gf_sb[0:1, 1:2], ALU.mult, ALU.add)
            p.drain("vector")
            p.op("vector", "tensor_scalar", cp_row, cp_row, 0.5,
                 gf_sb[0:1, 2:3], ALU.subtract, ALU.mult)
            p.drain("vector")
            p.op("vector", "tensor_scalar", cp_row, cp_row, 0.1, None, ALU.add)
            cpsig = p.signal("vector")
            p.wait("tensor", cpsig)
            for qq in range(QH // QW):
                p.op("tensor", "matmul", p.ps32(3072 + qq * QW, QW),
                     cm.ones_row_f[:], cp_row[0:1, qq * QW:(qq + 1) * QW],
                     start=True, stop=True)
            cpbsig = p.signal("tensor")
            p.wait("scalar", cpbsig)
            p.op("scalar", "activation", cp_sb[:], p.ps32(3072, QH), AF.Identity)
            cpssig = p.signal("scalar")
            tws = {}
            twv_pairs = []
            for eng, pairs, twr in (
                    ("vector", (((0, 4), 1.0, -1.0), ((1, 5), 1.0, -1.0)), tw),
                    ("gpsimd", (((2, 6), -1.0, 1.0), ((3, 7), -1.0, 1.0)), tw2)):
                p.wait(eng, cpssig)
                if eng == "gpsimd":
                    p.wait(eng, cvsig)
                p.drain(eng)
                for pr, sa, sb_ in pairs:
                    cA, cB = pr
                    a_ap = OUTh[:, cA * QH:(cA + 1) * QH]
                    b_ap = OUTh[:, cB * QH:(cB + 1) * QH]
                    p.op(eng, "tensor_tensor", twr[:, 0:QH], cp_sb[:], b_ap,
                         ALU.mult)
                    p.op(eng, "tensor_tensor", twr[:, QH:2 * QH], cp_sb[:], a_ap,
                         ALU.mult)
                    p.drain(eng)
                    if sa > 0:
                        p.op(eng, "tensor_tensor", a_ap, twr[:, 0:QH], a_ap,
                             ALU.add)
                    else:
                        p.op(eng, "tensor_tensor", a_ap, a_ap, twr[:, 0:QH],
                             ALU.subtract)
                    if sb_ > 0:
                        p.op(eng, "tensor_tensor", b_ap, twr[:, QH:2 * QH], b_ap,
                             ALU.add)
                    else:
                        p.op(eng, "tensor_tensor", b_ap, b_ap, twr[:, QH:2 * QH],
                             ALU.subtract)
                    p.drain(eng)
                    if eng == "vector":
                        twv_pairs.append(p.signal(eng))
                tws[eng] = twv_pairs[-1] if eng == "vector" else p.signal(eng)
            twv, twg = tws["vector"], tws["gpsimd"]
        for k, (oc, tc) in enumerate(
                (oc, tc) for oc in (2, 6, 3, 7, 0, 4, 1, 5)
                for tc in range(QH // QW)):
            if oc in (0, 4):
                p.wait("gpsimd", twv_pairs[0])
            elif oc in (1, 5):
                p.wait("gpsimd", twv_pairs[1])
            kk = half * CCN * (QH // QW) + k
            slot = kk % 2
            if kk >= 2:
                p.wait("sync", fsts[kk - 2])
            fdss[kk] = p.dma("sync", "resx%d" % slot,
                             xbuf[:, slot * QW:(slot + 1) * QW],
                             x2p3[:, oc, qlo + tc * QW: qlo + (tc + 1) * QW])
            p.wait("gpsimd", fdss[kk])
            if kk >= 2:
                p.wait("gpsimd", fsts[kk - 2])
            p.drain("gpsimd")
            p.op("gpsimd", "tensor_tensor", obuf[:, slot * QW:(slot + 1) * QW],
                 OUTh[:, oc * QH + tc * QW: oc * QH + (tc + 1) * QW],
                 xbuf[:, slot * QW:(slot + 1) * QW], ALU.add)
            esig = p.signal("gpsimd")
            p.wait("sync", esig)
            fsts[kk] = p.dma("sync", "ro%d" % slot,
                             oy3[:, oc, qlo + tc * QW: qlo + (tc + 1) * QW],
                             obuf[:, slot * QW:(slot + 1) * QW])
    cm.T = cm2T


def build_L1f(T):
    p = Prog("L1f")
    cm = Common(p, T)
    xn1b = p.din("i_xn1", [C, T], BF16)
    xn1f = p.din("i_xn1f", [C, T], F32)
    x2p = p.din("i_x2p", [C, T], F32)
    o_y = p.dout("o_y", [C, T], F32)
    emit_L1f(p, cm, T, xn1b, xn1f, x2p, o_y)
    return p
